# version 05
# baseline (speedup 1.0000x reference)
import contextlib
from functools import partial

import numpy as np

import concourse.bass as bass
import concourse.mybir as mybir
from concourse.bass_utils import run_bass_kernel_spmd

F32 = mybir.dt.float32
BF16 = mybir.dt.bfloat16
I16 = mybir.dt.int16
AF = mybir.ActivationFunctionType
ALU = mybir.AluOpType
AX = mybir.AxisListType

B, L, D = 2, 8192, 2048
H, DH, NH, CONV = 8, 128, 2, 4
KD = H * DH
C = 64
R = 2 * C
TPC = 2048
NCH = TPC // C
NTC = TPC // 128
NFC = D // 128
I_FFN = 8192
NIC = I_FFN // 128
ISQD = 1.0 / float(np.sqrt(np.float32(DH)))

ENGINES = ("tensor", "vector", "scalar", "gpsimd", "sync")


class Ev:
    __slots__ = ("sem", "val", "eng")

    def __init__(self, sem, val, eng):
        self.sem, self.val, self.eng = sem, val, eng


def merge(*evs):
    best = {}
    for e in evs:
        if e is None:
            continue
        if isinstance(e, (list, tuple)):
            for x in e:
                if x is not None and (x.sem not in best or
                                      best[x.sem].val < x.val):
                    best[x.sem] = x
        elif e.sem not in best or best[e.sem].val < e.val:
            best[e.sem] = e
    return list(best.values())


class Prog:
    def __init__(self):
        self.stream = {e: [] for e in ENGINES}
        self.count = {}
        self.sem_names = set()

    COMPUTE_SEMS = ()

    def on(self, eng, fn, waits=(), inc=None, amt=None):
        ws = merge(*waits)
        ws = [Ev(w.sem, self.count[w.sem], w.eng)
              if (w.sem.startswith("d") and self.count.get(w.sem, 0) > w.val)
              else w for w in ws]
        ev = None
        if inc is not None:
            if amt is None:
                amt = 16 if inc.startswith("d") else 1
            self.sem_names.add(inc)
            self.count[inc] = self.count.get(inc, 0) + amt
            ev = Ev(inc, self.count[inc], eng)
        for w in ws:
            self.sem_names.add(w.sem)
        self.stream[eng].append((ws, fn, inc, amt))
        return ev

    def replay(self, block, sems):
        for eng in ENGINES:
            ops = self.stream[eng]
            if not ops:
                continue

            def make_body(eng, ops):
                def body(engobj):
                    last = {}
                    for ws, fn, inc, amt in ops:
                        for w in ws:
                            if w.eng == eng and w.sem in Prog.COMPUTE_SEMS:
                                continue
                            if last.get(w.sem, -1) >= w.val:
                                continue
                            engobj.wait_ge(sems[w.sem], w.val)
                            last[w.sem] = w.val
                        instr = fn(engobj)
                        if inc is not None:
                            instr.then_inc(sems[inc], amt)
                return body
            getattr(block, eng)(make_body(eng, ops))


def f_dma(e, out, in_, slow=False):
    if slow:
        return e.dma_start(out=out, in_=in_, allow_slow_non_contiguous=True)
    return e.dma_start(out=out, in_=in_)


def f_mm(e, out, lhsT, rhs, start, stop):
    return e.matmul(out, lhsT, rhs, start=start, stop=stop)


def f_tr(e, out, in_, ident):
    return e.transpose(out, in_, ident)


def f_copy(e, out, in_):
    if hasattr(e, "tensor_copy"):
        return e.tensor_copy(out=out, in_=in_)
    return e.copy(out=out, in_=in_)


def f_ts(e, out, in0, s1, s2, op0, op1):
    return e.tensor_scalar(out=out, in0=in0, scalar1=s1, scalar2=s2,
                           op0=op0, op1=op1)


def f_tt(e, out, in0, in1, op):
    return e.tensor_tensor(out=out, in0=in0, in1=in1, op=op)


def f_stt(e, out, in0, s, in1, op0, op1):
    return e.scalar_tensor_tensor(out=out, in0=in0, scalar=s, in1=in1,
                                  op0=op0, op1=op1)


def f_act(e, out, in_, func, bias=0.0, scale=1.0, accum=None):
    if accum is None:
        return e.activation(out=out, in_=in_, func=func, bias=bias,
                            scale=scale)
    return e.activation(out=out, in_=in_, func=func, bias=bias, scale=scale,
                        accum_out=accum)


def f_red(e, out, in_, op, absval):
    return e.tensor_reduce(out=out, in_=in_, axis=AX.X, op=op,
                           apply_absolute_value=absval)


def f_scan(e, out, d0, d1, init, op0, op1):
    return e.tensor_tensor_scan(out=out, data0=d0, data1=d1, initial=init,
                                op0=op0, op1=op1)


def f_recip(e, out, in_):
    return e.reciprocal(out=out, in_=in_)


def f_memset(e, ap, v):
    return e.memset(ap, v)


def f_cc(e, kind, op, ins, outs):
    return e.collective_compute(kind, op, replica_groups=[list(range(8))],
                                ins=ins, outs=outs)


def f_nop(e):
    return e.nop()


DEBUG_TAPS = {
    "dbg_xnfm": ((128, 2048), BF16),
    "dbg_qt": ((128, NCH * 64), BF16),
    "dbg_kt": ((128, NCH * 128), BF16),
    "dbg_vt": ((128, NCH * 128), BF16),
    "dbg_G": ((8, TPC), F32),
    "dbg_Br": ((128, 8 * NCH), F32),
    "dbg_opre": ((NCH, 64, H * 128), F32),
    "dbg_ephi": ((NCH, 128, H * 64), BF16),
    "dbg_S0": ((128, H * 128), BF16),
    "dbg_Phi": ((128, H * 128), BF16),
    "dbg_sin": ((128, H * 128), BF16),
    "dbg_h": ((4, 128, TPC), F32),
    "dbg_g": ((2, 4, 128, 512), F32),
    "dbg_gall": ((NFC, 4, 128, 512), F32),
    "dbg_gate_all": ((NTC, 128, KD), F32),
    "dbg_gate": ((128, KD), F32),
    "dbg_qxn": ((128, 2048), BF16),
    "dbg_sx": ((128, 8), F32),
    "dbg_amx": ((128, 2), F32),
    "dbg_st1": ((1, 16), F32),
    "dbg_mach": ((128, 1472), BF16),
    "dbg_maf": ((128, 960), F32),
    "dbg_Gr": ((128, 8 * NCH), F32),
    "dbg_bagg": ((16, TPC), F32),
    "dbg_xnf2": ((128, 2048), BF16),
    "dbg_mxl": ((1, 4), F32),
    "dbg_P": ((128, 128), F32),
    "dbg_naf": ((128, 128), F32),
    "dbg_qlat": ((128, 16), F32),
    "dbg_qw": ((128, 2048), BF16),
    "dbg_sw": ((128, NIC), F32),
    "dbg_sc1": ((128, NIC), F32),
    "dbg_b1": ((128, NIC), F32),
    "dbg_hxx": ((128, 2048), F32),
    "dbg_swin": ((1, 1024), F32),
    "dbg_swout": ((8, 1024), F32),
    "dbg_qwin": ((128, 2048), BF16),
    "dbg_wst": ((128, 1024), F32),
    "dbg_wsc": ((128, 8), F32),
}


class Builder:
    def __init__(self, debug=()):
        self.nc = bass.Bass("TRN2", target_bir_lowering=False)
        self.P = Prog()
        self.debug = set(debug)
        self.tap_evs = []
        self.dr = {}
        self.stk = contextlib.ExitStack()
        self.sb_bytes = 0
        self.bankst = {i: {"w": None, "r": []} for i in range(8)}
        self._dram()
        self._sbuf()
        self._psum()

    def _dram(self):
        nc = self.nc

        def din(name, shape, dtype=F32):
            self.dr[name] = nc.dram_tensor(name, list(shape), dtype,
                                           kind="ExternalInput").ap()

        def dscr(name, shape, dtype=F32, space="Local"):
            self.dr[name] = nc.dram_tensor(name, list(shape), dtype,
                                           addr_space=space).ap()

        din("x_slice", (TPC, D)); din("xhalo", (3, D))
        din("Wq", (D, KD)); din("Wk", (D, 2 * KD)); din("Wv", (D, 2 * KD))
        din("Wb", (D, 16)); din("Wa", (D, 8)); din("Wg", (D, KD))
        din("Wo", (KD, D))
        din("A_log_c", (8, 1)); din("dt_bias_c", (8, 1))
        din("convw_all", (128, 40 * CONV))
        din("norm_w_rep", (128, 128))
        din("w1_slice", (8, 128, D)); din("b1_col", (NIC, 128))
        din("w2_slice", (2, 128, I_FFN)); din("b2_col", (NFC, 128))
        din("chain_m_rep", (128, 7))
        din("amask", (R, R)); din("mmaskT", (R, C))
        din("ident_bf", (128, 128), BF16)
        din("ident_f32", (128, 128), F32)
        din("i64p_bf", (128, 64), BF16)
        din("ones_f32", (128, 128), F32)
        din("ones_col_bf", (128, 1), BF16)
        din("ones_row_bf", (1, 128), BF16)
        din("ones_row_f32", (1, 128), F32)
        self.dr["out"] = nc.dram_tensor("out", [TPC, D], F32,
                                        kind="ExternalOutput").ap()
        for name in self.debug:
            shape, dtype = DEBUG_TAPS[name]
            self.dr[name] = nc.dram_tensor(name, list(shape), dtype,
                                           kind="ExternalOutput").ap()

        dscr("h_spill", (NIC, 128, TPC), F32)
        dscr("opre_spill", (NCH, 64, H * 128), F32)
        dscr("ephi_spill", (NCH, 128, H * 64), BF16)
        dscr("g_spill", (NFC, 4, 128, 512), F32)
        dscr("gate_spill", (NTC, 128, KD), F32)
        dscr("qw1_ag_in", (8, 128, NFC * 128), BF16)
        dscr("qw1_ag_out", (NIC, 128, NFC * 128), BF16, "Shared")
        dscr("sw1_ag_in", (1, 8 * 128), F32)
        dscr("sw1_ag_out", (8, 8 * 128), F32, "Shared")
        dscr("qw2_ag_in", (2, 128, NIC * 128), BF16)
        dscr("qw2_ag_out", (NFC, 128, NIC * 128), BF16, "Shared")
        dscr("sw2_ag_in", (1, 2 * 128), F32)
        dscr("sw2_ag_out", (8, 2 * 128), F32, "Shared")
        dscr("s0_ag_in", (128, H * 128), F32)
        dscr("s0_ag_out", (8 * 128, H * 128), F32, "Shared")
        dscr("phi_ag_in", (128, H * 128), BF16)
        dscr("phi_ag_out", (8 * 128, H * 128), BF16, "Shared")
        dscr("mx_in", (1, 8), F32); dscr("mx_out", (1, 8), F32)
        dscr("mh_in", (1, 8), F32); dscr("mh_out", (1, 8), F32)

    def _sbuf(self):
        def s(name, shape, dtype):
            t = self.stk.enter_context(
                self.nc.sbuf_tensor(name, list(shape), dtype))
            self.sb_bytes += int(np.prod(shape[1:])) * \
                {F32: 4, BF16: 2, I16: 2}[dtype]
            setattr(self, name, t)
            return t

        s("arena", (128, 32768), BF16)
        s("xnh", (128, 64), BF16)
        s("qt", (128, NCH * 64), BF16)
        s("kt", (128, NCH * 128), BF16)
        s("vt", (128, NCH * 128), BF16)
        s("af32", (128, 10240), F32)
        s("wqi", (128, 2048), I16)
        s("wb2", (128, 4096), BF16)
        s("l2tmp", (128, 2048), BF16)
        s("xtok", (128, 2048), F32)
        s("mb", (128, 2944), BF16)
        s("Grows", (128, 8 * NCH), F32)
        s("nGrows", (128, 8 * NCH), F32)
        s("Brows", (128, 8 * NCH), F32)
        s("eGrows", (128, 8 * NCH), F32)
        s("S0", (128, H * 128), BF16)
        s("Phi", (128, H * 128), BF16)
        s("Sin", (128, H * 128), BF16)
        s("bagg", (16, TPC), F32)
        s("gfm", (8, TPC), F32)
        s("Gfm", (8, TPC), F32)
        s("ln_m", (128, 17), F32)
        s("ln_r", (128, 17), F32)
        s("amx", (128, 2), F32)
        s("sc", (128, 26), F32)
        s("bsc", (8, 2), F32)
        s("mxl", (1, 4), F32)
        s("qlat", (128, 16), F32)
        s("st1r", (1, 512), F32)
        s("st2r", (1, 512), F32)
        s("stb", (1, 512), BF16)
        s("qcol", (128, 8), F32)
        s("b1sb", (128, NIC), F32)
        s("b2sb", (128, NFC), F32)
        s("sw1sb", (128, NIC), F32)
        s("sw2sb", (128, NFC), F32)
        s("sc1sb", (128, NIC), F32)
        s("sc2sb", (128, NFC), F32)
        s("opre_st", (64, 2 * 128), F32)
        s("ephi_st", (128, 2 * 64), BF16)
        s("ephi_ld", (128, H * 64), BF16)
        s("otok", (128, KD), BF16)
        s("ofm", (128, KD), BF16)
        s("mrep", (128, 7), F32)
        s("mrep1", (128, 7), F32)
        s("pgath", (128, 1024), BF16)
        s("convw", (128, 40 * CONV), F32)
        s("epsc", (128, 1), F32)
        for nm, shape, dt in [("amask", (R, R), F32), ("mmaskT", (R, C), F32),
                              ("ident_bf", (128, 128), BF16),
                              ("ident_f32", (128, 128), F32),
                              ("i64p_bf", (128, 64), BF16),
                              ("ones_f32", (128, 128), F32),
                              ("ones_col_bf", (128, 1), BF16),
                              ("ones_row_bf", (1, 128), BF16),
                              ("ones_row_f32", (1, 128), F32),
                              ("norm_w_rep", (128, 128), F32),
                              ("A_log_c", (8, 1), F32),
                              ("dt_bias_c", (8, 1), F32)]:
            s("c_" + nm, shape, dt)
        assert self.sb_bytes <= 213000, f"sbuf over: {self.sb_bytes}"
        self.sgath = self.af32[:, 0:1024]
        self.gtok = self.af32[:, 1024:1536]
        self.opre_ld = self.af32[:, 4104:5128]
        self.fin = self.af32[:, 5128:6152]

    def _psum(self):
        self.psB = []
        for i in range(6):
            self.psB.append(self.stk.enter_context(
                self.nc.psum_tensor(f"psf{i}", [128, 512], F32)))
        self.psB.append(self.stk.enter_context(
            self.nc.psum_tensor("psb6", [128, 1024], BF16)))
        self.psB.append(self.stk.enter_context(
            self.nc.psum_tensor("psf7", [128, 512], F32)))

    def dma(self, q, out, in_, waits=(), sem=None, slow=False):
        sem = sem or {"sync": "d_sp", "gpsimd": "d_pool",
                      "scalar": "d_act"}[q]
        return self.P.on(q, partial(f_dma, out=out, in_=in_, slow=slow),
                         waits, inc=sem)

    def ve(self, fn, waits=(), inc="s_dve"):
        return self.P.on("vector", fn, waits, inc=inc, amt=1)

    def ac(self, fn, waits=(), inc="s_act"):
        return self.P.on("scalar", fn, waits, inc=inc)

    def po(self, fn, waits=(), inc="s_pool"):
        return self.P.on("gpsimd", fn, waits, inc=inc)

    def bank_w(self, bk, mms, waits=()):
        st = self.bankst[bk]
        w = list(waits) + st["r"] + ([st["w"]] if st["w"] else [])
        st["r"] = []
        ev = None
        for i, (out, lhsT, rhs, a, b) in enumerate(mms):
            ev = self.P.on("tensor",
                           partial(f_mm, out=out, lhsT=lhsT, rhs=rhs,
                                   start=a, stop=b),
                           w if i == 0 else (),
                           inc="s_te" if i == len(mms) - 1 else None)
        st["w"] = ev
        return ev

    def bank_w_more(self, bk, mms, waits=()):
        st = self.bankst[bk]
        ev = None
        for i, (out, lhsT, rhs, a, b) in enumerate(mms):
            ev = self.P.on("tensor",
                           partial(f_mm, out=out, lhsT=lhsT, rhs=rhs,
                                   start=a, stop=b),
                           list(waits) if i == 0 else (),
                           inc="s_te" if i == len(mms) - 1 else None)
        st["w"] = ev
        return ev

    def bank_tr(self, bk, trs, waits=()):
        st = self.bankst[bk]
        w = list(waits) + st["r"] + ([st["w"]] if st["w"] else [])
        st["r"] = []
        ev = None
        for i, (out, in_, ident) in enumerate(trs):
            ev = self.P.on("tensor",
                           partial(f_tr, out=out, in_=in_, ident=ident),
                           w if i == 0 else (),
                           inc="s_te" if i == len(trs) - 1 else None)
        st["w"] = ev
        return ev

    def bank_r(self, bk, eng, fn, waits=(), sem=None):
        st = self.bankst[bk]
        w = list(waits) + ([st["w"]] if st["w"] else [])
        sem = sem or {"vector": "s_dve", "scalar": "s_act"}[eng]
        ev = self.P.on(eng, fn, w, inc=sem, amt=1)
        st["r"].append(ev)
        return ev

    def tap(self, name, src, waits):
        if name in self.debug:
            self.tap_evs.append(self.dma("gpsimd", self.dr[name], src,
                                         waits))
            return [self.tap_evs[-1]]
        return []

    def xn_rhs(self, fc, t0, n):
        return self.arena[:, fc * 2048 + t0:fc * 2048 + t0 + n]

    def build(self):
        ev_c = self.ph_consts()
        ev_ln = self.ph_ln(ev_c)
        ev_mx = self.ph_sx_ar(ev_ln)
        ev_wq, ev_stage = self.ph_wquant(ev_c, ev_ln)
        ev_bg = self.ph_beta_g(ev_c, ev_ln)
        ev_at = self.ph_heads(ev_c, ev_bg, ev_stage)
        ev_ch = self.ph_chain_ag(ev_at)
        ev_qx = self.ph_qxn(ev_mx, ev_at, [])
        ev_f1 = self.ph_ffn1(ev_qx, ev_wq, ev_mx)
        ev_si = self.ph_combine(ev_ch)
        ev_f2, g_evs = self.ph_ffn2(ev_f1, ev_wq)
        ev_ou = self.ph_output(ev_f2, ev_si, g_evs)
        self.P.on("sync", f_nop, [merge(ev_ou, self.tap_evs)])
        sems = {}
        with self.stk:
            with contextlib.ExitStack() as ss:
                for name in sorted(self.P.sem_names):
                    sems[name] = ss.enter_context(self.nc.semaphore(name))
                with self.nc.Block() as block:
                    self.P.replay(block, sems)
        return self.nc

    def ph_consts(self):
        evs = []
        for nm in ["amask", "mmaskT", "ident_bf", "ident_f32", "i64p_bf",
                   "ones_f32", "ones_col_bf", "ones_row_bf", "ones_row_f32",
                   "norm_w_rep", "A_log_c", "dt_bias_c"]:
            evs.append(self.dma("sync", getattr(self, "c_" + nm)[:],
                                self.dr[nm][:, :]))
        evs.append(self.dma("sync", self.convw[:], self.dr["convw_all"][:, :]))
        evs.append(self.dma("sync", self.mrep[:],
                            self.dr["chain_m_rep"][:, :]))
        evs.append(self.dma("sync", self.b1sb[:],
                            self.dr["b1_col"].rearrange("i p -> p i"),
                            slow=True))
        evs.append(self.dma("sync", self.b2sb[:],
                            self.dr["b2_col"].rearrange("i p -> p i"),
                            slow=True))
        e2 = self.ve(partial(f_ts, out=self.mrep1[:], in0=self.mrep[:],
                             s1=-1.0, s2=1.0, op0=ALU.mult, op1=ALU.add), evs)
        e_eps = self.ve(partial(f_ts, out=self.epsc[:],
                                in0=self.c_ones_f32[:, 0:1], s1=1e-5,
                                s2=None, op0=ALU.mult, op1=ALU.bypass),
                        list(evs))
        evs.append(e_eps)
        e3 = self.ve(partial(f_memset, ap=self.S0[:], v=0.0))
        e4 = self.ve(partial(f_memset, ap=self.Sin[:], v=0.0))
        phi_evs = [self.ve(partial(f_copy,
                                   out=self.Phi[:, hh * 128:(hh + 1) * 128],
                                   in_=self.c_ident_bf[:]), evs)
                   for hh in range(H)]
        return merge(evs, e2, e3, e4, phi_evs)

    def ph_ln(self, ev_c):
        dr = self.dr
        done = []
        ev_mx = None
        st1 = self.af32[:, 4104:6152]
        for tt in range(NTC + 1):
            halo = tt == NTC
            rows = 3 if halo else 128
            src = dr["xhalo"][:, :] if halo else \
                dr["x_slice"][tt * 128:(tt + 1) * 128, :]
            evd = self.dma("sync", self.xtok[0:rows, :], src,
                           done[-NFC:] if done else ev_c, sem="d_lx")
            xs = self.xtok[0:rows, :]
            col = self.ln_m[0:rows, tt:tt + 1]
            rcol = self.ln_r[0:rows, tt:tt + 1]
            e1 = self.ac(partial(f_act, out=st1[0:rows, :], in_=xs,
                                 func=AF.Identity,
                                 accum=self.sc[0:rows, 0:1]), [evd])
            ef = self.ac(partial(f_copy, out=self.sc[0:rows, 3:4],
                                 in_=self.sc[0:rows, 0:1]), [e1])
            e2 = self.ve(partial(f_ts, out=col, in0=self.sc[0:rows, 0:1],
                                 s1=-1.0 / D, s2=None, op0=ALU.mult,
                                 op1=ALU.bypass), [ef])
            e3 = self.ac(partial(f_act, out=st1[0:rows, :], in_=xs,
                                 func=AF.Square, bias=col, scale=1.0,
                                 accum=self.sc[0:rows, 1:2]), [e2])
            e5 = self.ac(partial(f_act, out=self.sc[0:rows, 3:4],
                                 in_=self.sc[0:rows, 1:2], func=AF.Ln,
                                 bias=self.epsc[0:rows, 0:1],
                                 scale=1.0 / D), [e3])
            e6 = self.ac(partial(f_act, out=rcol,
                                 in_=self.sc[0:rows, 3:4], func=AF.Exp,
                                 scale=-0.5), [e5])
            e7 = self.ve(partial(f_ts, out=st1[0:rows, :], in0=xs,
                                 s1=col.opt(), s2=rcol.opt(), op0=ALU.add,
                                 op1=ALU.mult), [e6])
            if not halo:
                if tt == 0:
                    ev_mx = self.ve(partial(f_red, out=self.amx[:, 0:1],
                                            in_=st1, op=ALU.max,
                                            absval=True), [e7], inc="d_vx")
                else:
                    em = self.ve(partial(f_red, out=self.sc[:, 4:5], in_=st1,
                                         op=ALU.max, absval=True), [e7],
                                 inc="d_vx")
                    ev_mx = self.ve(partial(f_tt, out=self.amx[:, 0:1],
                                            in0=self.amx[:, 0:1],
                                            in1=self.sc[:, 4:5], op=ALU.max),
                                    [em, ev_mx])
            e8 = self.ve(partial(f_copy, out=self.l2tmp[0:rows, :],
                                 in_=st1[0:rows, :]), [e7])
            for fc in range(NFC):
                if halo:
                    self.bank_tr(6, [(self.psB[6][0:128, 0:3],
                                      self.l2tmp[0:3,
                                                 fc * 128:(fc + 1) * 128],
                                      self.c_ident_bf[0:3, 0:3])], [e8])
                    eve = self.bank_r(6, "vector", partial(
                        f_copy, out=self.xnh[:, fc * 4:fc * 4 + 3],
                        in_=self.psB[6][0:128, 0:3]))
                else:
                    self.bank_tr(6, [(self.psB[6][0:128, 0:128],
                                      self.l2tmp[:, fc * 128:(fc + 1) * 128],
                                      self.c_ident_bf[:])], [e8])
                    eve = self.bank_r(6, "vector", partial(
                        f_copy,
                        out=self.arena[:, fc * 2048 + tt * 128:
                                       fc * 2048 + (tt + 1) * 128],
                        in_=self.psB[6][0:128, 0:128]))
                done.append(eve)
        done.append(ev_mx)
        dm = merge(done)
        dm += self.tap("dbg_xnfm", self.arena[:, 0:2048], dm)
        dm += self.tap("dbg_xnf2", self.arena[:, 15 * 2048:16 * 2048], dm)
        return dm

    def ph_sx_ar(self, ev_ln):
        dr = self.dr
        self.bank_tr(7, [(self.psB[7][0:1, 0:128], self.amx[:, 0:1],
                          self.c_ident_f32[:])], ev_ln)
        e2 = self.bank_r(7, "vector", partial(
            f_red, out=self.st1r[0:1, 0:1], in_=self.psB[7][0:1, 0:128],
            op=ALU.max, absval=False))
        el1 = self.ve(partial(f_copy, out=self.mxl[0:1, 0:1],
                              in_=self.st1r[0:1, 0:1]), [e2])
        e3 = self.dma("gpsimd", dr["mx_in"][0:1, 0:1], self.st1r[0:1, 0:1],
                      [e2], sem="d_mx")
        e4 = self.po(partial(f_cc, kind="AllReduce", op=ALU.max,
                             ins=[dr["mx_in"].opt()],
                             outs=[dr["mx_out"].opt()]), [e3], inc="s_ccM")
        e5 = self.dma("gpsimd", self.st1r[0:1, 1:2], dr["mx_out"][0:1, 0:1],
                      [e4], sem="d_mx")
        el2 = self.ve(partial(f_copy, out=self.mxl[0:1, 1:2],
                              in_=self.st1r[0:1, 1:2]), [e5])
        self.bank_w(7, [(self.psB[7][0:128, 0:1], self.c_ones_row_f32[:],
                         self.st1r[0:1, 1:2], True, True)], [e5])
        e9 = self.bank_r(7, "vector", partial(
            f_ts, out=self.qcol[:, 1:2], in0=self.psB[7][0:128, 0:1],
            s1=1.0 / 127.0, s2=1e-8, op0=ALU.mult, op1=ALU.max))
        e10a = self.ac(partial(f_act, out=self.qcol[:, 4:5],
                               in_=self.qcol[:, 1:2], func=AF.Ln), [e9])
        e10 = self.ac(partial(f_act, out=self.qcol[:, 0:1],
                              in_=self.qcol[:, 4:5], func=AF.Exp,
                              scale=-1.0), [e10a])
        out = [e10]
        out += self.tap("dbg_sx", self.qcol[:, :], [e10])
        out += self.tap("dbg_amx", self.amx[:, :], [e10])
        out += self.tap("dbg_st1", self.st1r[0:1, 0:16], [e10])
        out += self.tap("dbg_mxl", self.mxl[0:1, :], [e10])
        out += self.tap("dbg_qlat", self.qlat[:, :], [e10])
        return out

    def ph_wquant(self, ev_c, ev_ln):
        dr = self.dr
        self.wq_slot = 0
        self.wq_dma = [None] * 4
        self.wq_lasteve = None
        self.wq_scst = None
        stA = self.af32[:, 0:2048]
        stB = self.af32[:, 2051:4099]
        done = []
        stage_last = None
        prev = list(ev_ln)

        def quant_2048(st, rec, waits):
            e4 = self.ve(partial(f_ts, out=self.wqi[:, 0:2048], in0=st,
                                 s1=rec.opt(), s2=None, op0=ALU.mult,
                                 op1=ALU.bypass), waits)
            e5 = self.ve(partial(f_copy, out=self.l2tmp[:, 0:2048],
                                 in_=self.wqi[:, 0:2048]),
                         [e4] + ([self.wq_lasteve] if self.wq_lasteve
                                 else []))
            return e4, e5

        for icl in range(8):
            st = stA if icl % 2 == 0 else stB
            evd = self.dma("sync", st, dr["w1_slice"][icl],
                           prev + (list(ev_ln) if icl == 0 else []),
                           sem="d_wv")
            if icl == 0 and "dbg_wst" in self.debug:
                evl = self.ve(partial(f_copy, out=self.af32[:, 5128:6152],
                                      in_=st[:, 0:1024]), [evd])
                self.tap("dbg_wst", self.af32[:, 5128:6152], [evl])
            e1 = self.ve(partial(f_red, out=self.sc[:, 5:6], in_=st,
                                 op=ALU.max, absval=True), [evd],
                         inc="d_vx")
            e2 = self.ve(partial(f_ts, out=self.sc[:, 6:7],
                                 in0=self.sc[:, 5:6], s1=1.0 / 127.0,
                                 s2=1e-8, op0=ALU.mult, op1=ALU.max),
                         [e1] + ([self.wq_scst] if self.wq_scst else []),
                         inc="d_vx")
            e3a = self.ac(partial(f_act, out=self.sc[:, 7:8],
                                  in_=self.sc[:, 6:7], func=AF.Ln), [e2])
            e3 = self.ac(partial(f_act, out=self.sc[:, 5:6],
                                 in_=self.sc[:, 7:8], func=AF.Exp,
                                 scale=-1.0), [e3a])
            evst = self.dma(
                "gpsimd",
                dr["sw1_ag_in"][0:1, icl * 128:(icl + 1) * 128]
                .rearrange("o p -> p o"), self.sc[:, 6:7], [e2], slow=True,
                sem="d_ws")
            done.append(evst)
            self.wq_scst = evst
            if icl == 0 and "dbg_wsc" in self.debug:
                evl2 = self.ve(partial(f_copy, out=self.af32[:, 6152:6160],
                                       in_=self.sc[:, 5:13]), [e2, evst])
                self.tap("dbg_wsc", self.af32[:, 6152:6160], [evl2])
            e4, e5 = quant_2048(st, self.sc[:, 5:6], [e3])
            stage_last = e4
            prev = [e4]
            for fc in range(NFC):
                slot = self.wq_slot % 4
                self.wq_slot += 1
                so = self.ofm[:, slot * 128:(slot + 1) * 128]
                self.bank_tr(6, [(self.psB[6][0:128, 0:128],
                                  self.l2tmp[:, fc * 128:(fc + 1) * 128],
                                  self.c_ident_bf[:])], [e5] + ev_c)
                eve = self.bank_r(6, "scalar", partial(
                    f_copy, out=so, in_=self.psB[6][0:128, 0:128]),
                    [self.wq_dma[slot]] if self.wq_dma[slot] else [])
                self.wq_dma[slot] = self.dma(
                    "gpsimd",
                    dr["qw1_ag_in"][icl, :, fc * 128:(fc + 1) * 128],
                    so, [eve], sem="d_t%d" % slot)
                done.append(self.wq_dma[slot])
                self.wq_lasteve = eve
        for ocl in range(2):
            mx = []
            for q in range(4):
                st = stA if q % 2 == 0 else stB
                evd = self.dma("sync", st,
                               dr["w2_slice"][ocl][:, q * 2048:(q + 1) * 2048],
                               prev, sem="d_wv")
                e1 = self.ve(partial(f_red, out=self.sc[:, 8 + q:9 + q],
                                     in_=st, op=ALU.max, absval=True), [evd],
                             inc="d_vx")
                prev = [e1]
                mx.append(e1)
            em1 = self.ve(partial(f_tt, out=self.sc[:, 12:13],
                                  in0=self.sc[:, 8:9], in1=self.sc[:, 9:10],
                                  op=ALU.max), mx)
            em2 = self.ve(partial(f_tt, out=self.sc[:, 13:14],
                                  in0=self.sc[:, 10:11],
                                  in1=self.sc[:, 11:12], op=ALU.max), [em1])
            em3 = self.ve(partial(f_tt, out=self.sc[:, 12:13],
                                  in0=self.sc[:, 12:13],
                                  in1=self.sc[:, 13:14], op=ALU.max), [em2])
            e2 = self.ve(partial(f_ts, out=self.sc[:, 14:15],
                                 in0=self.sc[:, 12:13], s1=1.0 / 127.0,
                                 s2=1e-8, op0=ALU.mult, op1=ALU.max),
                         [em3] + ([self.wq_scst] if self.wq_scst else []),
                         inc="d_vx")
            e3a = self.ac(partial(f_act, out=self.sc[:, 15:16],
                                  in_=self.sc[:, 14:15], func=AF.Ln), [e2])
            e3 = self.ac(partial(f_act, out=self.sc[:, 13:14],
                                 in_=self.sc[:, 15:16], func=AF.Exp,
                                 scale=-1.0), [e3a])
            evst = self.dma(
                "gpsimd",
                dr["sw2_ag_in"][0:1, ocl * 128:(ocl + 1) * 128]
                .rearrange("o p -> p o"), self.sc[:, 14:15], [e2], slow=True,
                sem="d_ws")
            done.append(evst)
            self.wq_scst = evst
            for q in range(4):
                st = stA if q % 2 == 0 else stB
                evd = self.dma("sync", st,
                               dr["w2_slice"][ocl][:, q * 2048:(q + 1) * 2048],
                               prev, sem="d_wv")
                e4, e5 = quant_2048(st, self.sc[:, 13:14], [evd, e3])
                stage_last = e4
                prev = [e4]
                for t16 in range(16):
                    ic = q * 16 + t16
                    slot = self.wq_slot % 4
                    self.wq_slot += 1
                    so = self.ofm[:, slot * 128:(slot + 1) * 128]
                    self.bank_tr(6, [(self.psB[6][0:128, 0:128],
                                      self.l2tmp[:, t16 * 128:(t16 + 1) * 128],
                                      self.c_ident_bf[:])], [e5])
                    eve = self.bank_r(6, "scalar", partial(
                        f_copy, out=so, in_=self.psB[6][0:128, 0:128]),
                        [self.wq_dma[slot]] if self.wq_dma[slot] else [])
                    self.wq_dma[slot] = self.dma(
                        "gpsimd",
                        dr["qw2_ag_in"][ocl, :, ic * 128:(ic + 1) * 128],
                        so, [eve], sem="d_t%d" % slot)
                    done.append(self.wq_dma[slot])
                    self.wq_lasteve = eve

        a1 = self.po(partial(f_cc, kind="AllGather", op=ALU.bypass,
                             ins=[dr["qw1_ag_in"].opt()],
                             outs=[dr["qw1_ag_out"].opt()]), done, inc="s_ccW")
        a2 = self.po(partial(f_cc, kind="AllGather", op=ALU.bypass,
                             ins=[dr["sw1_ag_in"].opt()],
                             outs=[dr["sw1_ag_out"].opt()]), done, inc="s_ccW")
        a3 = self.po(partial(f_cc, kind="AllGather", op=ALU.bypass,
                             ins=[dr["qw2_ag_in"].opt()],
                             outs=[dr["qw2_ag_out"].opt()]), done, inc="s_ccW")
        a4 = self.po(partial(f_cc, kind="AllGather", op=ALU.bypass,
                             ins=[dr["sw2_ag_in"].opt()],
                             outs=[dr["sw2_ag_out"].opt()]), done, inc="s_ccW")
        e1 = self.dma("sync", self.sw1sb[:, :],
                      dr["sw1_ag_out"].rearrange("r (i p) -> p (r i)", p=128),
                      [a4], slow=True, sem="d_sw")
        e2 = self.dma("sync", self.sw2sb[:, :],
                      dr["sw2_ag_out"].rearrange("r (i p) -> p (r i)", p=128),
                      [a4], slow=True, sem="d_sw")
        return {"qw1": a4, "sw1": e1, "qw2": a4, "sw2": e2}, [stage_last]

    def ph_beta_g(self, ev_c, ev_ln):
        dr = self.dr
        evb = self.dma("gpsimd",
                       self.wb2[:, 0:NFC * 16].rearrange(
                           "p (fc o) -> p fc o", o=16),
                       dr["Wb"].rearrange("(fc p) o -> p fc o", p=128),
                       ev_ln)
        eva = self.dma("gpsimd",
                       self.wb2[:, NFC * 16:NFC * 24].rearrange(
                           "p (fc o) -> p fc o", o=8),
                       dr["Wa"].rearrange("(fc p) o -> p fc o", p=128),
                       ev_ln)
        ea = self.ac(partial(f_act, out=self.bsc[0:8, 0:1],
                             in_=self.c_A_log_c[0:8, 0:1], func=AF.Exp),
                     ev_c)
        enA = self.ve(partial(f_ts, out=self.bsc[0:8, 1:2],
                              in0=self.bsc[0:8, 0:1], s1=-1.0, s2=None,
                              op0=ALU.mult, op1=ALU.bypass), [ea])
        g_evs = []
        for tt4 in range(4):
            t0 = tt4 * 512
            mmsb = [(self.psB[0][0:16, 0:512],
                     self.wb2[:, fc * 16:(fc + 1) * 16],
                     self.xn_rhs(fc, t0, 512), fc == 0, fc == NFC - 1)
                    for fc in range(NFC)]
            self.bank_w(0, mmsb, [evb] + ev_ln)
            e1 = self.bank_r(0, "scalar", partial(
                f_act, out=self.bagg[0:16, t0:t0 + 512],
                in_=self.psB[0][0:16, 0:512], func=AF.Sigmoid))
            eb = self.ve(partial(f_ts, out=self.bagg[0:16, t0:t0 + 512],
                                 in0=self.bagg[0:16, t0:t0 + 512], s1=2.0,
                                 s2=None, op0=ALU.mult, op1=ALU.bypass),
                         [e1])
            mmsa = [(self.psB[1][0:8, 0:512],
                     self.wb2[:, NFC * 16 + fc * 8:NFC * 16 + (fc + 1) * 8],
                     self.xn_rhs(fc, t0, 512), fc == 0, fc == NFC - 1)
                    for fc in range(NFC)]
            self.bank_w(1, mmsa, [eva] + ev_ln)
            e2 = self.bank_r(1, "scalar", partial(
                f_act, out=self.gfm[0:8, t0:t0 + 512],
                in_=self.psB[1][0:8, 0:512], func=AF.Exp,
                bias=self.c_dt_bias_c[0:8, 0:1], scale=1.0), ev_c)
            e3 = self.ac(partial(f_act, out=self.Gfm[0:8, t0:t0 + 512],
                                 in_=self.gfm[0:8, t0:t0 + 512], func=AF.Ln,
                                 bias=1.0, scale=1.0), [e2])
            e4 = self.ve(partial(f_ts, out=self.gfm[0:8, t0:t0 + 512],
                                 in0=self.Gfm[0:8, t0:t0 + 512],
                                 s1=self.bsc[0:8, 1:2].opt(), s2=None,
                                 op0=ALU.mult, op1=ALU.bypass), [e3, enA])
            g_evs.append((eb, e4))
        out_evs = []
        for cch in range(NCH):
            sl = slice(cch * C, (cch + 1) * C)
            eb, e4 = g_evs[cch // 8]
            e1 = self.ve(partial(f_scan, out=self.Gfm[0:8, sl],
                                 d0=self.gfm[0:8, sl], d1=self.gfm[0:8, sl],
                                 init=0.0, op0=ALU.add, op1=ALU.bypass),
                         [e4], inc="d_vx")
            self.bank_tr(7, [(self.psB[7][0:64, 0:8], self.Gfm[0:8, sl],
                              self.c_ident_f32[0:8, 0:8])], [e1])
            e2 = self.bank_r(7, "vector", partial(
                f_copy, out=self.Grows[0:64, cch * 8:(cch + 1) * 8],
                in_=self.psB[7][0:64, 0:8]))
            e2b = self.dma("sync", self.Grows[64:128, cch * 8:(cch + 1) * 8],
                           self.Grows[0:64, cch * 8:(cch + 1) * 8], [e2])
            e3 = self.ve(partial(f_ts,
                                 out=self.nGrows[:, cch * 8:(cch + 1) * 8],
                                 in0=self.Grows[:, cch * 8:(cch + 1) * 8],
                                 s1=-1.0, s2=None, op0=ALU.mult,
                                 op1=ALU.bypass), [e2b])
            e3b = self.ac(partial(f_act,
                                  out=self.eGrows[:, cch * 8:(cch + 1) * 8],
                                  in_=self.Grows[:, cch * 8:(cch + 1) * 8],
                                  func=AF.Exp), [e2b])
            self.bank_tr(7, [(self.psB[7][0:64, 16:32], self.bagg[0:16, sl],
                              self.c_ident_f32[0:16, 0:16])], [eb])
            e5 = self.bank_r(7, "vector", partial(
                f_copy, out=self.Brows[0:64, cch * 8:(cch + 1) * 8],
                in_=self.psB[7][0:64, 16:24]))
            e5c = self.ve(partial(
                f_copy, out=self.sc[0:64, 10:18][:, 0:8]
                if False else self.Brows[0:64, cch * 8:(cch + 1) * 8],
                in_=self.psB[7][0:64, 16:24]), []) if False else None
            e5b0 = self.bank_r(7, "vector", partial(
                f_copy, out=self.l2tmp[0:64, 0:8].bitcast(F32)
                if False else self.af32[0:64, 9160:9168],
                in_=self.psB[7][0:64, 24:32]))
            e5b = self.dma("sync", self.Brows[64:128, cch * 8:(cch + 1) * 8],
                           self.af32[0:64, 9160:9168], [e5b0])
            out_evs += [e3, e3b, e5, e5b]
        dm = merge(out_evs)
        dm += self.tap("dbg_bagg", self.bagg[0:16, :], dm)
        dm += self.tap("dbg_G", self.Gfm[:, :], dm)
        dm += self.tap("dbg_Br", self.Brows[:, :], dm)
        dm += self.tap("dbg_Gr", self.Grows[:, :], dm)
        return dm

    def ph_heads(self, ev_c, ev_bg, ev_stage):
        dr = self.dr
        done = []
        prev_mach = list(ev_bg) + list(ev_stage)
        for h in range(H):
            par = h % 2
            wsl = self.wb2[:, par * 2048:(par + 1) * 2048]
            specs = [("Wq", h * 128), ("Wk", h * 128), ("Wk", KD + h * 128),
                     ("Wv", h * 128), ("Wv", KD + h * 128)]
            oc_global = [h, 8 + h, 16 + h, 24 + h, 32 + h]
            pc_ev = {}
            wprev = prev_mach
            for wi, (wn, c0) in enumerate(specs):
                evw = self.dma("gpsimd", wsl.rearrange(
                    "p (fc o) -> p fc o", o=128),
                    dr[wn].rearrange("(fc p) o -> p fc o", p=128)[
                        :, :, c0:c0 + 128], wprev)
                pre = self.af32[:, 0:2051] if wi % 2 == 0 else \
                    self.af32[:, 2051:4102]
                pre_evs = []
                for tt4 in range(4):
                    bk = tt4 % 2
                    mms = [(self.psB[bk][0:128, 0:512],
                            wsl[:, fc * 128:(fc + 1) * 128],
                            self.xn_rhs(fc, tt4 * 512, 512),
                            fc == 0, fc == NFC - 1) for fc in range(NFC)]
                    self.bank_w(bk, mms,
                                [evw] + (prev_mach if wi < 2 else []))
                    eng = "scalar" if tt4 % 2 == 0 else "vector"
                    pre_evs.append(self.bank_r(bk, eng, partial(
                        f_copy,
                        out=pre[:, 3 + tt4 * 512:3 + (tt4 + 1) * 512],
                        in_=self.psB[bk][0:128, 0:512])))
                mmsh = [(self.psB[2][0:128, 0:3],
                         wsl[:, fc * 128:(fc + 1) * 128],
                         self.xnh[:, fc * 4:fc * 4 + 3],
                         fc == 0, fc == NFC - 1) for fc in range(NFC)]
                self.bank_w(2, mmsh, [evw])
                pre_evs.append(self.bank_r(2, "vector", partial(
                    f_copy, out=pre[:, 0:3], in_=self.psB[2][0:128, 0:3])))
                wprev = pre_evs
                ocg = oc_global[wi]
                cw = self.convw
                cvA = self.af32[:, 4104:6152]
                cvB = self.af32[:, 6152:8200]
                e1 = self.ve(partial(
                    f_ts, out=cvA[:, 0:2048], in0=pre[:, 0:2048],
                    s1=cw[:, ocg * CONV:ocg * CONV + 1].opt(), s2=None,
                    op0=ALU.mult, op1=ALU.bypass), pre_evs)
                e2 = self.ve(partial(
                    f_stt, out=cvB[:, 0:2048], in0=pre[:, 1:2049],
                    s=cw[:, ocg * CONV + 1:ocg * CONV + 2].opt(),
                    in1=cvA[:, 0:2048], op0=ALU.mult, op1=ALU.add), [e1])
                e3 = self.ve(partial(
                    f_stt, out=cvA[:, 0:2048], in0=pre[:, 2:2050],
                    s=cw[:, ocg * CONV + 2:ocg * CONV + 3].opt(),
                    in1=cvB[:, 0:2048], op0=ALU.mult, op1=ALU.add), [e2])
                e4 = self.ve(partial(
                    f_stt, out=cvB[:, 0:2048], in0=pre[:, 3:2051],
                    s=cw[:, ocg * CONV + 3:ocg * CONV + 4].opt(),
                    in1=cvA[:, 0:2048], op0=ALU.mult, op1=ALU.add), [e3])
                e5 = self.ac(partial(f_act, out=cvA[:, 0:2048],
                                     in_=cvB[:, 0:2048], func=AF.Silu),
                             [e4])
                if wi >= 3:
                    j = wi - 3
                    dst = self.vt.ap().rearrange(
                        "p (cc two t) -> p cc two t", two=2, t=64)[:, :, j, :]
                    pc_ev[wi] = [self.ve(partial(
                        f_copy, out=dst,
                        in_=cvA[:, 0:2048].rearrange(
                            "p (cc t) -> p cc t", t=64)), [e5])]
                else:
                    evs_l2 = []
                    for tt4 in range(4):
                        t0 = tt4 * 512
                        sq = self.ve(partial(
                            f_tt, out=self.l2tmp[:, t0:t0 + 512],
                            in0=cvA[:, t0:t0 + 512],
                            in1=cvA[:, t0:t0 + 512], op=ALU.mult), [e5])
                        self.bank_w(5, [(self.psB[5][0:1, 0:512],
                                         self.c_ones_col_bf[:],
                                         self.l2tmp[:, t0:t0 + 512],
                                         True, True)], [sq])
                        n1 = self.bank_r(5, "vector", partial(
                            f_ts, out=self.st1r[0:1, 0:512],
                            in_=None, in0=self.psB[5][0:1, 0:512],
                            s1=1e-30, s2=None, op0=ALU.max,
                            op1=ALU.bypass) if False else partial(
                            f_ts, out=self.st1r[0:1, 0:512],
                            in0=self.psB[5][0:1, 0:512], s1=1e-30, s2=None,
                            op0=ALU.max, op1=ALU.bypass))
                        n2 = self.ac(partial(f_act,
                                             out=self.st1r[0:1, 0:512],
                                             in_=self.st1r[0:1, 0:512],
                                             func=AF.Ln), [n1])
                        n2b = self.ac(partial(f_act,
                                              out=self.st2r[0:1, 0:512],
                                              in_=self.st1r[0:1, 0:512],
                                              func=AF.Exp, scale=-0.5),
                                      [n2])
                        n3 = self.ve(partial(f_ts,
                                             out=self.st2r[0:1, 0:512],
                                             in0=self.st2r[0:1, 0:512],
                                             s1=1e12, s2=None, op0=ALU.min,
                                             op1=ALU.bypass), [n2b])
                        if wi == 0:
                            n3 = self.ve(partial(
                                f_ts, out=self.st2r[0:1, 0:512],
                                in0=self.st2r[0:1, 0:512], s1=ISQD, s2=None,
                                op0=ALU.mult, op1=ALU.bypass), [n3])
                        n4 = self.ve(partial(f_copy, out=self.stb[0:1, 0:512],
                                             in_=self.st2r[0:1, 0:512]),
                                     [n3])
                        self.bank_w(4, [(self.psB[4][0:128, 0:512],
                                         self.c_ones_row_bf[:],
                                         self.stb[0:1, 0:512], True, True)],
                                    [n4])
                        if wi == 0:
                            dst = self.qt[:, t0:t0 + 512]
                        else:
                            j = wi - 1
                            dst = self.kt.ap().rearrange(
                                "p (cc two t) -> p cc two t", two=2, t=64)[
                                :, 8 * tt4:8 * tt4 + 8, j, :]
                        if wi == 0:
                            evs_l2.append(self.bank_r(4, "vector", partial(
                                f_tt, out=dst, in0=cvA[:, t0:t0 + 512],
                                in1=self.psB[4][0:128, 0:512],
                                op=ALU.mult)))
                        else:
                            evs_l2.append(self.bank_r(4, "vector", partial(
                                f_tt, out=dst,
                                in0=cvA[:, t0:t0 + 512].rearrange(
                                    "p (cc t) -> p cc t", t=64),
                                in1=self.psB[4][0:128, 0:512].rearrange(
                                    "p (cc t) -> p cc t", t=64),
                                op=ALU.mult)))
                    pc_ev[wi] = evs_l2
            evwg = self.dma("gpsimd", wsl.rearrange(
                "p (fc o) -> p fc o", o=128),
                dr["Wg"].rearrange("(fc p) o -> p fc o", p=128)[
                    :, :, h * 128:(h + 1) * 128], wprev)
            gate_done = []
            for tcq in range(NTC):
                bk = tcq % 2
                mms = [(self.psB[bk][0:128, 0:128],
                        self.xn_rhs(fc, tcq * 128, 128),
                        wsl[:, fc * 128:(fc + 1) * 128],
                        fc == 0, fc == NFC - 1) for fc in range(NFC)]
                self.bank_w(bk, mms, [evwg])
                gst = self.af32[:, 4104 + bk * 128:4104 + bk * 128 + 128]
                e1 = self.bank_r(bk, "vector", partial(
                    f_copy, out=gst, in_=self.psB[bk][0:128, 0:128]),
                    [gate_done[-2]] if len(gate_done) >= 2 else [])
                gate_done.append(self.dma(
                    "scalar",
                    dr["gate_spill"][tcq, :, h * 128:(h + 1) * 128], gst,
                    [e1]))
            wprev = gate_done[-2:]
            ev_in = merge(pc_ev[0], pc_ev[1], pc_ev[2], pc_ev[3], pc_ev[4],
                          ev_bg, ev_c)
            if h == 0:
                ev_in += self.tap("dbg_qt", self.qt[:, :], ev_in)
                ev_in += self.tap("dbg_kt", self.kt[:, :], ev_in)
                ev_in += self.tap("dbg_vt", self.vt[:, :], ev_in)
            prev_mach = merge(self.machinery_head(h, ev_in), gate_done)
            done += prev_mach
        dm = merge(done)
        dm += self.tap("dbg_S0", self.S0[:, :], dm)
        dm += self.tap("dbg_Phi", self.Phi[:, :], dm)
        dm += self.tap("dbg_gate", self.dr["gate_spill"][0], dm)
        return dm

    def machinery_head(self, h, ev_in):
        dr = self.dr
        mbv = self.mb
        prev_tail = list(ev_in)
        out_evs = []
        spill_prev = [None, None]
        for cch in range(NCH):
            par = cch % 2
            MA = 8200 if par == 0 else 9224
            mo = par * 1472

            def ma(a, b, MA=MA):
                return self.af32[:, MA + a:MA + b]

            def mbo(a, b, mo=mo):
                return mbv[:, mo + a:mo + b]

            scc = 16 + par * 5
            gr = self.Grows[:, cch * 8 + h:cch * 8 + h + 1]
            ngr = self.nGrows[:, cch * 8 + h:cch * 8 + h + 1]
            br = self.Brows[:, cch * 8 + h:cch * 8 + h + 1]
            egr = self.eGrows[:, cch * 8 + h:cch * 8 + h + 1]
            ktc = self.kt[:, cch * 128:(cch + 1) * 128]
            qtc = self.qt[:, cch * 64:(cch + 1) * 64]
            S0s = self.S0[:, h * 128:(h + 1) * 128]
            Phs = self.Phi[:, h * 128:(h + 1) * 128]
            w0 = ev_in if cch < 2 else prev_tail
            yv = ma(0, 128)
            e1 = self.ve(partial(f_ts, out=yv, in0=self.c_ones_f32[:],
                                 s1=gr.opt(), s2=None, op0=ALU.mult,
                                 op1=ALU.bypass), w0)
            self.bank_tr(7, [(self.psB[7][0:128, 0:128], yv,
                              self.c_ident_f32[:])], [e1])
            pre2 = ma(0, 128)
            ev_pr2 = self.bank_r(7, "vector", partial(
                f_ts, out=pre2, in0=self.psB[7][0:128, 0:128],
                s1=gr.opt(), s2=0.0, op0=ALU.subtract, op1=ALU.max))
            e1n = ma(128, 256)
            ev_e1n = self.ac(partial(f_act, out=e1n, in_=pre2, func=AF.Exp,
                                     scale=-1.0), [ev_pr2])
            pre1 = ma(256, 320)
            ev_pr1 = self.bank_r(7, "vector", partial(
                f_ts, out=pre1, in0=self.psB[7][0:128, 0:64],
                s1=gr.opt(), s2=0.0, op0=ALU.subtract, op1=ALU.min))
            e1p = pre1
            ev_e1p = self.ac(partial(f_act, out=e1p, in_=pre1, func=AF.Exp),
                             [ev_pr1])
            lamq = ma(320, 384)
            ev_lq = self.bank_r(7, "scalar", partial(
                f_act, out=lamq, in_=self.psB[7][0:128, 0:64], func=AF.Exp))
            g63 = self.sc[:, scc:scc + 1]
            ev_g63 = self.bank_r(7, "vector", partial(
                f_copy, out=g63, in_=self.psB[7][0:128, 63:64]))
            khsc = self.sc[:, scc + 1:scc + 2]
            ev_kh = self.ac(partial(f_act, out=khsc, in_=gr, func=AF.Exp,
                                    bias=g63, scale=-1.0), [ev_g63])
            lend = self.sc[:, scc + 2:scc + 3]
            ev_le = self.ac(partial(f_act, out=lend, in_=g63, func=AF.Exp),
                            [ev_g63])
            self.bank_w(0, [(self.psB[0][0:128, 0:128], ktc, ktc,
                             True, True)], w0)
            e1nm = ma(384, 512)
            ev_m = self.ve(partial(f_tt, out=e1nm, in0=e1n,
                                   in1=self.c_amask[:], op=ALU.mult),
                           [ev_e1n])
            nbr = self.sc[:, scc + 3:scc + 4]
            ev_nb = self.ve(partial(f_ts, out=nbr, in0=br, s1=-1.0, s2=None,
                                    op0=ALU.mult, op1=ALU.bypass), w0)
            if h == 0 and cch == 0 and "dbg_P" in self.debug:
                evdbg = self.bank_r(0, "vector", partial(
                    f_copy, out=self.xtok[:, 0:128],
                    in_=self.psB[0][0:128, 0:128]))
                self.tap("dbg_P", self.xtok[:, 0:128], [evdbg])
            nab = mbo(1024, 1152)
            ev_nab = self.bank_r(0, "vector", partial(
                f_stt, out=nab, in0=self.psB[0][0:128, 0:128], s=nbr.opt(),
                in1=e1nm, op0=ALU.mult, op1=ALU.mult), [ev_m, ev_nb])
            if h == 0 and cch == 0 and "dbg_naf" in self.debug:
                ev2dbg = self.ve(partial(f_copy, out=self.xtok[:, 128:256],
                                         in_=nab), [ev_nab])
                self.tap("dbg_naf", self.xtok[:, 128:256], [ev2dbg])
            ppow = mbo(640, 768)
            ev_pp = self.ve(partial(f_copy, out=ppow, in_=nab), [ev_nab])
            self.bank_tr(6, [(self.psB[6][0:128, 0:128], ppow,
                              self.c_ident_bf[:])], [ev_pp])
            ppowT = mbo(768, 896)
            evT = self.bank_r(6, "scalar", partial(
                f_copy, out=ppowT, in_=self.psB[6][0:128, 0:128]))
            gacc = mbo(896, 1024)
            evG = self.ve(partial(f_tt, out=gacc, in0=ppowT,
                                  in1=self.c_ident_bf[:], op=ALU.add),
                          [evT])
            evTn = evT
            for j in range(6):
                self.bank_w(1, [(self.psB[1][0:128, 0:128], ppowT, ppow,
                                 True, True)], [evG, evTn])
                evPn = self.bank_r(1, "vector", partial(
                    f_copy, out=ppow, in_=self.psB[1][0:128, 0:128]))
                if j < 5:
                    self.bank_tr(6, [(self.psB[6][0:128, 0:128], ppow,
                                      self.c_ident_bf[:])], [evPn])
                    evTn = self.bank_r(6, "scalar", partial(
                        f_copy, out=ppowT, in_=self.psB[6][0:128, 0:128]))
                self.bank_w(2, [(self.psB[2][0:128, 0:128], ppow, gacc,
                                 True, True)], [evPn])
                evG = self.bank_r(2, "vector", partial(
                    f_tt, out=gacc, in0=gacc,
                    in1=self.psB[2][0:128, 0:128], op=ALU.add))
            X = mbo(1216, 1472)
            vch = self.vt[:, cch * 128:(cch + 1) * 128]
            kch = self.kt[:, cch * 128:(cch + 1) * 128]
            self.bank_tr(6, [(self.psB[6][0:128, 0:128], vch,
                              self.c_ident_bf[:])], w0 + [evG])
            evxv = self.bank_r(6, "vector", partial(
                f_ts, out=X[:, 0:128], in0=self.psB[6][0:128, 0:128],
                s1=br.opt(), s2=None, op0=ALU.mult, op1=ALU.bypass))
            bl = self.sc[:, scc + 4:scc + 5]
            ev_bl = self.ve(partial(f_tt, out=bl, in0=br, in1=egr,
                                    op=ALU.mult), w0)
            self.bank_tr(6, [(self.psB[6][0:128, 0:128], kch,
                              self.c_ident_bf[:])], [evxv])
            evxk = self.bank_r(6, "vector", partial(
                f_ts, out=X[:, 128:256], in0=self.psB[6][0:128, 0:128],
                s1=bl.opt(), s2=None, op0=ALU.mult, op1=ALU.bypass),
                [ev_bl])
            khat = mbo(256, 384)
            evkh = self.bank_r(6, "vector", partial(
                f_ts, out=khat, in0=self.psB[6][0:128, 0:128],
                s1=khsc.opt(), s2=None, op0=ALU.mult, op1=ALU.bypass),
                [ev_kh])
            self.bank_w(3, [(self.psB[3][0:128, 0:256], gacc, X,
                             True, True)], [evxv, evxk, evG])
            U = mbo(0, 256)
            evUe = self.bank_r(3, "vector", partial(
                f_copy, out=U, in_=self.psB[3][0:128, 0:256]))
            Ul, W = U[:, 0:128], U[:, 128:256]
            self.bank_w(4, [(self.psB[4][0:128, 0:64], ktc, qtc,
                             True, True)], w0 + [evG])
            e1q = ma(640, 704)
            ev_eq = self.ve(partial(f_tt, out=e1q, in0=e1p,
                                    in1=self.c_mmaskT[:, 0:64],
                                    op=ALU.mult), [ev_e1p])
            mtt = mbo(512, 576)
            ev_mt = self.bank_r(4, "vector", partial(
                f_tt, out=mtt, in0=self.psB[4][0:128, 0:64], in1=e1q,
                op=ALU.mult), [ev_eq])
            qlam = mbo(1152, 1216)
            ev_ql = self.ve(partial(f_tt, out=qlam, in0=qtc, in1=lamq,
                                    op=ALU.mult), [ev_lq])
            self.bank_w(4, [(self.psB[4][0:128, 0:64], W, mtt,
                             True, True)], [evUe, ev_mt])
            Et = mbo(576, 640)
            ev_et = self.bank_r(4, "vector", partial(
                f_tt, out=Et, in0=qlam, in1=self.psB[4][0:128, 0:64],
                op=ALU.subtract), [ev_ql])
            self.bank_w(5, [(self.psB[5][0:128, 0:128], W, khat,
                             True, True)], [evUe, evkh])
            diagl = ma(704, 832)
            ev_dg = self.ve(partial(f_ts, out=diagl,
                                    in0=self.c_ident_f32[:],
                                    s1=lend.opt(), s2=None, op0=ALU.mult,
                                    op1=ALU.bypass), [ev_le])
            Ft = mbo(384, 512)
            ev_ft = self.bank_r(5, "vector", partial(
                f_tt, out=Ft, in0=diagl, in1=self.psB[5][0:128, 0:128],
                op=ALU.subtract), [ev_dg])
            self.bank_w(0, [(self.psB[0][0:128, 0:128], khat, Ul,
                             True, True)], [evUe, evkh])
            sloc = ma(832, 960)
            ev_sl = self.bank_r(0, "vector", partial(
                f_copy, out=sloc, in_=self.psB[0][0:128, 0:128]))
            self.bank_w(1, [
                (self.psB[1][0:64, 0:128], mtt, Ul, True, False),
                (self.psB[1][0:64, 0:128], Et, S0s, False, True)],
                [evUe, ev_mt, ev_et] + prev_tail)
            sp = spill_prev[par]
            ev_ope = self.bank_r(1, "scalar", partial(
                f_copy, out=self.opre_st[0:64, par * 128:(par + 1) * 128],
                in_=self.psB[1][0:64, 0:128]), [sp] if sp else [])
            self.bank_w(2, [(self.psB[2][0:128, 0:64], Phs, Et,
                             True, True)], [ev_et] + prev_tail)
            ev_epe = self.bank_r(2, "scalar", partial(
                f_copy, out=self.ephi_st[:, par * 64:(par + 1) * 64],
                in_=self.psB[2][0:128, 0:64]), [sp] if sp else [])
            ev_sp1 = self.dma("scalar",
                              dr["opre_spill"][cch][:, h * 128:(h + 1) * 128],
                              self.opre_st[0:64, par * 128:(par + 1) * 128],
                              [ev_ope], sem="d_ms")
            ev_sp2 = self.dma("scalar",
                              dr["ephi_spill"][cch][:, h * 64:(h + 1) * 64],
                              self.ephi_st[:, par * 64:(par + 1) * 64],
                              [ev_epe], sem="d_ms")
            spill_prev[par] = merge(ev_sp1, ev_sp2)[0]
            self.bank_w(5, [
                (self.psB[5][0:128, 0:128], Ft, S0s, True, True),
                (self.psB[5][0:128, 128:256], Ft, Phs, True, True)],
                [ev_ft])
            ev_s0 = self.bank_r(5, "vector", partial(
                f_tt, out=S0s, in0=self.psB[5][0:128, 0:128], in1=sloc,
                op=ALU.add), [ev_sl])
            ev_ph = self.bank_r(5, "vector", partial(
                f_copy, out=Phs, in_=self.psB[5][0:128, 128:256]))
            prev_tail = [ev_s0, ev_ph]
            if h == 0 and cch == 0:
                prev_tail = prev_tail + \
                    self.tap("dbg_mach", mbv[:, 0:1472],
                             [ev_s0, ev_ph, ev_ope, ev_epe]) + \
                    self.tap("dbg_maf", self.af32[:, 8200:9160],
                             [ev_s0, ev_ph, ev_ope, ev_epe])
            out_evs += [ev_sp1, ev_sp2]
        out_evs += prev_tail
        return merge(out_evs)

    def ph_chain_ag(self, ev_at):
        dr = self.dr
        evs = []
        for h in range(H):
            self.bank_tr(6, [(self.psB[6][0:128, 0:128],
                              self.Phi[:, h * 128:(h + 1) * 128],
                              self.c_ident_bf[:])], ev_at)
            evs.append(self.bank_r(6, "vector", partial(
                f_copy, out=self.pgath[:, h * 128:(h + 1) * 128],
                in_=self.psB[6][0:128, 0:128])))
        evs.append(self.ve(partial(f_copy, out=self.sgath[:, 0:1024],
                                   in_=self.S0[:, :]), ev_at))
        e2 = self.dma("gpsimd", dr["phi_ag_in"][:, :], self.pgath[:, :],
                      evs, sem="d_ag")
        e3 = self.dma("gpsimd", dr["s0_ag_in"][:, :], self.sgath[:, 0:1024],
                      evs, sem="d_ag")
        a1 = self.po(partial(f_cc, kind="AllGather", op=ALU.bypass,
                             ins=[dr["phi_ag_in"].opt()],
                             outs=[dr["phi_ag_out"].opt()]), [e2, e3],
                     inc="s_ccC")
        a2 = self.po(partial(f_cc, kind="AllGather", op=ALU.bypass,
                             ins=[dr["s0_ag_in"].opt()],
                             outs=[dr["s0_ag_out"].opt()]), [], inc="s_ccC")
        return [a1, a2]

    def ph_combine(self, ev_ch):
        dr = self.dr
        prev = list(ev_ch)
        last_h = [None] * H
        for j in range(7):
            evp = self.dma("sync", self.pgath[:, :],
                           dr["phi_ag_out"][j * 128:(j + 1) * 128, :], prev,
                           sem="d_cb")
            evsg = self.dma("sync", self.sgath[:, :],
                            dr["s0_ag_out"][j * 128:(j + 1) * 128, :], prev,
                            sem="d_cb")
            prev = []
            for h in range(H):
                hp = h % 2
                ab = self.mb[:, hp * 1472:hp * 1472 + 128]
                e1 = self.ve(partial(
                    f_ts, out=ab, in0=self.pgath[:, h * 128:(h + 1) * 128],
                    s1=self.mrep[:, j:j + 1].opt(), s2=None, op0=ALU.mult,
                    op1=ALU.bypass),
                    [evp] + ([last_h[h]] if last_h[h] else []))
                e2 = self.ve(partial(
                    f_stt, out=ab, in0=self.c_ident_f32[:],
                    s=self.mrep1[:, j:j + 1].opt(), in1=ab, op0=ALU.mult,
                    op1=ALU.add), [e1])
                bk = 3 + hp
                self.bank_w(bk, [(self.psB[bk][0:128, 0:128], ab,
                                  self.Sin[:, h * 128:(h + 1) * 128],
                                  True, True)],
                            [e2] + ([last_h[h]] if last_h[h] else []))
                ms = self.af32[:, 8200 + 128 * hp:8328 + 128 * hp]
                e3 = self.ve(partial(
                    f_ts, out=ms, in0=self.sgath[:, h * 128:(h + 1) * 128],
                    s1=self.mrep[:, j:j + 1].opt(), s2=None, op0=ALU.mult,
                    op1=ALU.bypass), [evsg])
                e4 = self.bank_r(bk, "vector", partial(
                    f_tt, out=self.Sin[:, h * 128:(h + 1) * 128],
                    in0=self.psB[bk][0:128, 0:128], in1=ms, op=ALU.add),
                    [e3])
                last_h[h] = e4
                prev.append(e4)
        dm = merge(prev)
        dm += self.tap("dbg_sin", self.Sin[:, :], dm)
        return dm

    def ph_qxn(self, ev_mx, ev_at, ev_gt):
        dr = self.dr
        done = []
        st1 = self.af32[:, 4104:6152]
        first_w = merge(ev_at, ev_gt)
        for tc in range(NTC):
            evd = self.dma("sync", self.xtok[:, :],
                           dr["x_slice"][tc * 128:(tc + 1) * 128, :],
                           first_w if tc == 0 else done[-NFC:],
                           sem="d_qx")
            e1 = self.ve(partial(f_ts, out=st1, in0=self.xtok[:, :],
                                 s1=self.ln_m[:, tc:tc + 1].opt(),
                                 s2=self.ln_r[:, tc:tc + 1].opt(),
                                 op0=ALU.add, op1=ALU.mult),
                         [evd] + (first_w if tc == 0 else []))
            e2 = self.ve(partial(f_ts, out=self.wqi[:, 0:2048], in0=st1,
                                 s1=self.qcol[:, 0:1].opt(), s2=None,
                                 op0=ALU.mult, op1=ALU.bypass),
                         [e1] + ev_mx)

            e3 = self.ve(partial(f_copy, out=self.l2tmp[:, 0:2048],
                                 in_=self.wqi[:, 0:2048]), [e2])
            for fc in range(NFC):
                self.bank_tr(6, [(self.psB[6][0:128, 0:128],
                                  self.l2tmp[:, fc * 128:(fc + 1) * 128],
                                  self.c_ident_bf[:])], [e3])
                done.append(self.bank_r(6, "vector", partial(
                    f_copy,
                    out=self.arena[:, fc * 2048 + tc * 128:
                                   fc * 2048 + (tc + 1) * 128],
                    in_=self.psB[6][0:128, 0:128])))
        dm = merge(done)
        dm += self.tap("dbg_qxn", self.arena[:, 0:2048], dm)
        return dm

    def ph_ffn1(self, ev_qx, ev_wq, ev_mx):
        dr = self.dr
        e0 = self.ve(partial(f_ts, out=self.sc1sb[:, :],
                             in0=self.sw1sb[:, :],
                             s1=self.qcol[:, 1:2].opt(), s2=None,
                             op0=ALU.mult, op1=ALU.bypass),
                     [ev_wq["sw1"]] + ev_mx)
        done = []
        spills = []
        hm_ev = None
        wev = [None, None]
        for ic in range(NIC):
            par = ic % 2
            wsl = self.wb2[:, par * 2048:(par + 1) * 2048]
            wev[par] = self.dma("sync",
                                wsl.rearrange("p (fc i) -> p fc i", i=128),
                                dr["qw1_ag_out"][ic].rearrange(
                                    "p (fc i) -> p fc i", i=128),
                                [ev_wq["qw1"]] + (ev_qx if ic < 2 else
                                                  [done[-4]]), sem="d_w1")
            tile_evs = []
            for tt4 in range(4):
                bk = tt4
                mms = [(self.psB[bk][0:128, 0:512],
                        wsl[:, fc * 128:(fc + 1) * 128],
                        self.arena[:, fc * 2048 + tt4 * 512:
                                   fc * 2048 + (tt4 + 1) * 512],
                        fc == 0, fc == NFC - 1) for fc in range(NFC)]
                self.bank_w(bk, mms,
                            [wev[par]] + (ev_qx if ic < 2 else []))
                st = self.af32[:, 4104 + par * 2048 + tt4 * 512:
                               4104 + par * 2048 + (tt4 + 1) * 512]
                e1 = self.bank_r(bk, "scalar", partial(
                    f_act, out=st, in_=self.psB[bk][0:128, 0:512],
                    func=AF.Gelu, bias=self.b1sb[:, ic:ic + 1],
                    scale=self.sc1sb[:, ic:ic + 1]),
                    [e0] + ([spills[-2]] if len(spills) >= 2 else []))
                tile_evs.append(e1)
                done.append(e1)
            e2 = self.ve(partial(f_red, out=self.sc[:, 10:11],
                                 in_=self.af32[:, 4104 + par * 2048:
                                               4104 + par * 2048 + 2048],
                                 op=ALU.max, absval=True), tile_evs,
                         inc="d_vx")
            if hm_ev is None:
                hm_ev = self.ve(partial(f_copy, out=self.amx[:, 1:2],
                                        in_=self.sc[:, 10:11]), [e2])
            else:
                hm_ev = self.ve(partial(f_tt, out=self.amx[:, 1:2],
                                        in0=self.amx[:, 1:2],
                                        in1=self.sc[:, 10:11], op=ALU.max),
                                [e2, hm_ev])
            spills.append(self.dma("scalar", dr["h_spill"][ic],
                                   self.af32[:, 4104 + par * 2048:
                                             4104 + par * 2048 + 2048],
                                   tile_evs + [e2]))
        self.bank_tr(7, [(self.psB[7][0:1, 0:128], self.amx[:, 1:2],
                          self.c_ident_f32[:])], [hm_ev])
        e2 = self.bank_r(7, "vector", partial(
            f_red, out=self.st2r[0:1, 0:1], in_=self.psB[7][0:1, 0:128],
            op=ALU.max, absval=False), sem="d_vx")
        e3 = self.dma("gpsimd", dr["mh_in"][0:1, 0:1], self.st2r[0:1, 0:1],
                      [e2])
        e4 = self.po(partial(f_cc, kind="AllReduce", op=ALU.max,
                             ins=[dr["mh_in"].opt()],
                             outs=[dr["mh_out"].opt()]), [e3], inc="s_ccS")
        e5 = self.dma("gpsimd", self.st2r[0:1, 1:2], dr["mh_out"][0:1, 0:1],
                      [e4])
        self.bank_w(7, [(self.psB[7][0:128, 0:1], self.c_ones_row_f32[:],
                         self.st2r[0:1, 1:2], True, True)], [e5])
        e9 = self.bank_r(7, "vector", partial(
            f_ts, out=self.qcol[:, 2:3], in0=self.psB[7][0:128, 0:1],
            s1=1.0 / 127.0, s2=1e-8, op0=ALU.mult, op1=ALU.max))
        e10a = self.ac(partial(f_act, out=self.qcol[:, 5:6],
                               in_=self.qcol[:, 2:3], func=AF.Ln), [e9])
        e10 = self.ac(partial(f_act, out=self.qcol[:, 3:4],
                              in_=self.qcol[:, 5:6], func=AF.Exp,
                              scale=-1.0), [e10a])
        e11 = self.ve(partial(f_ts, out=self.sc2sb[:, :],
                              in0=self.sw2sb[:, :],
                              s1=self.qcol[:, 2:3].opt(), s2=None,
                              op0=ALU.mult, op1=ALU.bypass),
                      [e10, ev_wq["sw2"]])
        self.tap("dbg_qw", dr["qw1_ag_out"][0], [ev_wq["qw1"]])
        self.tap("dbg_swin", dr["sw1_ag_in"][:, :], [ev_wq["sw1"]])
        self.tap("dbg_swout", dr["sw1_ag_out"][:, :], [ev_wq["sw1"]])
        self.tap("dbg_qwin", dr["qw1_ag_in"][0], [ev_wq["qw1"]])
        self.tap("dbg_sw", self.sw1sb[:, :], [e0])
        self.tap("dbg_sc1", self.sc1sb[:, :], [e0])
        self.tap("dbg_b1", self.b1sb[:, :], [e0])
        if "dbg_h" in self.debug:
            for i in range(4):
                self.tap_evs.append(self.dma("gpsimd", self.dr["dbg_h"][i],
                                             dr["h_spill"][i], [spills[i]]))
        return {"h": merge(spills), "inv_s2": e10, "sc2": e11}

    def ph_ffn2(self, ev_f1, ev_wq):
        dr = self.dr
        g_evs = {}
        all_done = []
        blk_done = []
        for blk in range(4):
            t0 = blk * 512
            fills = []
            f_e1s = []
            for ic in range(NIC):
                par = ic % 2
                st = self.af32[:, 4104 + par * 2048:4104 + par * 2048 + 512]
                evd = self.dma("sync", st,
                               dr["h_spill"][ic][:, t0:t0 + 512],
                               merge(ev_f1["h"], blk_done) if ic < 2
                               else [f_e1s[-1]], sem="d_fh")
                e1 = self.ve(partial(
                    f_ts, out=self.wqi[:, par * 512:par * 512 + 512],
                    in0=st, s1=self.qcol[:, 3:4].opt(), s2=None,
                    op0=ALU.mult, op1=ALU.bypass),
                    [evd, ev_f1["inv_s2"]])
                f_e1s.append(e1)
                e2 = self.ve(partial(
                    f_copy, out=self.arena[:, ic * 512:(ic + 1) * 512],
                    in_=self.wqi[:, par * 512:par * 512 + 512]), [e1])
                fills.append(e2)
            blk_done = []
            wprev = [None, None]
            for oc2 in range(NFC):
                bk = oc2 % 2
                for q in range(4):
                    wpar = q % 2
                    wsl = self.wb2[:, wpar * 2048:(wpar + 1) * 2048]
                    evw = self.dma(
                        "sync", wsl.rearrange("p (i o) -> p i o", o=128),
                        dr["qw2_ag_out"][oc2].rearrange(
                            "p (i o) -> p i o", o=128)[
                            :, q * 16:(q + 1) * 16, :],
                        [ev_wq["qw2"]] + [w for w in wprev if w],
                        sem="d_w2")
                    mms = [(self.psB[bk][0:128, 0:512],
                            wsl[:, i16 * 128:(i16 + 1) * 128],
                            self.arena[:, (q * 16 + i16) * 512:
                                       (q * 16 + i16 + 1) * 512],
                            (q == 0 and i16 == 0), (q == 3 and i16 == 15))
                           for i16 in range(16)]
                    if q == 0:
                        ev_mm = self.bank_w(bk, mms, [evw] + fills)
                    else:
                        ev_mm = self.bank_w_more(bk, mms, [evw])
                    wprev[wpar] = ev_mm
                gst = self.af32[:, 2048 + (oc2 % 2) * 512:
                                2048 + (oc2 % 2) * 512 + 512]
                e1 = self.bank_r(bk, "vector", partial(
                    f_ts, out=gst, in0=self.psB[bk][0:128, 0:512],
                    s1=self.sc2sb[:, oc2:oc2 + 1].opt(),
                    s2=self.b2sb[:, oc2:oc2 + 1].opt(),
                    op0=ALU.mult, op1=ALU.add),
                    [ev_f1["sc2"]] +
                    ([g_evs[(oc2 - 2, blk)]] if (oc2 - 2, blk) in g_evs
                     else []))
                e2 = self.ac(partial(f_act, out=gst, in_=gst, func=AF.Silu),
                             [e1])
                ge = self.dma("scalar", dr["g_spill"][oc2, blk], gst, [e2])
                g_evs[(oc2, blk)] = ge
                blk_done.append(ge)
                all_done.append(ge)
        if "dbg_g" in self.debug:
            for i in range(2):
                for bq in range(4):
                    self.tap_evs.append(self.dma(
                        "gpsimd", self.dr["dbg_g"][i, bq],
                        dr["g_spill"][i, bq], [g_evs[(i, bq)]]))
        return merge(all_done), g_evs

    def ph_output(self, ev_f2, ev_si, g_evs):
        dr = self.dr
        evwo = self.dma("gpsimd", self.arena[:, 0:16384].rearrange(
            "p (kc o) -> p kc o", o=2048),
            dr["Wo"].rearrange("(kc p) o -> p kc o", p=128), ev_f2)
        out_evs = []
        prev_o = []
        prev_g = []
        prev_x = []
        for tc in range(NTC):
            w0 = ev_f2 if tc == 0 else prev_o
            ga = [self.dma("sync", self.af32[:, 6152:7176],
                           dr["gate_spill"][tc], w0, sem="d_oe")]
            ga.append(self.dma("sync", self.opre_ld[0:64, :],
                               dr["opre_spill"][tc * 2], w0, sem="d_oe"))
            ga.append(self.dma("sync", self.opre_ld[64:128, :],
                               dr["opre_spill"][tc * 2 + 1], w0, sem="d_oe"))
            ga.append(self.dma("sync", self.ephi_ld[:, :],
                               dr["ephi_spill"][tc * 2], w0, sem="d_oe"))
            ga.append(self.dma("sync", self.xtok[:, :],
                               dr["x_slice"][tc * 128:(tc + 1) * 128, :],
                               merge(w0, prev_x), sem="d_oe"))
            grpA = merge(ga)
            grp = grpA
            evx = grpA
            o_evs = []
            last_e8 = {}
            for half in range(2):
                p0, p1 = half * 64, half * 64 + 64
                if half == 1:
                    evb = self.dma("sync", self.ephi_ld[:, :],
                                   dr["ephi_spill"][tc * 2 + 1], o_evs,
                                   sem="d_oe")
                    grp = merge(grpA, [evb])
                for h in range(H):
                    hp = h % 2
                    bk = hp
                    scb = 8 + hp * 4
                    self.bank_w(bk, [(self.psB[bk][p0:p1, 0:128],
                                      self.ephi_ld[:, h * 64:(h + 1) * 64],
                                      self.Sin[:, h * 128:(h + 1) * 128],
                                      True, True)], grp + ev_si)
                    of = self.af32[p0:p1, 8200 + hp * 128:8328 + hp * 128]
                    e1 = self.bank_r(bk, "vector", partial(
                        f_tt, out=of, in0=self.psB[bk][p0:p1, 0:128],
                        in1=self.opre_ld[p0:p1, h * 128:(h + 1) * 128],
                        op=ALU.add), grp)
                    e2 = self.ac(partial(
                        f_act,
                        out=self.af32[p0:p1, 8456 + hp * 128:8584 + hp * 128],
                        in_=of, func=AF.Square,
                        accum=self.sc[p0:p1, scb:scb + 1]), [e1])
                    e3 = self.ac(partial(
                        f_act, out=self.sc[p0:p1, scb + 1:scb + 2],
                        in_=self.sc[p0:p1, scb:scb + 1],
                        func=AF.Ln, bias=self.epsc[p0:p1, 0:1],
                        scale=1.0 / 128.0), [e2])
                    e4 = self.ac(partial(
                        f_act, out=self.sc[p0:p1, scb + 2:scb + 3],
                        in_=self.sc[p0:p1, scb + 1:scb + 2],
                        func=AF.Exp, scale=-0.5), [e3])
                    e5 = self.ve(partial(
                        f_ts, out=of, in0=of,
                        s1=self.sc[p0:p1, scb + 2:scb + 3].opt(), s2=None,
                        op0=ALU.mult, op1=ALU.bypass), [e4])
                    e6 = self.ve(partial(
                        f_tt, out=of, in0=of,
                        in1=self.c_norm_w_rep[p0:p1, :], op=ALU.mult), [e5])
                    sg = self.af32[p0:p1, 8712 + hp * 128:8840 + hp * 128]
                    e7 = self.ac(partial(
                        f_act, out=sg,
                        in_=self.af32[p0:p1,
                                      6152 + h * 128:6152 + (h + 1) * 128],
                        func=AF.Silu),
                        grp + ([last_e8[hp]] if hp in last_e8 else []))
                    e8 = self.ve(partial(
                        f_tt, out=self.otok[p0:p1, h * 128:(h + 1) * 128],
                        in0=of, in1=sg, op=ALU.mult), [e6, e7])
                    last_e8[hp] = e8
                    o_evs.append(e8)
            tr_evs = []
            for h in range(H):
                self.bank_tr(6, [(self.psB[6][0:128, 0:128],
                                  self.otok[:, h * 128:(h + 1) * 128],
                                  self.c_ident_bf[:])], o_evs)
                tr_evs.append(self.bank_r(6, "vector", partial(
                    f_copy, out=self.ofm[:, h * 128:(h + 1) * 128],
                    in_=self.psB[6][0:128, 0:128])))
            prev_o = []
            for ot in range(4):
                bk = 2 + ot % 2
                mms = [(self.psB[bk][0:128, 0:512],
                        self.ofm[:, kc * 128:(kc + 1) * 128],
                        self.arena[:, kc * 2048 + ot * 512:
                                   kc * 2048 + (ot + 1) * 512],
                        kc == 0, kc == H - 1) for kc in range(H)]
                self.bank_w(bk, mms, tr_evs + [evwo])
                gq = [self.dma(
                    "sync", self.af32[:, 3072 + i4 * 128:
                                      3072 + (i4 + 1) * 128],
                    dr["g_spill"][ot * 4 + i4, tc // 4][
                        :, (tc % 4) * 128:(tc % 4 + 1) * 128],
                    [g_evs[(ot * 4 + i4, tc // 4)]] + prev_g, sem="d_gq")
                    for i4 in range(4)]
                gqa = merge(gq)
                gp_evs = []
                for i4 in range(4):
                    gpst = self.af32[:, 3072 + i4 * 128:3072 + (i4 + 1) * 128]
                    self.bank_tr(7, [(self.psB[7][0:128, 0:128], gpst,
                                      self.c_ident_f32[:])], gqa)
                    gp_evs.append(self.bank_r(7, "vector", partial(
                        f_copy, out=self.gtok[:, i4 * 128:(i4 + 1) * 128],
                        in_=self.psB[7][0:128, 0:128])))
                prev_g = gp_evs
                e1 = self.bank_r(bk, "vector", partial(
                    f_tt, out=self.fin[:, 0:512],
                    in0=self.psB[bk][0:128, 0:512],
                    in1=self.gtok[:, 0:512], op=ALU.mult),
                    gp_evs + (out_evs[-2:] if len(out_evs) >= 2 else []))
                e2 = self.ve(partial(
                    f_tt, out=self.fin[:, 512:1024],
                    in0=self.fin[:, 0:512],
                    in1=self.xtok[:, ot * 512:(ot + 1) * 512],
                    op=ALU.add), [e1] + evx)
                oev = self.dma("sync",
                               dr["out"][tc * 128:(tc + 1) * 128,
                                         ot * 512:(ot + 1) * 512],
                               self.fin[:, 512:1024], [e2])
                out_evs.append(oev)
                prev_o.append(oev)
            prev_x = prev_o[-1:]
        if "dbg_opre" in self.debug:
            for cch in range(NCH):
                self.tap_evs.append(self.dma(
                    "gpsimd", self.dr["dbg_opre"][cch],
                    dr["opre_spill"][cch], ev_f2))
        if "dbg_ephi" in self.debug:
            for cch in range(NCH):
                self.tap_evs.append(self.dma(
                    "gpsimd", self.dr["dbg_ephi"][cch],
                    dr["ephi_spill"][cch], ev_f2))
        if "dbg_gate_all" in self.debug:
            for tcq in range(NTC):
                self.tap_evs.append(self.dma(
                    "gpsimd", self.dr["dbg_gate_all"][tcq],
                    dr["gate_spill"][tcq], ev_f2))
        if "dbg_gall" in self.debug:
            for oc2 in range(NFC):
                for bq in range(4):
                    self.tap_evs.append(self.dma(
                        "gpsimd", self.dr["dbg_gall"][oc2, bq],
                        dr["g_spill"][oc2, bq], ev_f2))
        return merge(out_evs)


def make_consts():
    import ml_dtypes
    bf = ml_dtypes.bfloat16
    t = np.arange(C)
    t_of = np.concatenate([t, t])
    j_of = np.concatenate([np.zeros(C, np.int64), np.ones(C, np.int64)])
    orig = 2 * t_of + j_of
    amask = (orig[:, None] > orig[None, :]).astype(np.float32)
    mmaskT = (t_of[:, None] <= t[None, :]).astype(np.float32)
    i64p = np.concatenate([np.eye(64), np.eye(64)], axis=0).astype(bf)
    return {
        "amask": np.ascontiguousarray(amask),
        "mmaskT": np.ascontiguousarray(mmaskT),
        "ident_bf": np.ascontiguousarray(np.eye(128, dtype=bf)),
        "ident_f32": np.ascontiguousarray(np.eye(128, dtype=np.float32)),
        "i64p_bf": np.ascontiguousarray(i64p),
        "ones_f32": np.ascontiguousarray(np.ones((128, 128), np.float32)),
        "ones_col_bf": np.ascontiguousarray(np.ones((128, 1), dtype=bf)),
        "ones_row_bf": np.ascontiguousarray(np.ones((1, 128), dtype=bf)),
        "ones_row_f32": np.ascontiguousarray(np.ones((1, 128), np.float32)),
    }


def host_prepare(inputs):
    x = np.asarray(inputs["L0_attn_x"], np.float32)
    consts = make_consts()
    w1 = np.asarray(inputs["L0_ffn_w1"], np.float32)
    w2 = np.asarray(inputs["L0_ffn_w2"], np.float32)
    cw = np.concatenate([
        np.asarray(inputs["L0_attn_conv_q_w"], np.float32).reshape(KD, CONV),
        np.asarray(inputs["L0_attn_conv_k_w"],
                   np.float32).reshape(2 * KD, CONV),
        np.asarray(inputs["L0_attn_conv_v_w"],
                   np.float32).reshape(2 * KD, CONV),
    ], axis=0)
    cw_oc = np.zeros((128, 40 * CONV), np.float32)
    for oc in range(40):
        cw_oc[:, oc * CONV:(oc + 1) * CONV] = cw[oc * 128:(oc + 1) * 128]
    per_core = []
    for c in range(8):
        b, p = c // 4, c % 4
        t0 = p * TPC
        m = {
            "x_slice": np.ascontiguousarray(x[b, t0:t0 + TPC]),
            "xhalo": (np.zeros((3, D), np.float32) if p == 0 else
                      np.ascontiguousarray(x[b, t0 - 3:t0])),
            "Wq": np.ascontiguousarray(inputs["L0_attn_Wq"], np.float32),
            "Wk": np.ascontiguousarray(inputs["L0_attn_Wk"], np.float32),
            "Wv": np.ascontiguousarray(inputs["L0_attn_Wv"], np.float32),
            "Wb": np.ascontiguousarray(inputs["L0_attn_Wb"], np.float32),
            "Wa": np.ascontiguousarray(inputs["L0_attn_Wa"], np.float32),
            "Wg": np.ascontiguousarray(inputs["L0_attn_Wg"], np.float32),
            "Wo": np.ascontiguousarray(inputs["L0_attn_Wo"], np.float32),
            "A_log_c": np.ascontiguousarray(
                np.asarray(inputs["L0_attn_A_log"],
                           np.float32).reshape(8, 1)),
            "dt_bias_c": np.ascontiguousarray(
                np.asarray(inputs["L0_attn_dt_bias"],
                           np.float32).reshape(8, 1)),
            "convw_all": cw_oc,
            "norm_w_rep": np.ascontiguousarray(np.broadcast_to(
                np.asarray(inputs["L0_attn_norm_w"], np.float32)[None, :],
                (128, 128))),
            "w1_slice": np.ascontiguousarray(
                w1[c * 1024:(c + 1) * 1024].reshape(8, 128, D)),
            "b1_col": np.ascontiguousarray(
                np.asarray(inputs["L0_ffn_b1"],
                           np.float32).reshape(NIC, 128)),
            "w2_slice": np.ascontiguousarray(
                w2[c * 256:(c + 1) * 256].reshape(2, 128, I_FFN)),
            "b2_col": np.ascontiguousarray(
                np.asarray(inputs["L0_ffn_b2"],
                           np.float32).reshape(NFC, 128)),
            "chain_m_rep": np.ascontiguousarray(np.broadcast_to(
                np.array([1.0 if (j // 4 == b and j % 4 < p) else 0.0
                          for j in range(7)], np.float32)[None, :],
                (128, 7))),
        }
        m.update(consts)
        per_core.append(m)
    return per_core


DEBUG = ()


def kernel(**inputs):
    per_core = host_prepare(inputs)
    nc = Builder(debug=DEBUG).build()
    res = run_bass_kernel_spmd(nc, per_core, core_ids=list(range(8)))
    out = np.empty((B, L, D), np.float32)
    for c in range(8):
        b, p = c // 4, c % 4
        out[b, p * TPC:(p + 1) * TPC] = res.results[c]["out"]
    if DEBUG:
        kernel.dbg = [{k: np.asarray(v) for k, v in r.items()}
                      for r in res.results]
    return out


# version 08
# speedup vs baseline: 1.0358x; 1.0358x over previous
import contextlib
from functools import partial

import numpy as np

import concourse.bass as bass
import concourse.mybir as mybir
from concourse.bass_utils import run_bass_kernel_spmd

F32 = mybir.dt.float32
BF16 = mybir.dt.bfloat16
I16 = mybir.dt.int16
AF = mybir.ActivationFunctionType
ALU = mybir.AluOpType
AX = mybir.AxisListType

B, L, D = 2, 8192, 2048
H, DH, NH, CONV = 8, 128, 2, 4
KD = H * DH
C = 64
R = 2 * C
TPC = 2048
NCH = TPC // C
NTC = TPC // 128
NFC = D // 128
I_FFN = 8192
NIC = I_FFN // 128
ISQD = 1.0 / float(np.sqrt(np.float32(DH)))

ENGINES = ("tensor", "vector", "scalar", "gpsimd", "sync")


class Ev:
    __slots__ = ("sem", "val", "eng")

    def __init__(self, sem, val, eng):
        self.sem, self.val, self.eng = sem, val, eng


def merge(*evs):
    best = {}
    for e in evs:
        if e is None:
            continue
        if isinstance(e, (list, tuple)):
            for x in e:
                if x is not None and (x.sem not in best or
                                      best[x.sem].val < x.val):
                    best[x.sem] = x
        elif e.sem not in best or best[e.sem].val < e.val:
            best[e.sem] = e
    return list(best.values())


class Prog:
    def __init__(self):
        self.stream = {e: [] for e in ENGINES}
        self.count = {}
        self.sem_names = set()

    COMPUTE_SEMS = ("s_te",)

    def on(self, eng, fn, waits=(), inc=None, amt=None):
        ws = merge(*waits)
        ws = [Ev(w.sem, self.count[w.sem], w.eng)
              if (w.sem.startswith("d") and self.count.get(w.sem, 0) > w.val)
              else w for w in ws]
        ev = None
        if inc is not None:
            if amt is None:
                amt = 16 if inc.startswith("d") else 1
            self.sem_names.add(inc)
            self.count[inc] = self.count.get(inc, 0) + amt
            ev = Ev(inc, self.count[inc], eng)
        for w in ws:
            self.sem_names.add(w.sem)
        self.stream[eng].append((ws, fn, inc, amt))
        return ev

    def replay(self, block, sems):
        for eng in ENGINES:
            ops = self.stream[eng]
            if not ops:
                continue

            def make_body(eng, ops):
                def body(engobj):
                    last = {}
                    for ws, fn, inc, amt in ops:
                        for w in ws:
                            if w.eng == eng and w.sem in Prog.COMPUTE_SEMS:
                                continue
                            if last.get(w.sem, -1) >= w.val:
                                continue
                            engobj.wait_ge(sems[w.sem], w.val)
                            last[w.sem] = w.val
                        instr = fn(engobj)
                        if inc is not None:
                            instr.then_inc(sems[inc], amt)
                return body
            getattr(block, eng)(make_body(eng, ops))


def f_dma(e, out, in_, slow=False):
    if slow:
        return e.dma_start(out=out, in_=in_, allow_slow_non_contiguous=True)
    return e.dma_start(out=out, in_=in_)


def f_mm(e, out, lhsT, rhs, start, stop):
    return e.matmul(out, lhsT, rhs, start=start, stop=stop)


def f_tr(e, out, in_, ident):
    return e.transpose(out, in_, ident)


def f_copy(e, out, in_):
    if hasattr(e, "tensor_copy"):
        return e.tensor_copy(out=out, in_=in_)
    return e.copy(out=out, in_=in_)


def f_ts(e, out, in0, s1, s2, op0, op1):
    return e.tensor_scalar(out=out, in0=in0, scalar1=s1, scalar2=s2,
                           op0=op0, op1=op1)


def f_tt(e, out, in0, in1, op):
    return e.tensor_tensor(out=out, in0=in0, in1=in1, op=op)


def f_stt(e, out, in0, s, in1, op0, op1):
    return e.scalar_tensor_tensor(out=out, in0=in0, scalar=s, in1=in1,
                                  op0=op0, op1=op1)


def f_act(e, out, in_, func, bias=0.0, scale=1.0, accum=None):
    if accum is None:
        return e.activation(out=out, in_=in_, func=func, bias=bias,
                            scale=scale)
    return e.activation(out=out, in_=in_, func=func, bias=bias, scale=scale,
                        accum_out=accum)


def f_red(e, out, in_, op, absval):
    return e.tensor_reduce(out=out, in_=in_, axis=AX.X, op=op,
                           apply_absolute_value=absval)


def f_scan(e, out, d0, d1, init, op0, op1):
    return e.tensor_tensor_scan(out=out, data0=d0, data1=d1, initial=init,
                                op0=op0, op1=op1)


def f_recip(e, out, in_):
    return e.reciprocal(out=out, in_=in_)


def f_memset(e, ap, v):
    return e.memset(ap, v)


def f_cc(e, kind, op, ins, outs):
    return e.collective_compute(kind, op, replica_groups=[list(range(8))],
                                ins=ins, outs=outs)


def f_nop(e):
    return e.nop()


DEBUG_TAPS = {
    "dbg_xnfm": ((128, 2048), BF16),
    "dbg_qt": ((128, NCH * 64), BF16),
    "dbg_kt": ((128, NCH * 128), BF16),
    "dbg_vt": ((128, NCH * 128), BF16),
    "dbg_G": ((8, TPC), F32),
    "dbg_Br": ((128, 8 * NCH), F32),
    "dbg_opre": ((NCH, 64, H * 128), F32),
    "dbg_ephi": ((NCH, 128, H * 64), BF16),
    "dbg_S0": ((128, H * 128), BF16),
    "dbg_Phi": ((128, H * 128), BF16),
    "dbg_sin": ((128, H * 128), BF16),
    "dbg_h": ((4, 128, TPC), F32),
    "dbg_g": ((2, 4, 128, 512), F32),
    "dbg_gall": ((NFC, 4, 128, 512), F32),
    "dbg_gate_all": ((NTC, 128, KD), F32),
    "dbg_gate": ((128, KD), F32),
    "dbg_qxn": ((128, 2048), BF16),
    "dbg_sx": ((128, 8), F32),
    "dbg_amx": ((128, 2), F32),
    "dbg_st1": ((1, 16), F32),
    "dbg_mach": ((128, 1472), BF16),
    "dbg_maf": ((128, 960), F32),
    "dbg_Gr": ((128, 8 * NCH), F32),
    "dbg_bagg": ((16, TPC), F32),
    "dbg_xnf2": ((128, 2048), BF16),
    "dbg_mxl": ((1, 4), F32),
    "dbg_P": ((128, 128), F32),
    "dbg_naf": ((128, 128), F32),
    "dbg_qlat": ((128, 16), F32),
    "dbg_qw": ((128, 2048), BF16),
    "dbg_sw": ((128, NIC), F32),
    "dbg_sc1": ((128, NIC), F32),
    "dbg_b1": ((128, NIC), F32),
    "dbg_hxx": ((128, 2048), F32),
    "dbg_swin": ((1, 1024), F32),
    "dbg_swout": ((8, 1024), F32),
    "dbg_qwin": ((128, 2048), BF16),
    "dbg_wst": ((128, 1024), F32),
    "dbg_wsc": ((128, 8), F32),
}


class Builder:
    def __init__(self, debug=()):
        self.nc = bass.Bass("TRN2", target_bir_lowering=False)
        self.P = Prog()
        self.debug = set(debug)
        self.tap_evs = []
        self.dr = {}
        self.stk = contextlib.ExitStack()
        self.sb_bytes = 0
        self.bankst = {i: {"w": None, "r": []} for i in range(8)}
        self._dram()
        self._sbuf()
        self._psum()

    def _dram(self):
        nc = self.nc

        def din(name, shape, dtype=F32):
            self.dr[name] = nc.dram_tensor(name, list(shape), dtype,
                                           kind="ExternalInput").ap()

        def dscr(name, shape, dtype=F32, space="Local"):
            self.dr[name] = nc.dram_tensor(name, list(shape), dtype,
                                           addr_space=space).ap()

        din("x_slice", (TPC, D)); din("xhalo", (3, D))
        din("Wq", (D, KD)); din("Wk", (D, 2 * KD)); din("Wv", (D, 2 * KD))
        din("Wb", (D, 16)); din("Wa", (D, 8)); din("Wg", (D, KD))
        din("Wo", (KD, D))
        din("A_log_c", (8, 1)); din("dt_bias_c", (8, 1))
        din("convw_all", (128, 40 * CONV))
        din("norm_w_rep", (128, 128))
        din("w1_slice", (8, 128, D)); din("b1_col", (NIC, 128))
        din("w2_slice", (2, 128, I_FFN)); din("b2_col", (NFC, 128))
        din("chain_m_rep", (128, 7))
        din("amask", (R, R)); din("mmaskT", (R, C))
        din("ident_bf", (128, 128), BF16)
        din("ident_f32", (128, 128), F32)
        din("i64p_bf", (128, 64), BF16)
        din("ones_f32", (128, 128), F32)
        din("ones_col_bf", (128, 1), BF16)
        din("ones_row_bf", (1, 128), BF16)
        din("ones_row_f32", (1, 128), F32)
        self.dr["out"] = nc.dram_tensor("out", [TPC, D], F32,
                                        kind="ExternalOutput").ap()
        for name in self.debug:
            shape, dtype = DEBUG_TAPS[name]
            self.dr[name] = nc.dram_tensor(name, list(shape), dtype,
                                           kind="ExternalOutput").ap()

        dscr("h_spill", (NIC, 128, TPC), F32)
        dscr("opre_spill", (NCH, 64, H * 128), F32)
        dscr("ephi_spill", (NCH, 128, H * 64), BF16)
        dscr("g_spill", (NFC, 4, 128, 512), F32)
        dscr("gate_spill", (NTC, 128, KD), F32)
        dscr("qw1_ag_in", (8, 128, NFC * 128), BF16)
        dscr("qw1_ag_out", (NIC, 128, NFC * 128), BF16, "Shared")
        dscr("sw1_ag_in", (1, 8 * 128), F32)
        dscr("sw1_ag_out", (8, 8 * 128), F32, "Shared")
        dscr("qw2_ag_in", (2, 128, NIC * 128), BF16)
        dscr("qw2_ag_out", (NFC, 128, NIC * 128), BF16, "Shared")
        dscr("sw2_ag_in", (1, 2 * 128), F32)
        dscr("sw2_ag_out", (8, 2 * 128), F32, "Shared")
        dscr("s0_ag_in", (128, H * 128), F32)
        dscr("s0_ag_out", (8 * 128, H * 128), F32, "Shared")
        dscr("phi_ag_in", (128, H * 128), BF16)
        dscr("phi_ag_out", (8 * 128, H * 128), BF16, "Shared")
        dscr("mx_in", (1, 8), F32); dscr("mx_out", (1, 8), F32)
        dscr("mh_in", (1, 8), F32); dscr("mh_out", (1, 8), F32)

    def _sbuf(self):
        def s(name, shape, dtype):
            t = self.stk.enter_context(
                self.nc.sbuf_tensor(name, list(shape), dtype))
            self.sb_bytes += int(np.prod(shape[1:])) * \
                {F32: 4, BF16: 2, I16: 2}[dtype]
            setattr(self, name, t)
            return t

        s("arena", (128, 32768), BF16)
        s("xnh", (128, 64), BF16)
        s("qt", (128, NCH * 64), BF16)
        s("kt", (128, NCH * 128), BF16)
        s("vt", (128, NCH * 128), BF16)
        s("af32", (128, 10240), F32)
        s("wqi", (128, 2048), I16)
        s("wb2", (128, 4096), BF16)
        s("l2tmp", (128, 2048), BF16)
        s("xtok", (128, 2048), F32)
        s("mb", (128, 2944), BF16)
        s("Grows", (128, 8 * NCH), F32)
        s("nGrows", (128, 8 * NCH), F32)
        s("Brows", (128, 8 * NCH), F32)
        s("eGrows", (128, 8 * NCH), F32)
        s("S0", (128, H * 128), BF16)
        s("Phi", (128, H * 128), BF16)
        s("Sin", (128, H * 128), BF16)
        s("bagg", (16, TPC), F32)
        s("gfm", (8, TPC), F32)
        s("Gfm", (8, TPC), F32)
        s("ln_m", (128, 17), F32)
        s("ln_r", (128, 17), F32)
        s("amx", (128, 2), F32)
        s("sc", (128, 26), F32)
        s("bsc", (8, 2), F32)
        s("mxl", (1, 4), F32)
        s("qlat", (128, 16), F32)
        s("st1r", (1, 512), F32)
        s("st2r", (1, 512), F32)
        s("stb", (1, 512), BF16)
        s("qcol", (128, 8), F32)
        s("b1sb", (128, NIC), F32)
        s("b2sb", (128, NFC), F32)
        s("sw1sb", (128, NIC), F32)
        s("sw2sb", (128, NFC), F32)
        s("sc1sb", (128, NIC), F32)
        s("sc2sb", (128, NFC), F32)
        s("opre_st", (64, 2 * 128), F32)
        s("ephi_st", (128, 2 * 64), BF16)
        s("ephi_ld", (128, H * 64), BF16)
        s("otok", (128, KD), BF16)
        s("ofm", (128, KD), BF16)
        s("mrep", (128, 7), F32)
        s("mrep1", (128, 7), F32)
        s("pgath", (128, 1024), BF16)
        s("convw", (128, 40 * CONV), F32)
        s("epsc", (128, 1), F32)
        for nm, shape, dt in [("amask", (R, R), F32), ("mmaskT", (R, C), F32),
                              ("ident_bf", (128, 128), BF16),
                              ("ident_f32", (128, 128), F32),
                              ("i64p_bf", (128, 64), BF16),
                              ("ones_f32", (128, 128), F32),
                              ("ones_col_bf", (128, 1), BF16),
                              ("ones_row_bf", (1, 128), BF16),
                              ("ones_row_f32", (1, 128), F32),
                              ("norm_w_rep", (128, 128), F32),
                              ("A_log_c", (8, 1), F32),
                              ("dt_bias_c", (8, 1), F32)]:
            s("c_" + nm, shape, dt)
        assert self.sb_bytes <= 213000, f"sbuf over: {self.sb_bytes}"
        self.sgath = self.af32[:, 0:1024]
        self.gtok = self.af32[:, 1024:1536]
        self.opre_ld = self.af32[:, 4104:5128]
        self.fin = self.af32[:, 5128:6152]

    def _psum(self):
        self.psB = []
        for i in range(6):
            self.psB.append(self.stk.enter_context(
                self.nc.psum_tensor(f"psf{i}", [128, 512], F32)))
        self.psB.append(self.stk.enter_context(
            self.nc.psum_tensor("psb6", [128, 1024], BF16)))
        self.psB.append(self.stk.enter_context(
            self.nc.psum_tensor("psf7", [128, 512], F32)))

    def dma(self, q, out, in_, waits=(), sem=None, slow=False):
        sem = sem or {"sync": "d_sp", "gpsimd": "d_pool",
                      "scalar": "d_act"}[q]
        return self.P.on(q, partial(f_dma, out=out, in_=in_, slow=slow),
                         waits, inc=sem)

    def ve(self, fn, waits=(), inc="s_dve"):
        return self.P.on("vector", fn, waits, inc=inc, amt=1)

    def ac(self, fn, waits=(), inc="s_act"):
        return self.P.on("scalar", fn, waits, inc=inc)

    def po(self, fn, waits=(), inc="s_pool"):
        return self.P.on("gpsimd", fn, waits, inc=inc)

    def bank_w(self, bk, mms, waits=()):
        st = self.bankst[bk]
        w = list(waits) + st["r"] + ([st["w"]] if st["w"] else [])
        st["r"] = []
        ev = None
        for i, (out, lhsT, rhs, a, b) in enumerate(mms):
            ev = self.P.on("tensor",
                           partial(f_mm, out=out, lhsT=lhsT, rhs=rhs,
                                   start=a, stop=b),
                           w if i == 0 else (),
                           inc="s_te" if i == len(mms) - 1 else None)
        st["w"] = ev
        return ev

    def bank_w_more(self, bk, mms, waits=()):
        st = self.bankst[bk]
        ev = None
        for i, (out, lhsT, rhs, a, b) in enumerate(mms):
            ev = self.P.on("tensor",
                           partial(f_mm, out=out, lhsT=lhsT, rhs=rhs,
                                   start=a, stop=b),
                           list(waits) if i == 0 else (),
                           inc="s_te" if i == len(mms) - 1 else None)
        st["w"] = ev
        return ev

    def bank_tr(self, bk, trs, waits=()):
        st = self.bankst[bk]
        w = list(waits) + st["r"] + ([st["w"]] if st["w"] else [])
        st["r"] = []
        ev = None
        for i, (out, in_, ident) in enumerate(trs):
            ev = self.P.on("tensor",
                           partial(f_tr, out=out, in_=in_, ident=ident),
                           w if i == 0 else (),
                           inc="s_te" if i == len(trs) - 1 else None)
        st["w"] = ev
        return ev

    def bank_r(self, bk, eng, fn, waits=(), sem=None):
        st = self.bankst[bk]
        w = list(waits) + ([st["w"]] if st["w"] else [])
        sem = sem or {"vector": "s_dve", "scalar": "s_act"}[eng]
        ev = self.P.on(eng, fn, w, inc=sem, amt=1)
        st["r"].append(ev)
        return ev

    def tap(self, name, src, waits):
        if name in self.debug:
            self.tap_evs.append(self.dma("gpsimd", self.dr[name], src,
                                         waits))
            return [self.tap_evs[-1]]
        return []

    def xn_rhs(self, fc, t0, n):
        return self.arena[:, fc * 2048 + t0:fc * 2048 + t0 + n]

    def build(self):
        ev_c = self.ph_consts()
        ev_ln = self.ph_ln(ev_c)
        ev_mx = self.ph_sx_ar(ev_ln)
        ev_wq, ev_stage = self.ph_wquant(ev_c, ev_ln)
        ev_bg = self.ph_beta_g(ev_c, ev_ln)
        ev_at = self.ph_heads(ev_c, ev_bg, ev_stage)
        ev_ch = self.ph_chain_ag(ev_at)
        ev_qx = self.ph_qxn(ev_mx, ev_at, [])
        ev_f1 = self.ph_ffn1(ev_qx, ev_wq, ev_mx)
        ev_si = self.ph_combine(ev_ch)
        ev_f2, g_evs = self.ph_ffn2(ev_f1, ev_wq)
        ev_ou = self.ph_output(ev_f2, ev_si, g_evs)
        self.P.on("sync", f_nop, [merge(ev_ou, self.tap_evs)])
        sems = {}
        with self.stk:
            with contextlib.ExitStack() as ss:
                for name in sorted(self.P.sem_names):
                    sems[name] = ss.enter_context(self.nc.semaphore(name))
                with self.nc.Block() as block:
                    self.P.replay(block, sems)
        return self.nc

    def ph_consts(self):
        evs = []
        for nm in ["amask", "mmaskT", "ident_bf", "ident_f32", "i64p_bf",
                   "ones_f32", "ones_col_bf", "ones_row_bf", "ones_row_f32",
                   "norm_w_rep", "A_log_c", "dt_bias_c"]:
            evs.append(self.dma("sync", getattr(self, "c_" + nm)[:],
                                self.dr[nm][:, :]))
        evs.append(self.dma("sync", self.convw[:], self.dr["convw_all"][:, :]))
        evs.append(self.dma("sync", self.mrep[:],
                            self.dr["chain_m_rep"][:, :]))
        evs.append(self.dma("sync", self.b1sb[:],
                            self.dr["b1_col"].rearrange("i p -> p i"),
                            slow=True))
        evs.append(self.dma("sync", self.b2sb[:],
                            self.dr["b2_col"].rearrange("i p -> p i"),
                            slow=True))
        e2 = self.ve(partial(f_ts, out=self.mrep1[:], in0=self.mrep[:],
                             s1=-1.0, s2=1.0, op0=ALU.mult, op1=ALU.add), evs)
        e_eps = self.ve(partial(f_ts, out=self.epsc[:],
                                in0=self.c_ones_f32[:, 0:1], s1=1e-5,
                                s2=None, op0=ALU.mult, op1=ALU.bypass),
                        list(evs))
        evs.append(e_eps)
        e3 = self.ve(partial(f_memset, ap=self.S0[:], v=0.0))
        e4 = self.ve(partial(f_memset, ap=self.Sin[:], v=0.0))
        phi_evs = [self.ve(partial(f_copy,
                                   out=self.Phi[:, hh * 128:(hh + 1) * 128],
                                   in_=self.c_ident_bf[:]), evs)
                   for hh in range(H)]
        return merge(evs, e2, e3, e4, phi_evs)

    def ph_ln(self, ev_c):
        dr = self.dr
        done = []
        ev_mx = None
        st1 = self.af32[:, 4104:6152]
        for tt in range(NTC + 1):
            halo = tt == NTC
            rows = 3 if halo else 128
            src = dr["xhalo"][:, :] if halo else \
                dr["x_slice"][tt * 128:(tt + 1) * 128, :]
            evd = self.dma("sync", self.xtok[0:rows, :], src,
                           done[-NFC:] if done else ev_c, sem="d_lx")
            xs = self.xtok[0:rows, :]
            col = self.ln_m[0:rows, tt:tt + 1]
            rcol = self.ln_r[0:rows, tt:tt + 1]
            e1 = self.ac(partial(f_act, out=st1[0:rows, :], in_=xs,
                                 func=AF.Identity,
                                 accum=self.sc[0:rows, 0:1]), [evd])
            ef = self.ac(partial(f_copy, out=self.sc[0:rows, 3:4],
                                 in_=self.sc[0:rows, 0:1]), [e1])
            e2 = self.ve(partial(f_ts, out=col, in0=self.sc[0:rows, 0:1],
                                 s1=-1.0 / D, s2=None, op0=ALU.mult,
                                 op1=ALU.bypass), [ef])
            e3 = self.ac(partial(f_act, out=st1[0:rows, :], in_=xs,
                                 func=AF.Square, bias=col, scale=1.0,
                                 accum=self.sc[0:rows, 1:2]), [e2])
            e5 = self.ac(partial(f_act, out=self.sc[0:rows, 3:4],
                                 in_=self.sc[0:rows, 1:2], func=AF.Ln,
                                 bias=self.epsc[0:rows, 0:1],
                                 scale=1.0 / D), [e3])
            e6 = self.ac(partial(f_act, out=rcol,
                                 in_=self.sc[0:rows, 3:4], func=AF.Exp,
                                 scale=-0.5), [e5])
            e7 = self.ve(partial(f_ts, out=st1[0:rows, :], in0=xs,
                                 s1=col.opt(), s2=rcol.opt(), op0=ALU.add,
                                 op1=ALU.mult), [e6])
            if not halo:
                if tt == 0:
                    ev_mx = self.ve(partial(f_red, out=self.amx[:, 0:1],
                                            in_=st1, op=ALU.max,
                                            absval=True), [e7], inc="d_vx")
                else:
                    em = self.ve(partial(f_red, out=self.sc[:, 4:5], in_=st1,
                                         op=ALU.max, absval=True), [e7],
                                 inc="d_vx")
                    ev_mx = self.ve(partial(f_tt, out=self.amx[:, 0:1],
                                            in0=self.amx[:, 0:1],
                                            in1=self.sc[:, 4:5], op=ALU.max),
                                    [em, ev_mx])
            e8 = self.ve(partial(f_copy, out=self.l2tmp[0:rows, :],
                                 in_=st1[0:rows, :]), [e7])
            for fc in range(NFC):
                if halo:
                    self.bank_tr(6, [(self.psB[6][0:128, 0:3],
                                      self.l2tmp[0:3,
                                                 fc * 128:(fc + 1) * 128],
                                      self.c_ident_bf[0:3, 0:3])], [e8])
                    eve = self.bank_r(6, "vector", partial(
                        f_copy, out=self.xnh[:, fc * 4:fc * 4 + 3],
                        in_=self.psB[6][0:128, 0:3]))
                else:
                    self.bank_tr(6, [(self.psB[6][0:128, 0:128],
                                      self.l2tmp[:, fc * 128:(fc + 1) * 128],
                                      self.c_ident_bf[:])], [e8])
                    eve = self.bank_r(6, "vector", partial(
                        f_copy,
                        out=self.arena[:, fc * 2048 + tt * 128:
                                       fc * 2048 + (tt + 1) * 128],
                        in_=self.psB[6][0:128, 0:128]))
                done.append(eve)
        done.append(ev_mx)
        dm = merge(done)
        dm += self.tap("dbg_xnfm", self.arena[:, 0:2048], dm)
        dm += self.tap("dbg_xnf2", self.arena[:, 15 * 2048:16 * 2048], dm)
        return dm

    def ph_sx_ar(self, ev_ln):
        dr = self.dr
        self.bank_tr(7, [(self.psB[7][0:1, 0:128], self.amx[:, 0:1],
                          self.c_ident_f32[:])], ev_ln)
        e2 = self.bank_r(7, "vector", partial(
            f_red, out=self.st1r[0:1, 0:1], in_=self.psB[7][0:1, 0:128],
            op=ALU.max, absval=False))
        el1 = self.ve(partial(f_copy, out=self.mxl[0:1, 0:1],
                              in_=self.st1r[0:1, 0:1]), [e2])
        e3 = self.dma("gpsimd", dr["mx_in"][0:1, 0:1], self.st1r[0:1, 0:1],
                      [e2], sem="d_mx")
        e4 = self.po(partial(f_cc, kind="AllReduce", op=ALU.max,
                             ins=[dr["mx_in"].opt()],
                             outs=[dr["mx_out"].opt()]), [e3], inc="s_ccM")
        e5 = self.dma("gpsimd", self.st1r[0:1, 1:2], dr["mx_out"][0:1, 0:1],
                      [e4], sem="d_mx")
        el2 = self.ve(partial(f_copy, out=self.mxl[0:1, 1:2],
                              in_=self.st1r[0:1, 1:2]), [e5])
        self.bank_w(7, [(self.psB[7][0:128, 0:1], self.c_ones_row_f32[:],
                         self.st1r[0:1, 1:2], True, True)], [e5])
        e9 = self.bank_r(7, "vector", partial(
            f_ts, out=self.qcol[:, 1:2], in0=self.psB[7][0:128, 0:1],
            s1=1.0 / 127.0, s2=1e-8, op0=ALU.mult, op1=ALU.max))
        e10a = self.ac(partial(f_act, out=self.qcol[:, 4:5],
                               in_=self.qcol[:, 1:2], func=AF.Ln), [e9])
        e10 = self.ac(partial(f_act, out=self.qcol[:, 0:1],
                              in_=self.qcol[:, 4:5], func=AF.Exp,
                              scale=-1.0), [e10a])
        out = [e10]
        out += self.tap("dbg_sx", self.qcol[:, :], [e10])
        out += self.tap("dbg_amx", self.amx[:, :], [e10])
        out += self.tap("dbg_st1", self.st1r[0:1, 0:16], [e10])
        out += self.tap("dbg_mxl", self.mxl[0:1, :], [e10])
        out += self.tap("dbg_qlat", self.qlat[:, :], [e10])
        return out

    def ph_wquant(self, ev_c, ev_ln):
        dr = self.dr
        self.wq_slot = 0
        self.wq_dma = [None] * 4
        self.wq_lasteve = None
        self.wq_scst = None
        stA = self.af32[:, 0:2048]
        stB = self.af32[:, 2051:4099]
        done = []
        stage_last = None
        prev = list(ev_ln)

        def quant_2048(st, rec, waits):
            e4 = self.ve(partial(f_ts, out=self.wqi[:, 0:2048], in0=st,
                                 s1=rec.opt(), s2=None, op0=ALU.mult,
                                 op1=ALU.bypass), waits)
            e5 = self.ve(partial(f_copy, out=self.l2tmp[:, 0:2048],
                                 in_=self.wqi[:, 0:2048]),
                         [e4] + ([self.wq_lasteve] if self.wq_lasteve
                                 else []))
            return e4, e5

        for icl in range(8):
            st = stA if icl % 2 == 0 else stB
            evd = self.dma("sync", st, dr["w1_slice"][icl],
                           prev + (list(ev_ln) if icl == 0 else []),
                           sem="d_wv")
            if icl == 0 and "dbg_wst" in self.debug:
                evl = self.ve(partial(f_copy, out=self.af32[:, 5128:6152],
                                      in_=st[:, 0:1024]), [evd])
                self.tap("dbg_wst", self.af32[:, 5128:6152], [evl])
            e1 = self.ve(partial(f_red, out=self.sc[:, 5:6], in_=st,
                                 op=ALU.max, absval=True), [evd],
                         inc="d_vx")
            e2 = self.ve(partial(f_ts, out=self.sc[:, 6:7],
                                 in0=self.sc[:, 5:6], s1=1.0 / 127.0,
                                 s2=1e-8, op0=ALU.mult, op1=ALU.max),
                         [e1] + ([self.wq_scst] if self.wq_scst else []),
                         inc="d_vx")
            e3a = self.ac(partial(f_act, out=self.sc[:, 7:8],
                                  in_=self.sc[:, 6:7], func=AF.Ln), [e2])
            e3 = self.ac(partial(f_act, out=self.sc[:, 5:6],
                                 in_=self.sc[:, 7:8], func=AF.Exp,
                                 scale=-1.0), [e3a])
            evst = self.dma(
                "gpsimd",
                dr["sw1_ag_in"][0:1, icl * 128:(icl + 1) * 128]
                .rearrange("o p -> p o"), self.sc[:, 6:7], [e2], slow=True,
                sem="d_ws")
            done.append(evst)
            self.wq_scst = evst
            if icl == 0 and "dbg_wsc" in self.debug:
                evl2 = self.ve(partial(f_copy, out=self.af32[:, 6152:6160],
                                       in_=self.sc[:, 5:13]), [e2, evst])
                self.tap("dbg_wsc", self.af32[:, 6152:6160], [evl2])
            e4, e5 = quant_2048(st, self.sc[:, 5:6], [e3])
            stage_last = e4
            prev = [e4]
            for fc in range(NFC):
                slot = self.wq_slot % 4
                self.wq_slot += 1
                so = self.ofm[:, slot * 128:(slot + 1) * 128]
                self.bank_tr(6, [(self.psB[6][0:128, 0:128],
                                  self.l2tmp[:, fc * 128:(fc + 1) * 128],
                                  self.c_ident_bf[:])], [e5] + ev_c)
                eve = self.bank_r(6, "scalar", partial(
                    f_copy, out=so, in_=self.psB[6][0:128, 0:128]),
                    [self.wq_dma[slot]] if self.wq_dma[slot] else [])
                self.wq_dma[slot] = self.dma(
                    "gpsimd",
                    dr["qw1_ag_in"][icl, :, fc * 128:(fc + 1) * 128],
                    so, [eve], sem="d_t%d" % slot)
                done.append(self.wq_dma[slot])
                self.wq_lasteve = eve
        for ocl in range(2):
            mx = []
            for q in range(4):
                st = stA if q % 2 == 0 else stB
                evd = self.dma("sync", st,
                               dr["w2_slice"][ocl][:, q * 2048:(q + 1) * 2048],
                               prev, sem="d_wv")
                e1 = self.ve(partial(f_red, out=self.sc[:, 8 + q:9 + q],
                                     in_=st, op=ALU.max, absval=True), [evd],
                             inc="d_vx")
                prev = [e1]
                mx.append(e1)
            em1 = self.ve(partial(f_tt, out=self.sc[:, 12:13],
                                  in0=self.sc[:, 8:9], in1=self.sc[:, 9:10],
                                  op=ALU.max), mx)
            em2 = self.ve(partial(f_tt, out=self.sc[:, 13:14],
                                  in0=self.sc[:, 10:11],
                                  in1=self.sc[:, 11:12], op=ALU.max), [em1])
            em3 = self.ve(partial(f_tt, out=self.sc[:, 12:13],
                                  in0=self.sc[:, 12:13],
                                  in1=self.sc[:, 13:14], op=ALU.max), [em2])
            e2 = self.ve(partial(f_ts, out=self.sc[:, 14:15],
                                 in0=self.sc[:, 12:13], s1=1.0 / 127.0,
                                 s2=1e-8, op0=ALU.mult, op1=ALU.max),
                         [em3] + ([self.wq_scst] if self.wq_scst else []),
                         inc="d_vx")
            e3a = self.ac(partial(f_act, out=self.sc[:, 15:16],
                                  in_=self.sc[:, 14:15], func=AF.Ln), [e2])
            e3 = self.ac(partial(f_act, out=self.sc[:, 13:14],
                                 in_=self.sc[:, 15:16], func=AF.Exp,
                                 scale=-1.0), [e3a])
            evst = self.dma(
                "gpsimd",
                dr["sw2_ag_in"][0:1, ocl * 128:(ocl + 1) * 128]
                .rearrange("o p -> p o"), self.sc[:, 14:15], [e2], slow=True,
                sem="d_ws")
            done.append(evst)
            self.wq_scst = evst
            for q in range(4):
                st = stA if q % 2 == 0 else stB
                evd = self.dma("sync", st,
                               dr["w2_slice"][ocl][:, q * 2048:(q + 1) * 2048],
                               prev, sem="d_wv")
                e4, e5 = quant_2048(st, self.sc[:, 13:14], [evd, e3])
                stage_last = e4
                prev = [e4]
                for t16 in range(16):
                    ic = q * 16 + t16
                    slot = self.wq_slot % 4
                    self.wq_slot += 1
                    so = self.ofm[:, slot * 128:(slot + 1) * 128]
                    self.bank_tr(6, [(self.psB[6][0:128, 0:128],
                                      self.l2tmp[:, t16 * 128:(t16 + 1) * 128],
                                      self.c_ident_bf[:])], [e5])
                    eve = self.bank_r(6, "scalar", partial(
                        f_copy, out=so, in_=self.psB[6][0:128, 0:128]),
                        [self.wq_dma[slot]] if self.wq_dma[slot] else [])
                    self.wq_dma[slot] = self.dma(
                        "gpsimd",
                        dr["qw2_ag_in"][ocl, :, ic * 128:(ic + 1) * 128],
                        so, [eve], sem="d_t%d" % slot)
                    done.append(self.wq_dma[slot])
                    self.wq_lasteve = eve

        a1 = self.po(partial(f_cc, kind="AllGather", op=ALU.bypass,
                             ins=[dr["qw1_ag_in"].opt()],
                             outs=[dr["qw1_ag_out"].opt()]), done, inc="s_ccW")
        a2 = self.po(partial(f_cc, kind="AllGather", op=ALU.bypass,
                             ins=[dr["sw1_ag_in"].opt()],
                             outs=[dr["sw1_ag_out"].opt()]), done, inc="s_ccW")
        a3 = self.po(partial(f_cc, kind="AllGather", op=ALU.bypass,
                             ins=[dr["qw2_ag_in"].opt()],
                             outs=[dr["qw2_ag_out"].opt()]), done, inc="s_ccW")
        a4 = self.po(partial(f_cc, kind="AllGather", op=ALU.bypass,
                             ins=[dr["sw2_ag_in"].opt()],
                             outs=[dr["sw2_ag_out"].opt()]), done, inc="s_ccW")
        e1 = self.dma("sync", self.sw1sb[:, :],
                      dr["sw1_ag_out"].rearrange("r (i p) -> p (r i)", p=128),
                      [a4], slow=True, sem="d_sw")
        e2 = self.dma("sync", self.sw2sb[:, :],
                      dr["sw2_ag_out"].rearrange("r (i p) -> p (r i)", p=128),
                      [a4], slow=True, sem="d_sw")
        return {"qw1": a4, "sw1": e1, "qw2": a4, "sw2": e2}, [stage_last]

    def ph_beta_g(self, ev_c, ev_ln):
        dr = self.dr
        evb = self.dma("gpsimd",
                       self.wb2[:, 0:NFC * 16].rearrange(
                           "p (fc o) -> p fc o", o=16),
                       dr["Wb"].rearrange("(fc p) o -> p fc o", p=128),
                       ev_ln)
        eva = self.dma("gpsimd",
                       self.wb2[:, NFC * 16:NFC * 24].rearrange(
                           "p (fc o) -> p fc o", o=8),
                       dr["Wa"].rearrange("(fc p) o -> p fc o", p=128),
                       ev_ln)
        ea = self.ac(partial(f_act, out=self.bsc[0:8, 0:1],
                             in_=self.c_A_log_c[0:8, 0:1], func=AF.Exp),
                     ev_c)
        enA = self.ve(partial(f_ts, out=self.bsc[0:8, 1:2],
                              in0=self.bsc[0:8, 0:1], s1=-1.0, s2=None,
                              op0=ALU.mult, op1=ALU.bypass), [ea])
        g_evs = []
        for tt4 in range(4):
            t0 = tt4 * 512
            mmsb = [(self.psB[0][0:16, 0:512],
                     self.wb2[:, fc * 16:(fc + 1) * 16],
                     self.xn_rhs(fc, t0, 512), fc == 0, fc == NFC - 1)
                    for fc in range(NFC)]
            self.bank_w(0, mmsb, [evb] + ev_ln)
            e1 = self.bank_r(0, "scalar", partial(
                f_act, out=self.bagg[0:16, t0:t0 + 512],
                in_=self.psB[0][0:16, 0:512], func=AF.Sigmoid))
            eb = self.ve(partial(f_ts, out=self.bagg[0:16, t0:t0 + 512],
                                 in0=self.bagg[0:16, t0:t0 + 512], s1=2.0,
                                 s2=None, op0=ALU.mult, op1=ALU.bypass),
                         [e1])
            mmsa = [(self.psB[1][0:8, 0:512],
                     self.wb2[:, NFC * 16 + fc * 8:NFC * 16 + (fc + 1) * 8],
                     self.xn_rhs(fc, t0, 512), fc == 0, fc == NFC - 1)
                    for fc in range(NFC)]
            self.bank_w(1, mmsa, [eva] + ev_ln)
            e2 = self.bank_r(1, "scalar", partial(
                f_act, out=self.gfm[0:8, t0:t0 + 512],
                in_=self.psB[1][0:8, 0:512], func=AF.Exp,
                bias=self.c_dt_bias_c[0:8, 0:1], scale=1.0), ev_c)
            e3 = self.ac(partial(f_act, out=self.Gfm[0:8, t0:t0 + 512],
                                 in_=self.gfm[0:8, t0:t0 + 512], func=AF.Ln,
                                 bias=1.0, scale=1.0), [e2])
            e4 = self.ve(partial(f_ts, out=self.gfm[0:8, t0:t0 + 512],
                                 in0=self.Gfm[0:8, t0:t0 + 512],
                                 s1=self.bsc[0:8, 1:2].opt(), s2=None,
                                 op0=ALU.mult, op1=ALU.bypass), [e3, enA])
            g_evs.append((eb, e4))
        out_evs = []
        for cch in range(NCH):
            sl = slice(cch * C, (cch + 1) * C)
            eb, e4 = g_evs[cch // 8]
            e1 = self.ve(partial(f_scan, out=self.Gfm[0:8, sl],
                                 d0=self.gfm[0:8, sl], d1=self.gfm[0:8, sl],
                                 init=0.0, op0=ALU.add, op1=ALU.bypass),
                         [e4], inc="d_vx")
            self.bank_tr(7, [(self.psB[7][0:64, 0:8], self.Gfm[0:8, sl],
                              self.c_ident_f32[0:8, 0:8])], [e1])
            e2 = self.bank_r(7, "vector", partial(
                f_copy, out=self.Grows[0:64, cch * 8:(cch + 1) * 8],
                in_=self.psB[7][0:64, 0:8]))
            e2b = self.dma("sync", self.Grows[64:128, cch * 8:(cch + 1) * 8],
                           self.Grows[0:64, cch * 8:(cch + 1) * 8], [e2])
            e3 = self.ve(partial(f_ts,
                                 out=self.nGrows[:, cch * 8:(cch + 1) * 8],
                                 in0=self.Grows[:, cch * 8:(cch + 1) * 8],
                                 s1=-1.0, s2=None, op0=ALU.mult,
                                 op1=ALU.bypass), [e2b])
            e3b = self.ac(partial(f_act,
                                  out=self.eGrows[:, cch * 8:(cch + 1) * 8],
                                  in_=self.Grows[:, cch * 8:(cch + 1) * 8],
                                  func=AF.Exp), [e2b])
            self.bank_tr(7, [(self.psB[7][0:64, 16:32], self.bagg[0:16, sl],
                              self.c_ident_f32[0:16, 0:16])], [eb])
            e5 = self.bank_r(7, "vector", partial(
                f_copy, out=self.Brows[0:64, cch * 8:(cch + 1) * 8],
                in_=self.psB[7][0:64, 16:24]))
            e5c = self.ve(partial(
                f_copy, out=self.sc[0:64, 10:18][:, 0:8]
                if False else self.Brows[0:64, cch * 8:(cch + 1) * 8],
                in_=self.psB[7][0:64, 16:24]), []) if False else None
            e5b0 = self.bank_r(7, "vector", partial(
                f_copy, out=self.l2tmp[0:64, 0:8].bitcast(F32)
                if False else self.af32[0:64, 9160:9168],
                in_=self.psB[7][0:64, 24:32]))
            e5b = self.dma("sync", self.Brows[64:128, cch * 8:(cch + 1) * 8],
                           self.af32[0:64, 9160:9168], [e5b0])
            out_evs += [e3, e3b, e5, e5b]
        dm = merge(out_evs)
        dm += self.tap("dbg_bagg", self.bagg[0:16, :], dm)
        dm += self.tap("dbg_G", self.Gfm[:, :], dm)
        dm += self.tap("dbg_Br", self.Brows[:, :], dm)
        dm += self.tap("dbg_Gr", self.Grows[:, :], dm)
        return dm

    def ph_heads(self, ev_c, ev_bg, ev_stage):
        dr = self.dr
        done = []
        prev_mach = list(ev_bg) + list(ev_stage)
        for h in range(H):
            par = h % 2
            wsl = self.wb2[:, par * 2048:(par + 1) * 2048]
            specs = [("Wq", h * 128), ("Wk", h * 128), ("Wk", KD + h * 128),
                     ("Wv", h * 128), ("Wv", KD + h * 128)]
            oc_global = [h, 8 + h, 16 + h, 24 + h, 32 + h]
            pc_ev = {}
            wprev = prev_mach
            for wi, (wn, c0) in enumerate(specs):
                evw = self.dma("gpsimd", wsl.rearrange(
                    "p (fc o) -> p fc o", o=128),
                    dr[wn].rearrange("(fc p) o -> p fc o", p=128)[
                        :, :, c0:c0 + 128], wprev)
                pre = self.af32[:, 0:2051] if wi % 2 == 0 else \
                    self.af32[:, 2051:4102]
                pre_evs = []
                for tt4 in range(4):
                    bk = tt4 % 2
                    mms = [(self.psB[bk][0:128, 0:512],
                            wsl[:, fc * 128:(fc + 1) * 128],
                            self.xn_rhs(fc, tt4 * 512, 512),
                            fc == 0, fc == NFC - 1) for fc in range(NFC)]
                    self.bank_w(bk, mms,
                                [evw] + (prev_mach if wi < 2 else []))
                    eng = "scalar" if tt4 % 2 == 0 else "vector"
                    pre_evs.append(self.bank_r(bk, eng, partial(
                        f_copy,
                        out=pre[:, 3 + tt4 * 512:3 + (tt4 + 1) * 512],
                        in_=self.psB[bk][0:128, 0:512])))
                mmsh = [(self.psB[2][0:128, 0:3],
                         wsl[:, fc * 128:(fc + 1) * 128],
                         self.xnh[:, fc * 4:fc * 4 + 3],
                         fc == 0, fc == NFC - 1) for fc in range(NFC)]
                self.bank_w(2, mmsh, [evw])
                pre_evs.append(self.bank_r(2, "vector", partial(
                    f_copy, out=pre[:, 0:3], in_=self.psB[2][0:128, 0:3])))
                wprev = pre_evs
                ocg = oc_global[wi]
                cw = self.convw
                cvA = self.af32[:, 4104:6152]
                cvB = self.af32[:, 6152:8200]
                e1 = self.ve(partial(
                    f_ts, out=cvA[:, 0:2048], in0=pre[:, 0:2048],
                    s1=cw[:, ocg * CONV:ocg * CONV + 1].opt(), s2=None,
                    op0=ALU.mult, op1=ALU.bypass), pre_evs)
                e2 = self.ve(partial(
                    f_stt, out=cvB[:, 0:2048], in0=pre[:, 1:2049],
                    s=cw[:, ocg * CONV + 1:ocg * CONV + 2].opt(),
                    in1=cvA[:, 0:2048], op0=ALU.mult, op1=ALU.add), [e1])
                e3 = self.ve(partial(
                    f_stt, out=cvA[:, 0:2048], in0=pre[:, 2:2050],
                    s=cw[:, ocg * CONV + 2:ocg * CONV + 3].opt(),
                    in1=cvB[:, 0:2048], op0=ALU.mult, op1=ALU.add), [e2])
                e4 = self.ve(partial(
                    f_stt, out=cvB[:, 0:2048], in0=pre[:, 3:2051],
                    s=cw[:, ocg * CONV + 3:ocg * CONV + 4].opt(),
                    in1=cvA[:, 0:2048], op0=ALU.mult, op1=ALU.add), [e3])
                e5 = self.ac(partial(f_act, out=cvA[:, 0:2048],
                                     in_=cvB[:, 0:2048], func=AF.Silu),
                             [e4])
                if wi >= 3:
                    j = wi - 3
                    dst = self.vt.ap().rearrange(
                        "p (cc two t) -> p cc two t", two=2, t=64)[:, :, j, :]
                    pc_ev[wi] = [self.ve(partial(
                        f_copy, out=dst,
                        in_=cvA[:, 0:2048].rearrange(
                            "p (cc t) -> p cc t", t=64)), [e5])]
                else:
                    evs_l2 = []
                    for tt4 in range(4):
                        t0 = tt4 * 512
                        sq = self.ve(partial(
                            f_tt, out=self.l2tmp[:, t0:t0 + 512],
                            in0=cvA[:, t0:t0 + 512],
                            in1=cvA[:, t0:t0 + 512], op=ALU.mult), [e5])
                        self.bank_w(5, [(self.psB[5][0:1, 0:512],
                                         self.c_ones_col_bf[:],
                                         self.l2tmp[:, t0:t0 + 512],
                                         True, True)], [sq])
                        n1 = self.bank_r(5, "vector", partial(
                            f_ts, out=self.st1r[0:1, 0:512],
                            in_=None, in0=self.psB[5][0:1, 0:512],
                            s1=1e-30, s2=None, op0=ALU.max,
                            op1=ALU.bypass) if False else partial(
                            f_ts, out=self.st1r[0:1, 0:512],
                            in0=self.psB[5][0:1, 0:512], s1=1e-30, s2=None,
                            op0=ALU.max, op1=ALU.bypass))
                        n2 = self.ac(partial(f_act,
                                             out=self.st1r[0:1, 0:512],
                                             in_=self.st1r[0:1, 0:512],
                                             func=AF.Ln), [n1])
                        n2b = self.ac(partial(f_act,
                                              out=self.st2r[0:1, 0:512],
                                              in_=self.st1r[0:1, 0:512],
                                              func=AF.Exp, scale=-0.5),
                                      [n2])
                        n3 = self.ve(partial(f_ts,
                                             out=self.st2r[0:1, 0:512],
                                             in0=self.st2r[0:1, 0:512],
                                             s1=1e12, s2=None, op0=ALU.min,
                                             op1=ALU.bypass), [n2b])
                        if wi == 0:
                            n3 = self.ve(partial(
                                f_ts, out=self.st2r[0:1, 0:512],
                                in0=self.st2r[0:1, 0:512], s1=ISQD, s2=None,
                                op0=ALU.mult, op1=ALU.bypass), [n3])
                        n4 = self.ve(partial(f_copy, out=self.stb[0:1, 0:512],
                                             in_=self.st2r[0:1, 0:512]),
                                     [n3])
                        self.bank_w(4, [(self.psB[4][0:128, 0:512],
                                         self.c_ones_row_bf[:],
                                         self.stb[0:1, 0:512], True, True)],
                                    [n4])
                        if wi == 0:
                            dst = self.qt[:, t0:t0 + 512]
                        else:
                            j = wi - 1
                            dst = self.kt.ap().rearrange(
                                "p (cc two t) -> p cc two t", two=2, t=64)[
                                :, 8 * tt4:8 * tt4 + 8, j, :]
                        if wi == 0:
                            evs_l2.append(self.bank_r(4, "vector", partial(
                                f_tt, out=dst, in0=cvA[:, t0:t0 + 512],
                                in1=self.psB[4][0:128, 0:512],
                                op=ALU.mult)))
                        else:
                            evs_l2.append(self.bank_r(4, "vector", partial(
                                f_tt, out=dst,
                                in0=cvA[:, t0:t0 + 512].rearrange(
                                    "p (cc t) -> p cc t", t=64),
                                in1=self.psB[4][0:128, 0:512].rearrange(
                                    "p (cc t) -> p cc t", t=64),
                                op=ALU.mult)))
                    pc_ev[wi] = evs_l2
            evwg = self.dma("gpsimd", wsl.rearrange(
                "p (fc o) -> p fc o", o=128),
                dr["Wg"].rearrange("(fc p) o -> p fc o", p=128)[
                    :, :, h * 128:(h + 1) * 128], wprev)
            gate_done = []
            for tcq in range(NTC):
                bk = tcq % 2
                mms = [(self.psB[bk][0:128, 0:128],
                        self.xn_rhs(fc, tcq * 128, 128),
                        wsl[:, fc * 128:(fc + 1) * 128],
                        fc == 0, fc == NFC - 1) for fc in range(NFC)]
                self.bank_w(bk, mms, [evwg])
                gst = self.af32[:, 4104 + bk * 128:4104 + bk * 128 + 128]
                e1 = self.bank_r(bk, "vector", partial(
                    f_copy, out=gst, in_=self.psB[bk][0:128, 0:128]),
                    [gate_done[-2]] if len(gate_done) >= 2 else [])
                gate_done.append(self.dma(
                    "scalar",
                    dr["gate_spill"][tcq, :, h * 128:(h + 1) * 128], gst,
                    [e1]))
            wprev = gate_done[-2:]
            ev_in = merge(pc_ev[0], pc_ev[1], pc_ev[2], pc_ev[3], pc_ev[4],
                          ev_bg, ev_c)
            if h == 0:
                ev_in += self.tap("dbg_qt", self.qt[:, :], ev_in)
                ev_in += self.tap("dbg_kt", self.kt[:, :], ev_in)
                ev_in += self.tap("dbg_vt", self.vt[:, :], ev_in)
            prev_mach = merge(self.machinery_head(h, ev_in), gate_done)
            done += prev_mach
        dm = merge(done)
        dm += self.tap("dbg_S0", self.S0[:, :], dm)
        dm += self.tap("dbg_Phi", self.Phi[:, :], dm)
        dm += self.tap("dbg_gate", self.dr["gate_spill"][0], dm)
        return dm

    def machinery_head(self, h, ev_in):
        dr = self.dr
        mbv = self.mb
        prev_tail = list(ev_in)
        out_evs = []
        spill_prev = [None, None]
        for cch in range(NCH):
            par = cch % 2
            MA = 8200 if par == 0 else 9224
            mo = par * 1472

            def ma(a, b, MA=MA):
                return self.af32[:, MA + a:MA + b]

            def mbo(a, b, mo=mo):
                return mbv[:, mo + a:mo + b]

            scc = 16 + par * 5
            gr = self.Grows[:, cch * 8 + h:cch * 8 + h + 1]
            ngr = self.nGrows[:, cch * 8 + h:cch * 8 + h + 1]
            br = self.Brows[:, cch * 8 + h:cch * 8 + h + 1]
            egr = self.eGrows[:, cch * 8 + h:cch * 8 + h + 1]
            ktc = self.kt[:, cch * 128:(cch + 1) * 128]
            qtc = self.qt[:, cch * 64:(cch + 1) * 64]
            S0s = self.S0[:, h * 128:(h + 1) * 128]
            Phs = self.Phi[:, h * 128:(h + 1) * 128]
            w0 = ev_in if cch < 2 else prev_tail
            yv = ma(0, 128)
            e1 = self.ve(partial(f_ts, out=yv, in0=self.c_ones_f32[:],
                                 s1=gr.opt(), s2=None, op0=ALU.mult,
                                 op1=ALU.bypass), w0)
            self.bank_tr(7, [(self.psB[7][0:128, 0:128], yv,
                              self.c_ident_f32[:])], [e1])
            pre2 = ma(0, 128)
            ev_pr2 = self.bank_r(7, "vector", partial(
                f_ts, out=pre2, in0=self.psB[7][0:128, 0:128],
                s1=gr.opt(), s2=0.0, op0=ALU.subtract, op1=ALU.max))
            e1n = ma(128, 256)
            ev_e1n = self.ac(partial(f_act, out=e1n, in_=pre2, func=AF.Exp,
                                     scale=-1.0), [ev_pr2])
            pre1 = ma(256, 320)
            ev_pr1 = self.bank_r(7, "vector", partial(
                f_ts, out=pre1, in0=self.psB[7][0:128, 0:64],
                s1=gr.opt(), s2=0.0, op0=ALU.subtract, op1=ALU.min))
            e1p = pre1
            ev_e1p = self.ac(partial(f_act, out=e1p, in_=pre1, func=AF.Exp),
                             [ev_pr1])
            lamq = ma(320, 384)
            ev_lq = self.bank_r(7, "scalar", partial(
                f_act, out=lamq, in_=self.psB[7][0:128, 0:64], func=AF.Exp))
            g63 = self.sc[:, scc:scc + 1]
            ev_g63 = self.bank_r(7, "vector", partial(
                f_copy, out=g63, in_=self.psB[7][0:128, 63:64]))
            khsc = self.sc[:, scc + 1:scc + 2]
            ev_kh = self.ac(partial(f_act, out=khsc, in_=gr, func=AF.Exp,
                                    bias=g63, scale=-1.0), [ev_g63])
            lend = self.sc[:, scc + 2:scc + 3]
            ev_le = self.ac(partial(f_act, out=lend, in_=g63, func=AF.Exp),
                            [ev_g63])
            self.bank_w(0, [(self.psB[0][0:128, 0:128], ktc, ktc,
                             True, True)], w0)
            e1nm = ma(384, 512)
            ev_m = self.ve(partial(f_tt, out=e1nm, in0=e1n,
                                   in1=self.c_amask[:], op=ALU.mult),
                           [ev_e1n])
            nbr = self.sc[:, scc + 3:scc + 4]
            ev_nb = self.ve(partial(f_ts, out=nbr, in0=br, s1=-1.0, s2=None,
                                    op0=ALU.mult, op1=ALU.bypass), w0)
            if h == 0 and cch == 0 and "dbg_P" in self.debug:
                evdbg = self.bank_r(0, "vector", partial(
                    f_copy, out=self.xtok[:, 0:128],
                    in_=self.psB[0][0:128, 0:128]))
                self.tap("dbg_P", self.xtok[:, 0:128], [evdbg])
            nab = mbo(1024, 1152)
            ev_nab = self.bank_r(0, "vector", partial(
                f_stt, out=nab, in0=self.psB[0][0:128, 0:128], s=nbr.opt(),
                in1=e1nm, op0=ALU.mult, op1=ALU.mult), [ev_m, ev_nb])
            if h == 0 and cch == 0 and "dbg_naf" in self.debug:
                ev2dbg = self.ve(partial(f_copy, out=self.xtok[:, 128:256],
                                         in_=nab), [ev_nab])
                self.tap("dbg_naf", self.xtok[:, 128:256], [ev2dbg])
            ppow = mbo(640, 768)
            ev_pp = self.ve(partial(f_copy, out=ppow, in_=nab), [ev_nab])
            self.bank_tr(6, [(self.psB[6][0:128, 0:128], ppow,
                              self.c_ident_bf[:])], [ev_pp])
            ppowT = mbo(768, 896)
            evT = self.bank_r(6, "scalar", partial(
                f_copy, out=ppowT, in_=self.psB[6][0:128, 0:128]))
            gacc = mbo(896, 1024)
            evG = self.ve(partial(f_tt, out=gacc, in0=ppowT,
                                  in1=self.c_ident_bf[:], op=ALU.add),
                          [evT])
            evTn = evT
            for j in range(6):
                self.bank_w(1, [(self.psB[1][0:128, 0:128], ppowT, ppow,
                                 True, True)], [evG, evTn])
                evPn = self.bank_r(1, "vector", partial(
                    f_copy, out=ppow, in_=self.psB[1][0:128, 0:128]))
                if j < 5:
                    self.bank_tr(6, [(self.psB[6][0:128, 0:128], ppow,
                                      self.c_ident_bf[:])], [evPn])
                    evTn = self.bank_r(6, "scalar", partial(
                        f_copy, out=ppowT, in_=self.psB[6][0:128, 0:128]))
                self.bank_w(2, [(self.psB[2][0:128, 0:128], ppow, gacc,
                                 True, True)], [evPn])
                evG = self.bank_r(2, "vector", partial(
                    f_tt, out=gacc, in0=gacc,
                    in1=self.psB[2][0:128, 0:128], op=ALU.add))
            X = mbo(1216, 1472)
            vch = self.vt[:, cch * 128:(cch + 1) * 128]
            kch = self.kt[:, cch * 128:(cch + 1) * 128]
            self.bank_tr(6, [(self.psB[6][0:128, 0:128], vch,
                              self.c_ident_bf[:])], w0 + [evG])
            evxv = self.bank_r(6, "vector", partial(
                f_ts, out=X[:, 0:128], in0=self.psB[6][0:128, 0:128],
                s1=br.opt(), s2=None, op0=ALU.mult, op1=ALU.bypass))
            bl = self.sc[:, scc + 4:scc + 5]
            ev_bl = self.ve(partial(f_tt, out=bl, in0=br, in1=egr,
                                    op=ALU.mult), w0)
            self.bank_tr(6, [(self.psB[6][0:128, 0:128], kch,
                              self.c_ident_bf[:])], [evxv])
            evxk = self.bank_r(6, "vector", partial(
                f_ts, out=X[:, 128:256], in0=self.psB[6][0:128, 0:128],
                s1=bl.opt(), s2=None, op0=ALU.mult, op1=ALU.bypass),
                [ev_bl])
            khat = mbo(256, 384)
            evkh = self.bank_r(6, "vector", partial(
                f_ts, out=khat, in0=self.psB[6][0:128, 0:128],
                s1=khsc.opt(), s2=None, op0=ALU.mult, op1=ALU.bypass),
                [ev_kh])
            self.bank_w(3, [(self.psB[3][0:128, 0:256], gacc, X,
                             True, True)], [evxv, evxk, evG])
            U = mbo(0, 256)
            evUe = self.bank_r(3, "vector", partial(
                f_copy, out=U, in_=self.psB[3][0:128, 0:256]))
            Ul, W = U[:, 0:128], U[:, 128:256]
            self.bank_w(4, [(self.psB[4][0:128, 0:64], ktc, qtc,
                             True, True)], w0 + [evG])
            e1q = ma(640, 704)
            ev_eq = self.ve(partial(f_tt, out=e1q, in0=e1p,
                                    in1=self.c_mmaskT[:, 0:64],
                                    op=ALU.mult), [ev_e1p])
            mtt = mbo(512, 576)
            ev_mt = self.bank_r(4, "vector", partial(
                f_tt, out=mtt, in0=self.psB[4][0:128, 0:64], in1=e1q,
                op=ALU.mult), [ev_eq])
            qlam = mbo(1152, 1216)
            ev_ql = self.ve(partial(f_tt, out=qlam, in0=qtc, in1=lamq,
                                    op=ALU.mult), [ev_lq])
            self.bank_w(4, [(self.psB[4][0:128, 0:64], W, mtt,
                             True, True)], [evUe, ev_mt])
            Et = mbo(576, 640)
            ev_et = self.bank_r(4, "vector", partial(
                f_tt, out=Et, in0=qlam, in1=self.psB[4][0:128, 0:64],
                op=ALU.subtract), [ev_ql])
            self.bank_w(5, [(self.psB[5][0:128, 0:128], W, khat,
                             True, True)], [evUe, evkh])
            diagl = ma(704, 832)
            ev_dg = self.ve(partial(f_ts, out=diagl,
                                    in0=self.c_ident_f32[:],
                                    s1=lend.opt(), s2=None, op0=ALU.mult,
                                    op1=ALU.bypass), [ev_le])
            Ft = mbo(384, 512)
            ev_ft = self.bank_r(5, "vector", partial(
                f_tt, out=Ft, in0=diagl, in1=self.psB[5][0:128, 0:128],
                op=ALU.subtract), [ev_dg])
            self.bank_w(0, [(self.psB[0][0:128, 0:128], khat, Ul,
                             True, True)], [evUe, evkh])
            sloc = ma(832, 960)
            ev_sl = self.bank_r(0, "vector", partial(
                f_copy, out=sloc, in_=self.psB[0][0:128, 0:128]))
            self.bank_w(1, [
                (self.psB[1][0:64, 0:128], mtt, Ul, True, False),
                (self.psB[1][0:64, 0:128], Et, S0s, False, True)],
                [evUe, ev_mt, ev_et] + prev_tail)
            sp = spill_prev[par]
            ev_ope = self.bank_r(1, "scalar", partial(
                f_copy, out=self.opre_st[0:64, par * 128:(par + 1) * 128],
                in_=self.psB[1][0:64, 0:128]), [sp] if sp else [])
            self.bank_w(2, [(self.psB[2][0:128, 0:64], Phs, Et,
                             True, True)], [ev_et] + prev_tail)
            ev_epe = self.bank_r(2, "scalar", partial(
                f_copy, out=self.ephi_st[:, par * 64:(par + 1) * 64],
                in_=self.psB[2][0:128, 0:64]), [sp] if sp else [])
            ev_sp1 = self.dma("scalar",
                              dr["opre_spill"][cch][:, h * 128:(h + 1) * 128],
                              self.opre_st[0:64, par * 128:(par + 1) * 128],
                              [ev_ope], sem="d_ms")
            ev_sp2 = self.dma("scalar",
                              dr["ephi_spill"][cch][:, h * 64:(h + 1) * 64],
                              self.ephi_st[:, par * 64:(par + 1) * 64],
                              [ev_epe], sem="d_ms")
            spill_prev[par] = merge(ev_sp1, ev_sp2)[0]
            self.bank_w(5, [
                (self.psB[5][0:128, 0:128], Ft, S0s, True, True),
                (self.psB[5][0:128, 128:256], Ft, Phs, True, True)],
                [ev_ft])
            ev_s0 = self.bank_r(5, "vector", partial(
                f_tt, out=S0s, in0=self.psB[5][0:128, 0:128], in1=sloc,
                op=ALU.add), [ev_sl])
            ev_ph = self.bank_r(5, "vector", partial(
                f_copy, out=Phs, in_=self.psB[5][0:128, 128:256]))
            prev_tail = [ev_s0, ev_ph]
            if h == 0 and cch == 0:
                prev_tail = prev_tail + \
                    self.tap("dbg_mach", mbv[:, 0:1472],
                             [ev_s0, ev_ph, ev_ope, ev_epe]) + \
                    self.tap("dbg_maf", self.af32[:, 8200:9160],
                             [ev_s0, ev_ph, ev_ope, ev_epe])
            out_evs += [ev_sp1, ev_sp2]
        out_evs += prev_tail
        return merge(out_evs)

    def ph_chain_ag(self, ev_at):
        dr = self.dr
        evs = []
        for h in range(H):
            self.bank_tr(6, [(self.psB[6][0:128, 0:128],
                              self.Phi[:, h * 128:(h + 1) * 128],
                              self.c_ident_bf[:])], ev_at)
            evs.append(self.bank_r(6, "vector", partial(
                f_copy, out=self.pgath[:, h * 128:(h + 1) * 128],
                in_=self.psB[6][0:128, 0:128])))
        evs.append(self.ve(partial(f_copy, out=self.sgath[:, 0:1024],
                                   in_=self.S0[:, :]), ev_at))
        e2 = self.dma("gpsimd", dr["phi_ag_in"][:, :], self.pgath[:, :],
                      evs, sem="d_ag")
        e3 = self.dma("gpsimd", dr["s0_ag_in"][:, :], self.sgath[:, 0:1024],
                      evs, sem="d_ag")
        a1 = self.po(partial(f_cc, kind="AllGather", op=ALU.bypass,
                             ins=[dr["phi_ag_in"].opt()],
                             outs=[dr["phi_ag_out"].opt()]), [e2, e3],
                     inc="s_ccC")
        a2 = self.po(partial(f_cc, kind="AllGather", op=ALU.bypass,
                             ins=[dr["s0_ag_in"].opt()],
                             outs=[dr["s0_ag_out"].opt()]), [], inc="s_ccC")
        return [a1, a2]

    def ph_combine(self, ev_ch):
        dr = self.dr
        prev = list(ev_ch)
        last_h = [None] * H
        for j in range(7):
            evp = self.dma("sync", self.pgath[:, :],
                           dr["phi_ag_out"][j * 128:(j + 1) * 128, :], prev,
                           sem="d_cb")
            evsg = self.dma("sync", self.sgath[:, :],
                            dr["s0_ag_out"][j * 128:(j + 1) * 128, :], prev,
                            sem="d_cb")
            prev = []
            for h in range(H):
                hp = h % 2
                ab = self.mb[:, hp * 1472:hp * 1472 + 128]
                e1 = self.ve(partial(
                    f_ts, out=ab, in0=self.pgath[:, h * 128:(h + 1) * 128],
                    s1=self.mrep[:, j:j + 1].opt(), s2=None, op0=ALU.mult,
                    op1=ALU.bypass),
                    [evp] + ([last_h[h]] if last_h[h] else []))
                e2 = self.ve(partial(
                    f_stt, out=ab, in0=self.c_ident_f32[:],
                    s=self.mrep1[:, j:j + 1].opt(), in1=ab, op0=ALU.mult,
                    op1=ALU.add), [e1])
                bk = 3 + hp
                self.bank_w(bk, [(self.psB[bk][0:128, 0:128], ab,
                                  self.Sin[:, h * 128:(h + 1) * 128],
                                  True, True)],
                            [e2] + ([last_h[h]] if last_h[h] else []))
                ms = self.af32[:, 8200 + 128 * hp:8328 + 128 * hp]
                e3 = self.ve(partial(
                    f_ts, out=ms, in0=self.sgath[:, h * 128:(h + 1) * 128],
                    s1=self.mrep[:, j:j + 1].opt(), s2=None, op0=ALU.mult,
                    op1=ALU.bypass), [evsg])
                e4 = self.bank_r(bk, "vector", partial(
                    f_tt, out=self.Sin[:, h * 128:(h + 1) * 128],
                    in0=self.psB[bk][0:128, 0:128], in1=ms, op=ALU.add),
                    [e3])
                last_h[h] = e4
                prev.append(e4)
        dm = merge(prev)
        dm += self.tap("dbg_sin", self.Sin[:, :], dm)
        return dm

    def ph_qxn(self, ev_mx, ev_at, ev_gt):
        dr = self.dr
        done = []
        st1 = self.af32[:, 4104:6152]
        first_w = merge(ev_at, ev_gt)
        for tc in range(NTC):
            evd = self.dma("sync", self.xtok[:, :],
                           dr["x_slice"][tc * 128:(tc + 1) * 128, :],
                           first_w if tc == 0 else done[-NFC:],
                           sem="d_qx")
            e1 = self.ve(partial(f_ts, out=st1, in0=self.xtok[:, :],
                                 s1=self.ln_m[:, tc:tc + 1].opt(),
                                 s2=self.ln_r[:, tc:tc + 1].opt(),
                                 op0=ALU.add, op1=ALU.mult),
                         [evd] + (first_w if tc == 0 else []))
            e2 = self.ve(partial(f_ts, out=self.wqi[:, 0:2048], in0=st1,
                                 s1=self.qcol[:, 0:1].opt(), s2=None,
                                 op0=ALU.mult, op1=ALU.bypass),
                         [e1] + ev_mx)

            e3 = self.ve(partial(f_copy, out=self.l2tmp[:, 0:2048],
                                 in_=self.wqi[:, 0:2048]), [e2])
            for fc in range(NFC):
                self.bank_tr(6, [(self.psB[6][0:128, 0:128],
                                  self.l2tmp[:, fc * 128:(fc + 1) * 128],
                                  self.c_ident_bf[:])], [e3])
                done.append(self.bank_r(6, "vector", partial(
                    f_copy,
                    out=self.arena[:, fc * 2048 + tc * 128:
                                   fc * 2048 + (tc + 1) * 128],
                    in_=self.psB[6][0:128, 0:128])))
        dm = merge(done)
        dm += self.tap("dbg_qxn", self.arena[:, 0:2048], dm)
        return dm

    def ph_ffn1(self, ev_qx, ev_wq, ev_mx):
        dr = self.dr
        e0 = self.ve(partial(f_ts, out=self.sc1sb[:, :],
                             in0=self.sw1sb[:, :],
                             s1=self.qcol[:, 1:2].opt(), s2=None,
                             op0=ALU.mult, op1=ALU.bypass),
                     [ev_wq["sw1"]] + ev_mx)
        done = []
        spills = []
        hm_ev = None
        wev = [None, None]
        for ic in range(NIC):
            par = ic % 2
            wsl = self.wb2[:, par * 2048:(par + 1) * 2048]
            wev[par] = self.dma("sync",
                                wsl.rearrange("p (fc i) -> p fc i", i=128),
                                dr["qw1_ag_out"][ic].rearrange(
                                    "p (fc i) -> p fc i", i=128),
                                [ev_wq["qw1"]] + (ev_qx if ic < 2 else
                                                  [done[-4]]), sem="d_w1")
            tile_evs = []
            for tt4 in range(4):
                bk = tt4
                mms = [(self.psB[bk][0:128, 0:512],
                        wsl[:, fc * 128:(fc + 1) * 128],
                        self.arena[:, fc * 2048 + tt4 * 512:
                                   fc * 2048 + (tt4 + 1) * 512],
                        fc == 0, fc == NFC - 1) for fc in range(NFC)]
                self.bank_w(bk, mms,
                            [wev[par]] + (ev_qx if ic < 2 else []))
                st = self.af32[:, 4104 + par * 2048 + tt4 * 512:
                               4104 + par * 2048 + (tt4 + 1) * 512]
                e1 = self.bank_r(bk, "scalar", partial(
                    f_act, out=st, in_=self.psB[bk][0:128, 0:512],
                    func=AF.Gelu, bias=self.b1sb[:, ic:ic + 1],
                    scale=self.sc1sb[:, ic:ic + 1]),
                    [e0] + ([spills[-2]] if len(spills) >= 2 else []))
                tile_evs.append(e1)
                done.append(e1)
            e2 = self.ve(partial(f_red, out=self.sc[:, 10:11],
                                 in_=self.af32[:, 4104 + par * 2048:
                                               4104 + par * 2048 + 2048],
                                 op=ALU.max, absval=True), tile_evs,
                         inc="d_vx")
            if hm_ev is None:
                hm_ev = self.ve(partial(f_copy, out=self.amx[:, 1:2],
                                        in_=self.sc[:, 10:11]), [e2])
            else:
                hm_ev = self.ve(partial(f_tt, out=self.amx[:, 1:2],
                                        in0=self.amx[:, 1:2],
                                        in1=self.sc[:, 10:11], op=ALU.max),
                                [e2, hm_ev])
            spills.append(self.dma("scalar", dr["h_spill"][ic],
                                   self.af32[:, 4104 + par * 2048:
                                             4104 + par * 2048 + 2048],
                                   tile_evs + [e2]))
        self.bank_tr(7, [(self.psB[7][0:1, 0:128], self.amx[:, 1:2],
                          self.c_ident_f32[:])], [hm_ev])
        e2 = self.bank_r(7, "vector", partial(
            f_red, out=self.st2r[0:1, 0:1], in_=self.psB[7][0:1, 0:128],
            op=ALU.max, absval=False), sem="d_vx")
        e3 = self.dma("gpsimd", dr["mh_in"][0:1, 0:1], self.st2r[0:1, 0:1],
                      [e2])
        e4 = self.po(partial(f_cc, kind="AllReduce", op=ALU.max,
                             ins=[dr["mh_in"].opt()],
                             outs=[dr["mh_out"].opt()]), [e3], inc="s_ccS")
        e5 = self.dma("gpsimd", self.st2r[0:1, 1:2], dr["mh_out"][0:1, 0:1],
                      [e4])
        self.bank_w(7, [(self.psB[7][0:128, 0:1], self.c_ones_row_f32[:],
                         self.st2r[0:1, 1:2], True, True)], [e5])
        e9 = self.bank_r(7, "vector", partial(
            f_ts, out=self.qcol[:, 2:3], in0=self.psB[7][0:128, 0:1],
            s1=1.0 / 127.0, s2=1e-8, op0=ALU.mult, op1=ALU.max))
        e10a = self.ac(partial(f_act, out=self.qcol[:, 5:6],
                               in_=self.qcol[:, 2:3], func=AF.Ln), [e9])
        e10 = self.ac(partial(f_act, out=self.qcol[:, 3:4],
                              in_=self.qcol[:, 5:6], func=AF.Exp,
                              scale=-1.0), [e10a])
        e11 = self.ve(partial(f_ts, out=self.sc2sb[:, :],
                              in0=self.sw2sb[:, :],
                              s1=self.qcol[:, 2:3].opt(), s2=None,
                              op0=ALU.mult, op1=ALU.bypass),
                      [e10, ev_wq["sw2"]])
        self.tap("dbg_qw", dr["qw1_ag_out"][0], [ev_wq["qw1"]])
        self.tap("dbg_swin", dr["sw1_ag_in"][:, :], [ev_wq["sw1"]])
        self.tap("dbg_swout", dr["sw1_ag_out"][:, :], [ev_wq["sw1"]])
        self.tap("dbg_qwin", dr["qw1_ag_in"][0], [ev_wq["qw1"]])
        self.tap("dbg_sw", self.sw1sb[:, :], [e0])
        self.tap("dbg_sc1", self.sc1sb[:, :], [e0])
        self.tap("dbg_b1", self.b1sb[:, :], [e0])
        if "dbg_h" in self.debug:
            for i in range(4):
                self.tap_evs.append(self.dma("gpsimd", self.dr["dbg_h"][i],
                                             dr["h_spill"][i], [spills[i]]))
        return {"h": merge(spills), "inv_s2": e10, "sc2": e11}

    def ph_ffn2(self, ev_f1, ev_wq):
        dr = self.dr
        g_evs = {}
        all_done = []
        blk_done = []
        for blk in range(4):
            t0 = blk * 512
            fills = []
            f_e1s = []
            for ic in range(NIC):
                par = ic % 2
                st = self.af32[:, 4104 + par * 2048:4104 + par * 2048 + 512]
                evd = self.dma("sync", st,
                               dr["h_spill"][ic][:, t0:t0 + 512],
                               merge(ev_f1["h"], blk_done) if ic < 2
                               else [f_e1s[-1]], sem="d_fh")
                e1 = self.ve(partial(
                    f_ts, out=self.wqi[:, par * 512:par * 512 + 512],
                    in0=st, s1=self.qcol[:, 3:4].opt(), s2=None,
                    op0=ALU.mult, op1=ALU.bypass),
                    [evd, ev_f1["inv_s2"]])
                f_e1s.append(e1)
                e2 = self.ve(partial(
                    f_copy, out=self.arena[:, ic * 512:(ic + 1) * 512],
                    in_=self.wqi[:, par * 512:par * 512 + 512]), [e1])
                fills.append(e2)
            blk_done = []
            wprev = [None, None]
            for oc2 in range(NFC):
                bk = oc2 % 2
                for q in range(4):
                    wpar = q % 2
                    wsl = self.wb2[:, wpar * 2048:(wpar + 1) * 2048]
                    evw = self.dma(
                        "sync", wsl.rearrange("p (i o) -> p i o", o=128),
                        dr["qw2_ag_out"][oc2].rearrange(
                            "p (i o) -> p i o", o=128)[
                            :, q * 16:(q + 1) * 16, :],
                        [ev_wq["qw2"]] + [w for w in wprev if w],
                        sem="d_w2")
                    mms = [(self.psB[bk][0:128, 0:512],
                            wsl[:, i16 * 128:(i16 + 1) * 128],
                            self.arena[:, (q * 16 + i16) * 512:
                                       (q * 16 + i16 + 1) * 512],
                            (q == 0 and i16 == 0), (q == 3 and i16 == 15))
                           for i16 in range(16)]
                    if q == 0:
                        ev_mm = self.bank_w(bk, mms, [evw] + fills)
                    else:
                        ev_mm = self.bank_w_more(bk, mms, [evw])
                    wprev[wpar] = ev_mm
                gst = self.af32[:, 2048 + (oc2 % 2) * 512:
                                2048 + (oc2 % 2) * 512 + 512]
                e1 = self.bank_r(bk, "vector", partial(
                    f_ts, out=gst, in0=self.psB[bk][0:128, 0:512],
                    s1=self.sc2sb[:, oc2:oc2 + 1].opt(),
                    s2=self.b2sb[:, oc2:oc2 + 1].opt(),
                    op0=ALU.mult, op1=ALU.add),
                    [ev_f1["sc2"]] +
                    ([g_evs[(oc2 - 2, blk)]] if (oc2 - 2, blk) in g_evs
                     else []))
                e2 = self.ac(partial(f_act, out=gst, in_=gst, func=AF.Silu),
                             [e1])
                ge = self.dma("scalar", dr["g_spill"][oc2, blk], gst, [e2])
                g_evs[(oc2, blk)] = ge
                blk_done.append(ge)
                all_done.append(ge)
        if "dbg_g" in self.debug:
            for i in range(2):
                for bq in range(4):
                    self.tap_evs.append(self.dma(
                        "gpsimd", self.dr["dbg_g"][i, bq],
                        dr["g_spill"][i, bq], [g_evs[(i, bq)]]))
        return merge(all_done), g_evs

    def ph_output(self, ev_f2, ev_si, g_evs):
        dr = self.dr
        evwo = self.dma("gpsimd", self.arena[:, 0:16384].rearrange(
            "p (kc o) -> p kc o", o=2048),
            dr["Wo"].rearrange("(kc p) o -> p kc o", p=128), ev_f2)
        out_evs = []
        prev_o = []
        prev_g = []
        prev_x = []
        for tc in range(NTC):
            w0 = ev_f2 if tc == 0 else prev_o
            ga = [self.dma("sync", self.af32[:, 6152:7176],
                           dr["gate_spill"][tc], w0, sem="d_oe")]
            ga.append(self.dma("sync", self.opre_ld[0:64, :],
                               dr["opre_spill"][tc * 2], w0, sem="d_oe"))
            ga.append(self.dma("sync", self.opre_ld[64:128, :],
                               dr["opre_spill"][tc * 2 + 1], w0, sem="d_oe"))
            ga.append(self.dma("sync", self.ephi_ld[:, :],
                               dr["ephi_spill"][tc * 2], w0, sem="d_oe"))
            ga.append(self.dma("sync", self.xtok[:, :],
                               dr["x_slice"][tc * 128:(tc + 1) * 128, :],
                               merge(w0, prev_x), sem="d_oe"))
            grpA = merge(ga)
            grp = grpA
            evx = grpA
            o_evs = []
            last_e8 = {}
            for half in range(2):
                p0, p1 = half * 64, half * 64 + 64
                if half == 1:
                    evb = self.dma("sync", self.ephi_ld[:, :],
                                   dr["ephi_spill"][tc * 2 + 1], o_evs,
                                   sem="d_oe")
                    grp = merge(grpA, [evb])
                for h in range(H):
                    hp = h % 2
                    bk = hp
                    scb = 8 + hp * 4
                    self.bank_w(bk, [(self.psB[bk][p0:p1, 0:128],
                                      self.ephi_ld[:, h * 64:(h + 1) * 64],
                                      self.Sin[:, h * 128:(h + 1) * 128],
                                      True, True)], grp + ev_si)
                    of = self.af32[p0:p1, 8200 + hp * 128:8328 + hp * 128]
                    e1 = self.bank_r(bk, "vector", partial(
                        f_tt, out=of, in0=self.psB[bk][p0:p1, 0:128],
                        in1=self.opre_ld[p0:p1, h * 128:(h + 1) * 128],
                        op=ALU.add), grp)
                    e2 = self.ac(partial(
                        f_act,
                        out=self.af32[p0:p1, 8456 + hp * 128:8584 + hp * 128],
                        in_=of, func=AF.Square,
                        accum=self.sc[p0:p1, scb:scb + 1]), [e1])
                    e3 = self.ac(partial(
                        f_act, out=self.sc[p0:p1, scb + 1:scb + 2],
                        in_=self.sc[p0:p1, scb:scb + 1],
                        func=AF.Ln, bias=self.epsc[p0:p1, 0:1],
                        scale=1.0 / 128.0), [e2])
                    e4 = self.ac(partial(
                        f_act, out=self.sc[p0:p1, scb + 2:scb + 3],
                        in_=self.sc[p0:p1, scb + 1:scb + 2],
                        func=AF.Exp, scale=-0.5), [e3])
                    e5 = self.ve(partial(
                        f_ts, out=of, in0=of,
                        s1=self.sc[p0:p1, scb + 2:scb + 3].opt(), s2=None,
                        op0=ALU.mult, op1=ALU.bypass), [e4])
                    e6 = self.ve(partial(
                        f_tt, out=of, in0=of,
                        in1=self.c_norm_w_rep[p0:p1, :], op=ALU.mult), [e5])
                    sg = self.af32[p0:p1, 8712 + hp * 128:8840 + hp * 128]
                    e7 = self.ac(partial(
                        f_act, out=sg,
                        in_=self.af32[p0:p1,
                                      6152 + h * 128:6152 + (h + 1) * 128],
                        func=AF.Silu),
                        grp + ([last_e8[hp]] if hp in last_e8 else []))
                    e8 = self.ve(partial(
                        f_tt, out=self.otok[p0:p1, h * 128:(h + 1) * 128],
                        in0=of, in1=sg, op=ALU.mult), [e6, e7])
                    last_e8[hp] = e8
                    o_evs.append(e8)
            tr_evs = []
            for h in range(H):
                self.bank_tr(6, [(self.psB[6][0:128, 0:128],
                                  self.otok[:, h * 128:(h + 1) * 128],
                                  self.c_ident_bf[:])], o_evs)
                tr_evs.append(self.bank_r(6, "vector", partial(
                    f_copy, out=self.ofm[:, h * 128:(h + 1) * 128],
                    in_=self.psB[6][0:128, 0:128])))
            prev_o = []
            for ot in range(4):
                bk = 2 + ot % 2
                mms = [(self.psB[bk][0:128, 0:512],
                        self.ofm[:, kc * 128:(kc + 1) * 128],
                        self.arena[:, kc * 2048 + ot * 512:
                                   kc * 2048 + (ot + 1) * 512],
                        kc == 0, kc == H - 1) for kc in range(H)]
                self.bank_w(bk, mms, tr_evs + [evwo])
                gq = [self.dma(
                    "sync", self.af32[:, 3072 + i4 * 128:
                                      3072 + (i4 + 1) * 128],
                    dr["g_spill"][ot * 4 + i4, tc // 4][
                        :, (tc % 4) * 128:(tc % 4 + 1) * 128],
                    [g_evs[(ot * 4 + i4, tc // 4)]] + prev_g, sem="d_gq")
                    for i4 in range(4)]
                gqa = merge(gq)
                gp_evs = []
                for i4 in range(4):
                    gpst = self.af32[:, 3072 + i4 * 128:3072 + (i4 + 1) * 128]
                    self.bank_tr(7, [(self.psB[7][0:128, 0:128], gpst,
                                      self.c_ident_f32[:])], gqa)
                    gp_evs.append(self.bank_r(7, "vector", partial(
                        f_copy, out=self.gtok[:, i4 * 128:(i4 + 1) * 128],
                        in_=self.psB[7][0:128, 0:128])))
                prev_g = gp_evs
                e1 = self.bank_r(bk, "vector", partial(
                    f_tt, out=self.fin[:, 0:512],
                    in0=self.psB[bk][0:128, 0:512],
                    in1=self.gtok[:, 0:512], op=ALU.mult),
                    gp_evs + (out_evs[-2:] if len(out_evs) >= 2 else []))
                e2 = self.ve(partial(
                    f_tt, out=self.fin[:, 512:1024],
                    in0=self.fin[:, 0:512],
                    in1=self.xtok[:, ot * 512:(ot + 1) * 512],
                    op=ALU.add), [e1] + evx)
                oev = self.dma("sync",
                               dr["out"][tc * 128:(tc + 1) * 128,
                                         ot * 512:(ot + 1) * 512],
                               self.fin[:, 512:1024], [e2])
                out_evs.append(oev)
                prev_o.append(oev)
            prev_x = prev_o[-1:]
        if "dbg_opre" in self.debug:
            for cch in range(NCH):
                self.tap_evs.append(self.dma(
                    "gpsimd", self.dr["dbg_opre"][cch],
                    dr["opre_spill"][cch], ev_f2))
        if "dbg_ephi" in self.debug:
            for cch in range(NCH):
                self.tap_evs.append(self.dma(
                    "gpsimd", self.dr["dbg_ephi"][cch],
                    dr["ephi_spill"][cch], ev_f2))
        if "dbg_gate_all" in self.debug:
            for tcq in range(NTC):
                self.tap_evs.append(self.dma(
                    "gpsimd", self.dr["dbg_gate_all"][tcq],
                    dr["gate_spill"][tcq], ev_f2))
        if "dbg_gall" in self.debug:
            for oc2 in range(NFC):
                for bq in range(4):
                    self.tap_evs.append(self.dma(
                        "gpsimd", self.dr["dbg_gall"][oc2, bq],
                        dr["g_spill"][oc2, bq], ev_f2))
        return merge(out_evs)


def make_consts():
    import ml_dtypes
    bf = ml_dtypes.bfloat16
    t = np.arange(C)
    t_of = np.concatenate([t, t])
    j_of = np.concatenate([np.zeros(C, np.int64), np.ones(C, np.int64)])
    orig = 2 * t_of + j_of
    amask = (orig[:, None] > orig[None, :]).astype(np.float32)
    mmaskT = (t_of[:, None] <= t[None, :]).astype(np.float32)
    i64p = np.concatenate([np.eye(64), np.eye(64)], axis=0).astype(bf)
    return {
        "amask": np.ascontiguousarray(amask),
        "mmaskT": np.ascontiguousarray(mmaskT),
        "ident_bf": np.ascontiguousarray(np.eye(128, dtype=bf)),
        "ident_f32": np.ascontiguousarray(np.eye(128, dtype=np.float32)),
        "i64p_bf": np.ascontiguousarray(i64p),
        "ones_f32": np.ascontiguousarray(np.ones((128, 128), np.float32)),
        "ones_col_bf": np.ascontiguousarray(np.ones((128, 1), dtype=bf)),
        "ones_row_bf": np.ascontiguousarray(np.ones((1, 128), dtype=bf)),
        "ones_row_f32": np.ascontiguousarray(np.ones((1, 128), np.float32)),
    }


def host_prepare(inputs):
    x = np.asarray(inputs["L0_attn_x"], np.float32)
    consts = make_consts()
    w1 = np.asarray(inputs["L0_ffn_w1"], np.float32)
    w2 = np.asarray(inputs["L0_ffn_w2"], np.float32)
    cw = np.concatenate([
        np.asarray(inputs["L0_attn_conv_q_w"], np.float32).reshape(KD, CONV),
        np.asarray(inputs["L0_attn_conv_k_w"],
                   np.float32).reshape(2 * KD, CONV),
        np.asarray(inputs["L0_attn_conv_v_w"],
                   np.float32).reshape(2 * KD, CONV),
    ], axis=0)
    cw_oc = np.zeros((128, 40 * CONV), np.float32)
    for oc in range(40):
        cw_oc[:, oc * CONV:(oc + 1) * CONV] = cw[oc * 128:(oc + 1) * 128]
    per_core = []
    for c in range(8):
        b, p = c // 4, c % 4
        t0 = p * TPC
        m = {
            "x_slice": np.ascontiguousarray(x[b, t0:t0 + TPC]),
            "xhalo": (np.zeros((3, D), np.float32) if p == 0 else
                      np.ascontiguousarray(x[b, t0 - 3:t0])),
            "Wq": np.ascontiguousarray(inputs["L0_attn_Wq"], np.float32),
            "Wk": np.ascontiguousarray(inputs["L0_attn_Wk"], np.float32),
            "Wv": np.ascontiguousarray(inputs["L0_attn_Wv"], np.float32),
            "Wb": np.ascontiguousarray(inputs["L0_attn_Wb"], np.float32),
            "Wa": np.ascontiguousarray(inputs["L0_attn_Wa"], np.float32),
            "Wg": np.ascontiguousarray(inputs["L0_attn_Wg"], np.float32),
            "Wo": np.ascontiguousarray(inputs["L0_attn_Wo"], np.float32),
            "A_log_c": np.ascontiguousarray(
                np.asarray(inputs["L0_attn_A_log"],
                           np.float32).reshape(8, 1)),
            "dt_bias_c": np.ascontiguousarray(
                np.asarray(inputs["L0_attn_dt_bias"],
                           np.float32).reshape(8, 1)),
            "convw_all": cw_oc,
            "norm_w_rep": np.ascontiguousarray(np.broadcast_to(
                np.asarray(inputs["L0_attn_norm_w"], np.float32)[None, :],
                (128, 128))),
            "w1_slice": np.ascontiguousarray(
                w1[c * 1024:(c + 1) * 1024].reshape(8, 128, D)),
            "b1_col": np.ascontiguousarray(
                np.asarray(inputs["L0_ffn_b1"],
                           np.float32).reshape(NIC, 128)),
            "w2_slice": np.ascontiguousarray(
                w2[c * 256:(c + 1) * 256].reshape(2, 128, I_FFN)),
            "b2_col": np.ascontiguousarray(
                np.asarray(inputs["L0_ffn_b2"],
                           np.float32).reshape(NFC, 128)),
            "chain_m_rep": np.ascontiguousarray(np.broadcast_to(
                np.array([1.0 if (j // 4 == b and j % 4 < p) else 0.0
                          for j in range(7)], np.float32)[None, :],
                (128, 7))),
        }
        m.update(consts)
        per_core.append(m)
    return per_core


DEBUG = ()


def kernel(**inputs):
    per_core = host_prepare(inputs)
    nc = Builder(debug=DEBUG).build()
    res = run_bass_kernel_spmd(nc, per_core, core_ids=list(range(8)))
    out = np.empty((B, L, D), np.float32)
    for c in range(8):
        b, p = c // 4, c % 4
        out[b, p * TPC:(p + 1) * TPC] = res.results[c]["out"]
    if DEBUG:
        kernel.dbg = [{k: np.asarray(v) for k, v in r.items()}
                      for r in res.results]
    return out


# version 09
# speedup vs baseline: 1.0455x; 1.0094x over previous
import contextlib
from functools import partial

import numpy as np

import concourse.bass as bass
import concourse.mybir as mybir
from concourse.bass_utils import run_bass_kernel_spmd

F32 = mybir.dt.float32
BF16 = mybir.dt.bfloat16
I16 = mybir.dt.int16
AF = mybir.ActivationFunctionType
ALU = mybir.AluOpType
AX = mybir.AxisListType

B, L, D = 2, 8192, 2048
H, DH, NH, CONV = 8, 128, 2, 4
KD = H * DH
C = 64
R = 2 * C
TPC = 2048
NCH = TPC // C
NTC = TPC // 128
NFC = D // 128
I_FFN = 8192
NIC = I_FFN // 128
ISQD = 1.0 / float(np.sqrt(np.float32(DH)))

ENGINES = ("tensor", "vector", "scalar", "gpsimd", "sync")


class Ev:
    __slots__ = ("sem", "val", "eng")

    def __init__(self, sem, val, eng):
        self.sem, self.val, self.eng = sem, val, eng


def merge(*evs):
    best = {}
    for e in evs:
        if e is None:
            continue
        if isinstance(e, (list, tuple)):
            for x in e:
                if x is not None and (x.sem not in best or
                                      best[x.sem].val < x.val):
                    best[x.sem] = x
        elif e.sem not in best or best[e.sem].val < e.val:
            best[e.sem] = e
    return list(best.values())


class Prog:
    def __init__(self):
        self.stream = {e: [] for e in ENGINES}
        self.count = {}
        self.sem_names = set()

    COMPUTE_SEMS = ("s_te",)

    def on(self, eng, fn, waits=(), inc=None, amt=None):
        ws = merge(*waits)
        ws = [Ev(w.sem, self.count[w.sem], w.eng)
              if (w.sem.startswith("d") and self.count.get(w.sem, 0) > w.val)
              else w for w in ws]
        ev = None
        if inc is not None:
            if amt is None:
                amt = 16 if inc.startswith("d") else 1
            self.sem_names.add(inc)
            self.count[inc] = self.count.get(inc, 0) + amt
            ev = Ev(inc, self.count[inc], eng)
        for w in ws:
            self.sem_names.add(w.sem)
        self.stream[eng].append((ws, fn, inc, amt))
        return ev

    def replay(self, block, sems):
        for eng in ENGINES:
            ops = self.stream[eng]
            if not ops:
                continue

            def make_body(eng, ops):
                def body(engobj):
                    last = {}
                    for ws, fn, inc, amt in ops:
                        for w in ws:
                            if w.eng == eng and w.sem in Prog.COMPUTE_SEMS:
                                continue
                            if last.get(w.sem, -1) >= w.val:
                                continue
                            engobj.wait_ge(sems[w.sem], w.val)
                            last[w.sem] = w.val
                        instr = fn(engobj)
                        if inc is not None:
                            instr.then_inc(sems[inc], amt)
                return body
            getattr(block, eng)(make_body(eng, ops))


def f_dma(e, out, in_, slow=False):
    if slow:
        return e.dma_start(out=out, in_=in_, allow_slow_non_contiguous=True)
    return e.dma_start(out=out, in_=in_)


def f_mm(e, out, lhsT, rhs, start, stop):
    return e.matmul(out, lhsT, rhs, start=start, stop=stop)


def f_tr(e, out, in_, ident):
    return e.transpose(out, in_, ident)


def f_copy(e, out, in_):
    if hasattr(e, "tensor_copy"):
        return e.tensor_copy(out=out, in_=in_)
    return e.copy(out=out, in_=in_)


def f_ts(e, out, in0, s1, s2, op0, op1):
    return e.tensor_scalar(out=out, in0=in0, scalar1=s1, scalar2=s2,
                           op0=op0, op1=op1)


def f_tt(e, out, in0, in1, op):
    return e.tensor_tensor(out=out, in0=in0, in1=in1, op=op)


def f_stt(e, out, in0, s, in1, op0, op1):
    return e.scalar_tensor_tensor(out=out, in0=in0, scalar=s, in1=in1,
                                  op0=op0, op1=op1)


def f_act(e, out, in_, func, bias=0.0, scale=1.0, accum=None):
    if accum is None:
        return e.activation(out=out, in_=in_, func=func, bias=bias,
                            scale=scale)
    return e.activation(out=out, in_=in_, func=func, bias=bias, scale=scale,
                        accum_out=accum)


def f_red(e, out, in_, op, absval):
    return e.tensor_reduce(out=out, in_=in_, axis=AX.X, op=op,
                           apply_absolute_value=absval)


def f_scan(e, out, d0, d1, init, op0, op1):
    return e.tensor_tensor_scan(out=out, data0=d0, data1=d1, initial=init,
                                op0=op0, op1=op1)


def f_recip(e, out, in_):
    return e.reciprocal(out=out, in_=in_)


def f_memset(e, ap, v):
    return e.memset(ap, v)


def f_cc(e, kind, op, ins, outs):
    return e.collective_compute(kind, op, replica_groups=[list(range(8))],
                                ins=ins, outs=outs)


def f_nop(e):
    return e.nop()


DEBUG_TAPS = {
    "dbg_xnfm": ((128, 2048), BF16),
    "dbg_qt": ((128, NCH * 64), BF16),
    "dbg_kt": ((128, NCH * 128), BF16),
    "dbg_vt": ((128, NCH * 128), BF16),
    "dbg_G": ((8, TPC), F32),
    "dbg_Br": ((128, 8 * NCH), F32),
    "dbg_opre": ((NCH, 64, H * 128), F32),
    "dbg_ephi": ((NCH, 128, H * 64), BF16),
    "dbg_S0": ((128, H * 128), BF16),
    "dbg_Phi": ((128, H * 128), BF16),
    "dbg_sin": ((128, H * 128), BF16),
    "dbg_h": ((4, 128, TPC), F32),
    "dbg_g": ((2, 4, 128, 512), F32),
    "dbg_gall": ((NFC, 4, 128, 512), F32),
    "dbg_gate_all": ((NTC, 128, KD), F32),
    "dbg_gate": ((128, KD), F32),
    "dbg_qxn": ((128, 2048), BF16),
    "dbg_sx": ((128, 8), F32),
    "dbg_amx": ((128, 2), F32),
    "dbg_st1": ((1, 16), F32),
    "dbg_mach": ((128, 1472), BF16),
    "dbg_maf": ((128, 960), F32),
    "dbg_Gr": ((128, 8 * NCH), F32),
    "dbg_bagg": ((16, TPC), F32),
    "dbg_xnf2": ((128, 2048), BF16),
    "dbg_mxl": ((1, 4), F32),
    "dbg_P": ((128, 128), F32),
    "dbg_naf": ((128, 128), F32),
    "dbg_qlat": ((128, 16), F32),
    "dbg_qw": ((128, 2048), BF16),
    "dbg_sw": ((128, NIC), F32),
    "dbg_sc1": ((128, NIC), F32),
    "dbg_b1": ((128, NIC), F32),
    "dbg_hxx": ((128, 2048), F32),
    "dbg_swin": ((1, 1024), F32),
    "dbg_swout": ((8, 1024), F32),
    "dbg_qwin": ((128, 2048), BF16),
    "dbg_wst": ((128, 1024), F32),
    "dbg_wsc": ((128, 8), F32),
}


class Builder:
    def __init__(self, debug=()):
        self.nc = bass.Bass("TRN2", target_bir_lowering=False)
        self.P = Prog()
        self.debug = set(debug)
        self.tap_evs = []
        self.dr = {}
        self.stk = contextlib.ExitStack()
        self.sb_bytes = 0
        self.bankst = {i: {"w": None, "r": []} for i in range(8)}
        self._dram()
        self._sbuf()
        self._psum()

    def _dram(self):
        nc = self.nc

        def din(name, shape, dtype=F32):
            self.dr[name] = nc.dram_tensor(name, list(shape), dtype,
                                           kind="ExternalInput").ap()

        def dscr(name, shape, dtype=F32, space="Local"):
            self.dr[name] = nc.dram_tensor(name, list(shape), dtype,
                                           addr_space=space).ap()

        din("x_slice", (TPC, D)); din("xhalo", (3, D))
        din("Wq", (D, KD)); din("Wk", (D, 2 * KD)); din("Wv", (D, 2 * KD))
        din("Wb", (D, 16)); din("Wa", (D, 8)); din("Wg", (D, KD))
        din("Wo", (KD, D))
        din("A_log_c", (8, 1)); din("dt_bias_c", (8, 1))
        din("convw_all", (128, 40 * CONV))
        din("norm_w_rep", (128, 128))
        din("w1_slice", (8, 128, D)); din("b1_col", (NIC, 128))
        din("w2_slice", (2, 128, I_FFN)); din("b2_col", (NFC, 128))
        din("chain_m_rep", (128, 7))
        din("amask", (R, R)); din("mmaskT", (R, C))
        din("ident_bf", (128, 128), BF16)
        din("ident_f32", (128, 128), F32)
        din("i64p_bf", (128, 64), BF16)
        din("ones_f32", (128, 128), F32)
        din("ones_col_bf", (128, 1), BF16)
        din("ones_row_bf", (1, 128), BF16)
        din("ones_row_f32", (1, 128), F32)
        self.dr["out"] = nc.dram_tensor("out", [TPC, D], F32,
                                        kind="ExternalOutput").ap()
        for name in self.debug:
            shape, dtype = DEBUG_TAPS[name]
            self.dr[name] = nc.dram_tensor(name, list(shape), dtype,
                                           kind="ExternalOutput").ap()

        dscr("h_spill", (NIC, 128, TPC), F32)
        dscr("opre_spill", (NCH, 64, H * 128), F32)
        dscr("ephi_spill", (NCH, 128, H * 64), BF16)
        dscr("g_spill", (NFC, 4, 128, 512), F32)
        dscr("gate_spill", (NTC, 128, KD), F32)
        dscr("qw1_ag_in", (8, 128, NFC * 128), BF16)
        dscr("qw1_ag_out", (NIC, 128, NFC * 128), BF16, "Shared")
        dscr("sw1_ag_in", (1, 8 * 128), F32)
        dscr("sw1_ag_out", (8, 8 * 128), F32, "Shared")
        dscr("qw2_ag_in", (2, 128, NIC * 128), BF16)
        dscr("qw2_ag_out", (NFC, 128, NIC * 128), BF16, "Shared")
        dscr("sw2_ag_in", (1, 2 * 128), F32)
        dscr("sw2_ag_out", (8, 2 * 128), F32, "Shared")
        dscr("s0_ag_in", (128, H * 128), F32)
        dscr("s0_ag_out", (8 * 128, H * 128), F32, "Shared")
        dscr("phi_ag_in", (128, H * 128), BF16)
        dscr("phi_ag_out", (8 * 128, H * 128), BF16, "Shared")
        dscr("mx_in", (1, 8), F32); dscr("mx_out", (1, 8), F32)
        dscr("mh_in", (1, 8), F32); dscr("mh_out", (1, 8), F32)

    def _sbuf(self):
        def s(name, shape, dtype):
            t = self.stk.enter_context(
                self.nc.sbuf_tensor(name, list(shape), dtype))
            self.sb_bytes += int(np.prod(shape[1:])) * \
                {F32: 4, BF16: 2, I16: 2}[dtype]
            setattr(self, name, t)
            return t

        s("arena", (128, 32768), BF16)
        s("xnh", (128, 64), BF16)
        s("qt", (128, NCH * 64), BF16)
        s("kt", (128, NCH * 128), BF16)
        s("vt", (128, NCH * 128), BF16)
        s("af32", (128, 10240), F32)
        s("wqi", (128, 2048), I16)
        s("wb2", (128, 4096), BF16)
        s("l2tmp", (128, 2048), BF16)
        s("xtok", (128, 2048), F32)
        s("mb", (128, 2944), BF16)
        s("Grows", (128, 8 * NCH), F32)
        s("nGrows", (128, 8 * NCH), F32)
        s("Brows", (128, 8 * NCH), F32)
        s("eGrows", (128, 8 * NCH), F32)
        s("S0", (128, H * 128), BF16)
        s("Phi", (128, H * 128), BF16)
        s("Sin", (128, H * 128), BF16)
        s("bagg", (16, TPC), F32)
        s("gfm", (8, TPC), F32)
        s("Gfm", (8, TPC), F32)
        s("ln_m", (128, 17), F32)
        s("ln_r", (128, 17), F32)
        s("amx", (128, 2), F32)
        s("sc", (128, 26), F32)
        s("bsc", (8, 2), F32)
        s("mxl", (1, 4), F32)
        s("qlat", (128, 16), F32)
        s("st1r", (1, 512), F32)
        s("st2r", (1, 512), F32)
        s("stb", (1, 512), BF16)
        s("qcol", (128, 8), F32)
        s("b1sb", (128, NIC), F32)
        s("b2sb", (128, NFC), F32)
        s("sw1sb", (128, NIC), F32)
        s("sw2sb", (128, NFC), F32)
        s("sc1sb", (128, NIC), F32)
        s("sc2sb", (128, NFC), F32)
        s("opre_st", (64, 2 * 128), F32)
        s("ephi_st", (128, 2 * 64), BF16)
        s("ephi_ld", (128, H * 64), BF16)
        s("otok", (128, KD), BF16)
        s("ofm", (128, KD), BF16)
        s("mrep", (128, 7), F32)
        s("mrep1", (128, 7), F32)
        s("pgath", (128, 1024), BF16)
        s("convw", (128, 40 * CONV), F32)
        s("epsc", (128, 1), F32)
        for nm, shape, dt in [("amask", (R, R), F32), ("mmaskT", (R, C), F32),
                              ("ident_bf", (128, 128), BF16),
                              ("ident_f32", (128, 128), F32),
                              ("i64p_bf", (128, 64), BF16),
                              ("ones_f32", (128, 128), F32),
                              ("ones_col_bf", (128, 1), BF16),
                              ("ones_row_bf", (1, 128), BF16),
                              ("ones_row_f32", (1, 128), F32),
                              ("norm_w_rep", (128, 128), F32),
                              ("A_log_c", (8, 1), F32),
                              ("dt_bias_c", (8, 1), F32)]:
            s("c_" + nm, shape, dt)
        assert self.sb_bytes <= 213000, f"sbuf over: {self.sb_bytes}"
        self.sgath = self.af32[:, 0:1024]
        self.gtok = self.af32[:, 1024:1536]
        self.opre_ld = self.af32[:, 4104:5128]
        self.fin = self.af32[:, 5128:6152]

    def _psum(self):
        self.psB = []
        for i in range(6):
            self.psB.append(self.stk.enter_context(
                self.nc.psum_tensor(f"psf{i}", [128, 512], F32)))
        self.psB.append(self.stk.enter_context(
            self.nc.psum_tensor("psb6", [128, 1024], BF16)))
        self.psB.append(self.stk.enter_context(
            self.nc.psum_tensor("psf7", [128, 512], F32)))

    def dma(self, q, out, in_, waits=(), sem=None, slow=False):
        sem = sem or {"sync": "d_sp", "gpsimd": "d_pool",
                      "scalar": "d_act"}[q]
        return self.P.on(q, partial(f_dma, out=out, in_=in_, slow=slow),
                         waits, inc=sem)

    def ve(self, fn, waits=(), inc="s_dve"):
        return self.P.on("vector", fn, waits, inc=inc, amt=1)

    def ac(self, fn, waits=(), inc="s_act"):
        return self.P.on("scalar", fn, waits, inc=inc)

    def po(self, fn, waits=(), inc="s_pool"):
        return self.P.on("gpsimd", fn, waits, inc=inc)

    def bank_w(self, bk, mms, waits=()):
        st = self.bankst[bk]
        w = list(waits) + st["r"] + ([st["w"]] if st["w"] else [])
        st["r"] = []
        ev = None
        for i, (out, lhsT, rhs, a, b) in enumerate(mms):
            ev = self.P.on("tensor",
                           partial(f_mm, out=out, lhsT=lhsT, rhs=rhs,
                                   start=a, stop=b),
                           w if i == 0 else (),
                           inc="s_te" if i == len(mms) - 1 else None)
        st["w"] = ev
        return ev

    def bank_w_more(self, bk, mms, waits=()):
        st = self.bankst[bk]
        ev = None
        for i, (out, lhsT, rhs, a, b) in enumerate(mms):
            ev = self.P.on("tensor",
                           partial(f_mm, out=out, lhsT=lhsT, rhs=rhs,
                                   start=a, stop=b),
                           list(waits) if i == 0 else (),
                           inc="s_te" if i == len(mms) - 1 else None)
        st["w"] = ev
        return ev

    def bank_tr(self, bk, trs, waits=()):
        st = self.bankst[bk]
        w = list(waits) + st["r"] + ([st["w"]] if st["w"] else [])
        st["r"] = []
        ev = None
        for i, (out, in_, ident) in enumerate(trs):
            ev = self.P.on("tensor",
                           partial(f_tr, out=out, in_=in_, ident=ident),
                           w if i == 0 else (),
                           inc="s_te" if i == len(trs) - 1 else None)
        st["w"] = ev
        return ev

    def bank_r(self, bk, eng, fn, waits=(), sem=None):
        st = self.bankst[bk]
        w = list(waits) + ([st["w"]] if st["w"] else [])
        sem = sem or {"vector": "s_dve", "scalar": "s_act"}[eng]
        ev = self.P.on(eng, fn, w, inc=sem, amt=1)
        st["r"].append(ev)
        return ev

    def tap(self, name, src, waits):
        if name in self.debug:
            self.tap_evs.append(self.dma("gpsimd", self.dr[name], src,
                                         waits))
            return [self.tap_evs[-1]]
        return []

    def xn_rhs(self, fc, t0, n):
        return self.arena[:, fc * 2048 + t0:fc * 2048 + t0 + n]

    def build(self):
        ev_c = self.ph_consts()
        ev_ln = self.ph_ln(ev_c)
        ev_mx = self.ph_sx_ar(ev_ln)
        ev_wq, ev_stage = self.ph_wquant(ev_c, ev_ln)
        ev_bg = self.ph_beta_g(ev_c, ev_ln)
        ev_at = self.ph_heads(ev_c, ev_bg, ev_stage)
        ev_ch = self.ph_chain_ag(ev_at)
        ev_qx = self.ph_qxn(ev_mx, ev_at, [])
        ev_f1 = self.ph_ffn1(ev_qx, ev_wq, ev_mx)
        ev_si = self.ph_combine(ev_ch)
        ev_f2, g_evs = self.ph_ffn2(ev_f1, ev_wq)
        ev_ou = self.ph_output(ev_f2, ev_si, g_evs)
        self.P.on("sync", f_nop, [merge(ev_ou, self.tap_evs)])
        sems = {}
        with self.stk:
            with contextlib.ExitStack() as ss:
                for name in sorted(self.P.sem_names):
                    sems[name] = ss.enter_context(self.nc.semaphore(name))
                with self.nc.Block() as block:
                    self.P.replay(block, sems)
        return self.nc

    def ph_consts(self):
        evs = []
        for nm in ["amask", "mmaskT", "ident_bf", "ident_f32", "i64p_bf",
                   "ones_f32", "ones_col_bf", "ones_row_bf", "ones_row_f32",
                   "norm_w_rep", "A_log_c", "dt_bias_c"]:
            evs.append(self.dma("sync", getattr(self, "c_" + nm)[:],
                                self.dr[nm][:, :]))
        evs.append(self.dma("sync", self.convw[:], self.dr["convw_all"][:, :]))
        evs.append(self.dma("sync", self.mrep[:],
                            self.dr["chain_m_rep"][:, :]))
        evs.append(self.dma("sync", self.b1sb[:],
                            self.dr["b1_col"].rearrange("i p -> p i"),
                            slow=True))
        evs.append(self.dma("sync", self.b2sb[:],
                            self.dr["b2_col"].rearrange("i p -> p i"),
                            slow=True))
        e2 = self.ve(partial(f_ts, out=self.mrep1[:], in0=self.mrep[:],
                             s1=-1.0, s2=1.0, op0=ALU.mult, op1=ALU.add), evs)
        e_eps = self.ve(partial(f_ts, out=self.epsc[:],
                                in0=self.c_ones_f32[:, 0:1], s1=1e-5,
                                s2=None, op0=ALU.mult, op1=ALU.bypass),
                        list(evs))
        evs.append(e_eps)
        e3 = self.ve(partial(f_memset, ap=self.S0[:], v=0.0))
        e4 = self.ve(partial(f_memset, ap=self.Sin[:], v=0.0))
        phi_evs = [self.ve(partial(f_copy,
                                   out=self.Phi[:, hh * 128:(hh + 1) * 128],
                                   in_=self.c_ident_bf[:]), evs)
                   for hh in range(H)]
        return merge(evs, e2, e3, e4, phi_evs)

    def ph_ln(self, ev_c):
        dr = self.dr
        done = []
        ev_mx = None
        st1 = self.af32[:, 4104:6152]
        for tt in range(NTC + 1):
            halo = tt == NTC
            rows = 3 if halo else 128
            src = dr["xhalo"][:, :] if halo else \
                dr["x_slice"][tt * 128:(tt + 1) * 128, :]
            evd = self.dma("sync", self.xtok[0:rows, :], src,
                           done[-NFC:] if done else ev_c, sem="d_lx")
            xs = self.xtok[0:rows, :]
            col = self.ln_m[0:rows, tt:tt + 1]
            rcol = self.ln_r[0:rows, tt:tt + 1]
            e1 = self.ac(partial(f_act, out=st1[0:rows, :], in_=xs,
                                 func=AF.Identity,
                                 accum=self.sc[0:rows, 0:1]), [evd])
            ef = self.ac(partial(f_copy, out=self.sc[0:rows, 3:4],
                                 in_=self.sc[0:rows, 0:1]), [e1])
            e2 = self.ve(partial(f_ts, out=col, in0=self.sc[0:rows, 0:1],
                                 s1=-1.0 / D, s2=None, op0=ALU.mult,
                                 op1=ALU.bypass), [ef])
            e3 = self.ac(partial(f_act, out=st1[0:rows, :], in_=xs,
                                 func=AF.Square, bias=col, scale=1.0,
                                 accum=self.sc[0:rows, 1:2]), [e2])
            e5 = self.ac(partial(f_act, out=self.sc[0:rows, 3:4],
                                 in_=self.sc[0:rows, 1:2], func=AF.Ln,
                                 bias=self.epsc[0:rows, 0:1],
                                 scale=1.0 / D), [e3])
            e6 = self.ac(partial(f_act, out=rcol,
                                 in_=self.sc[0:rows, 3:4], func=AF.Exp,
                                 scale=-0.5), [e5])
            e7 = self.ve(partial(f_ts, out=st1[0:rows, :], in0=xs,
                                 s1=col.opt(), s2=rcol.opt(), op0=ALU.add,
                                 op1=ALU.mult), [e6])
            if not halo:
                if tt == 0:
                    ev_mx = self.ve(partial(f_red, out=self.amx[:, 0:1],
                                            in_=st1, op=ALU.max,
                                            absval=True), [e7], inc="d_vx")
                else:
                    em = self.ve(partial(f_red, out=self.sc[:, 4:5], in_=st1,
                                         op=ALU.max, absval=True), [e7],
                                 inc="d_vx")
                    ev_mx = self.ve(partial(f_tt, out=self.amx[:, 0:1],
                                            in0=self.amx[:, 0:1],
                                            in1=self.sc[:, 4:5], op=ALU.max),
                                    [em, ev_mx])
            e8 = self.ve(partial(f_copy, out=self.l2tmp[0:rows, :],
                                 in_=st1[0:rows, :]), [e7])
            for fc in range(NFC):
                if halo:
                    self.bank_tr(6, [(self.psB[6][0:128, 0:3],
                                      self.l2tmp[0:3,
                                                 fc * 128:(fc + 1) * 128],
                                      self.c_ident_bf[0:3, 0:3])], [e8])
                    eve = self.bank_r(6, "vector", partial(
                        f_copy, out=self.xnh[:, fc * 4:fc * 4 + 3],
                        in_=self.psB[6][0:128, 0:3]))
                else:
                    self.bank_tr(6, [(self.psB[6][0:128, 0:128],
                                      self.l2tmp[:, fc * 128:(fc + 1) * 128],
                                      self.c_ident_bf[:])], [e8])
                    eve = self.bank_r(6, "vector", partial(
                        f_copy,
                        out=self.arena[:, fc * 2048 + tt * 128:
                                       fc * 2048 + (tt + 1) * 128],
                        in_=self.psB[6][0:128, 0:128]))
                done.append(eve)
        done.append(ev_mx)
        dm = merge(done)
        dm += self.tap("dbg_xnfm", self.arena[:, 0:2048], dm)
        dm += self.tap("dbg_xnf2", self.arena[:, 15 * 2048:16 * 2048], dm)
        return dm

    def ph_sx_ar(self, ev_ln):
        dr = self.dr
        self.bank_tr(7, [(self.psB[7][0:1, 0:128], self.amx[:, 0:1],
                          self.c_ident_f32[:])], ev_ln)
        e2 = self.bank_r(7, "vector", partial(
            f_red, out=self.st1r[0:1, 0:1], in_=self.psB[7][0:1, 0:128],
            op=ALU.max, absval=False))
        el1 = self.ve(partial(f_copy, out=self.mxl[0:1, 0:1],
                              in_=self.st1r[0:1, 0:1]), [e2])
        e3 = self.dma("gpsimd", dr["mx_in"][0:1, 0:1], self.st1r[0:1, 0:1],
                      [e2], sem="d_mx")
        e4 = self.po(partial(f_cc, kind="AllReduce", op=ALU.max,
                             ins=[dr["mx_in"].opt()],
                             outs=[dr["mx_out"].opt()]), [e3], inc="s_ccM")
        e5 = self.dma("gpsimd", self.st1r[0:1, 1:2], dr["mx_out"][0:1, 0:1],
                      [e4], sem="d_mx")
        el2 = self.ve(partial(f_copy, out=self.mxl[0:1, 1:2],
                              in_=self.st1r[0:1, 1:2]), [e5])
        self.bank_w(7, [(self.psB[7][0:128, 0:1], self.c_ones_row_f32[:],
                         self.st1r[0:1, 1:2], True, True)], [e5])
        e9 = self.bank_r(7, "vector", partial(
            f_ts, out=self.qcol[:, 1:2], in0=self.psB[7][0:128, 0:1],
            s1=1.0 / 127.0, s2=1e-8, op0=ALU.mult, op1=ALU.max))
        e10a = self.ac(partial(f_act, out=self.qcol[:, 4:5],
                               in_=self.qcol[:, 1:2], func=AF.Ln), [e9])
        e10 = self.ac(partial(f_act, out=self.qcol[:, 0:1],
                              in_=self.qcol[:, 4:5], func=AF.Exp,
                              scale=-1.0), [e10a])
        out = [e10]
        out += self.tap("dbg_sx", self.qcol[:, :], [e10])
        out += self.tap("dbg_amx", self.amx[:, :], [e10])
        out += self.tap("dbg_st1", self.st1r[0:1, 0:16], [e10])
        out += self.tap("dbg_mxl", self.mxl[0:1, :], [e10])
        out += self.tap("dbg_qlat", self.qlat[:, :], [e10])
        return out

    def ph_wquant(self, ev_c, ev_ln):
        dr = self.dr
        self.wq_slot = 0
        self.wq_dma = [None] * 4
        self.wq_lasteve = None
        self.wq_scst = None
        stA = self.af32[:, 0:2048]
        stB = self.af32[:, 2051:4099]
        done = []
        stage_last = None
        prev = list(ev_ln)

        def quant_2048(st, rec, waits):
            e4 = self.ve(partial(f_ts, out=self.wqi[:, 0:2048], in0=st,
                                 s1=rec.opt(), s2=None, op0=ALU.mult,
                                 op1=ALU.bypass), waits)
            e5 = self.ve(partial(f_copy, out=self.l2tmp[:, 0:2048],
                                 in_=self.wqi[:, 0:2048]),
                         [e4] + ([self.wq_lasteve] if self.wq_lasteve
                                 else []))
            return e4, e5

        for icl in range(8):
            st = stA if icl % 2 == 0 else stB
            evd = self.dma("sync", st, dr["w1_slice"][icl],
                           prev + (list(ev_ln) if icl == 0 else []),
                           sem="d_wv")
            if icl == 0 and "dbg_wst" in self.debug:
                evl = self.ve(partial(f_copy, out=self.af32[:, 5128:6152],
                                      in_=st[:, 0:1024]), [evd])
                self.tap("dbg_wst", self.af32[:, 5128:6152], [evl])
            e1 = self.ve(partial(f_red, out=self.sc[:, 5:6], in_=st,
                                 op=ALU.max, absval=True), [evd],
                         inc="d_vx")
            e2 = self.ve(partial(f_ts, out=self.sc[:, 6:7],
                                 in0=self.sc[:, 5:6], s1=1.0 / 127.0,
                                 s2=1e-8, op0=ALU.mult, op1=ALU.max),
                         [e1] + ([self.wq_scst] if self.wq_scst else []),
                         inc="d_vx")
            e3a = self.ac(partial(f_act, out=self.sc[:, 7:8],
                                  in_=self.sc[:, 6:7], func=AF.Ln), [e2])
            e3 = self.ac(partial(f_act, out=self.sc[:, 5:6],
                                 in_=self.sc[:, 7:8], func=AF.Exp,
                                 scale=-1.0), [e3a])
            evst = self.dma(
                "gpsimd",
                dr["sw1_ag_in"][0:1, icl * 128:(icl + 1) * 128]
                .rearrange("o p -> p o"), self.sc[:, 6:7], [e2], slow=True,
                sem="d_ws")
            done.append(evst)
            self.wq_scst = evst
            if icl == 0 and "dbg_wsc" in self.debug:
                evl2 = self.ve(partial(f_copy, out=self.af32[:, 6152:6160],
                                       in_=self.sc[:, 5:13]), [e2, evst])
                self.tap("dbg_wsc", self.af32[:, 6152:6160], [evl2])
            e4, e5 = quant_2048(st, self.sc[:, 5:6], [e3])
            stage_last = e4
            prev = [e4]
            for fc in range(NFC):
                slot = self.wq_slot % 4
                self.wq_slot += 1
                so = self.ofm[:, slot * 128:(slot + 1) * 128]
                self.bank_tr(6, [(self.psB[6][0:128, 0:128],
                                  self.l2tmp[:, fc * 128:(fc + 1) * 128],
                                  self.c_ident_bf[:])], [e5] + ev_c)
                eve = self.bank_r(6, "scalar", partial(
                    f_copy, out=so, in_=self.psB[6][0:128, 0:128]),
                    [self.wq_dma[slot]] if self.wq_dma[slot] else [])
                self.wq_dma[slot] = self.dma(
                    "gpsimd",
                    dr["qw1_ag_in"][icl, :, fc * 128:(fc + 1) * 128],
                    so, [eve], sem="d_t%d" % slot)
                done.append(self.wq_dma[slot])
                self.wq_lasteve = eve
        for ocl in range(2):
            mx = []
            for q in range(4):
                st = stA if q % 2 == 0 else stB
                evd = self.dma("sync", st,
                               dr["w2_slice"][ocl][:, q * 2048:(q + 1) * 2048],
                               prev, sem="d_wv")
                e1 = self.ve(partial(f_red, out=self.sc[:, 8 + q:9 + q],
                                     in_=st, op=ALU.max, absval=True), [evd],
                             inc="d_vx")
                prev = [e1]
                mx.append(e1)
            em1 = self.ve(partial(f_tt, out=self.sc[:, 12:13],
                                  in0=self.sc[:, 8:9], in1=self.sc[:, 9:10],
                                  op=ALU.max), mx)
            em2 = self.ve(partial(f_tt, out=self.sc[:, 13:14],
                                  in0=self.sc[:, 10:11],
                                  in1=self.sc[:, 11:12], op=ALU.max), [em1])
            em3 = self.ve(partial(f_tt, out=self.sc[:, 12:13],
                                  in0=self.sc[:, 12:13],
                                  in1=self.sc[:, 13:14], op=ALU.max), [em2])
            e2 = self.ve(partial(f_ts, out=self.sc[:, 14:15],
                                 in0=self.sc[:, 12:13], s1=1.0 / 127.0,
                                 s2=1e-8, op0=ALU.mult, op1=ALU.max),
                         [em3] + ([self.wq_scst] if self.wq_scst else []),
                         inc="d_vx")
            e3a = self.ac(partial(f_act, out=self.sc[:, 15:16],
                                  in_=self.sc[:, 14:15], func=AF.Ln), [e2])
            e3 = self.ac(partial(f_act, out=self.sc[:, 13:14],
                                 in_=self.sc[:, 15:16], func=AF.Exp,
                                 scale=-1.0), [e3a])
            evst = self.dma(
                "gpsimd",
                dr["sw2_ag_in"][0:1, ocl * 128:(ocl + 1) * 128]
                .rearrange("o p -> p o"), self.sc[:, 14:15], [e2], slow=True,
                sem="d_ws")
            done.append(evst)
            self.wq_scst = evst
            for q in range(4):
                st = stA if q % 2 == 0 else stB
                evd = self.dma("sync", st,
                               dr["w2_slice"][ocl][:, q * 2048:(q + 1) * 2048],
                               prev, sem="d_wv")
                e4, e5 = quant_2048(st, self.sc[:, 13:14], [evd, e3])
                stage_last = e4
                prev = [e4]
                for t16 in range(16):
                    ic = q * 16 + t16
                    slot = self.wq_slot % 4
                    self.wq_slot += 1
                    so = self.ofm[:, slot * 128:(slot + 1) * 128]
                    self.bank_tr(6, [(self.psB[6][0:128, 0:128],
                                      self.l2tmp[:, t16 * 128:(t16 + 1) * 128],
                                      self.c_ident_bf[:])], [e5])
                    eve = self.bank_r(6, "scalar", partial(
                        f_copy, out=so, in_=self.psB[6][0:128, 0:128]),
                        [self.wq_dma[slot]] if self.wq_dma[slot] else [])
                    self.wq_dma[slot] = self.dma(
                        "gpsimd",
                        dr["qw2_ag_in"][ocl, :, ic * 128:(ic + 1) * 128],
                        so, [eve], sem="d_t%d" % slot)
                    done.append(self.wq_dma[slot])
                    self.wq_lasteve = eve

        a1 = self.po(partial(f_cc, kind="AllGather", op=ALU.bypass,
                             ins=[dr["qw1_ag_in"].opt()],
                             outs=[dr["qw1_ag_out"].opt()]), done, inc="s_ccW")
        a2 = self.po(partial(f_cc, kind="AllGather", op=ALU.bypass,
                             ins=[dr["sw1_ag_in"].opt()],
                             outs=[dr["sw1_ag_out"].opt()]), done, inc="s_ccW")
        a3 = self.po(partial(f_cc, kind="AllGather", op=ALU.bypass,
                             ins=[dr["qw2_ag_in"].opt()],
                             outs=[dr["qw2_ag_out"].opt()]), done, inc="s_ccW")
        a4 = self.po(partial(f_cc, kind="AllGather", op=ALU.bypass,
                             ins=[dr["sw2_ag_in"].opt()],
                             outs=[dr["sw2_ag_out"].opt()]), done, inc="s_ccW")
        e1 = self.dma("sync", self.sw1sb[:, :],
                      dr["sw1_ag_out"].rearrange("r (i p) -> p (r i)", p=128),
                      [a4], slow=True, sem="d_sw")
        e2 = self.dma("sync", self.sw2sb[:, :],
                      dr["sw2_ag_out"].rearrange("r (i p) -> p (r i)", p=128),
                      [a4], slow=True, sem="d_sw")
        return {"qw1": a4, "sw1": e1, "qw2": a4, "sw2": e2}, [stage_last]

    def ph_beta_g(self, ev_c, ev_ln):
        dr = self.dr
        evb = self.dma("gpsimd",
                       self.wb2[:, 0:NFC * 16].rearrange(
                           "p (fc o) -> p fc o", o=16),
                       dr["Wb"].rearrange("(fc p) o -> p fc o", p=128),
                       ev_ln)
        eva = self.dma("gpsimd",
                       self.wb2[:, NFC * 16:NFC * 24].rearrange(
                           "p (fc o) -> p fc o", o=8),
                       dr["Wa"].rearrange("(fc p) o -> p fc o", p=128),
                       ev_ln)
        ea = self.ac(partial(f_act, out=self.bsc[0:8, 0:1],
                             in_=self.c_A_log_c[0:8, 0:1], func=AF.Exp),
                     ev_c)
        enA = self.ve(partial(f_ts, out=self.bsc[0:8, 1:2],
                              in0=self.bsc[0:8, 0:1], s1=-1.0, s2=None,
                              op0=ALU.mult, op1=ALU.bypass), [ea])
        g_evs = []
        for tt4 in range(4):
            t0 = tt4 * 512
            mmsb = [(self.psB[0][0:16, 0:512],
                     self.wb2[:, fc * 16:(fc + 1) * 16],
                     self.xn_rhs(fc, t0, 512), fc == 0, fc == NFC - 1)
                    for fc in range(NFC)]
            self.bank_w(0, mmsb, [evb] + ev_ln)
            e1 = self.bank_r(0, "scalar", partial(
                f_act, out=self.bagg[0:16, t0:t0 + 512],
                in_=self.psB[0][0:16, 0:512], func=AF.Sigmoid))
            eb = self.ve(partial(f_ts, out=self.bagg[0:16, t0:t0 + 512],
                                 in0=self.bagg[0:16, t0:t0 + 512], s1=2.0,
                                 s2=None, op0=ALU.mult, op1=ALU.bypass),
                         [e1])
            mmsa = [(self.psB[1][0:8, 0:512],
                     self.wb2[:, NFC * 16 + fc * 8:NFC * 16 + (fc + 1) * 8],
                     self.xn_rhs(fc, t0, 512), fc == 0, fc == NFC - 1)
                    for fc in range(NFC)]
            self.bank_w(1, mmsa, [eva] + ev_ln)
            e2 = self.bank_r(1, "scalar", partial(
                f_act, out=self.gfm[0:8, t0:t0 + 512],
                in_=self.psB[1][0:8, 0:512], func=AF.Exp,
                bias=self.c_dt_bias_c[0:8, 0:1], scale=1.0), ev_c)
            e3 = self.ac(partial(f_act, out=self.Gfm[0:8, t0:t0 + 512],
                                 in_=self.gfm[0:8, t0:t0 + 512], func=AF.Ln,
                                 bias=1.0, scale=1.0), [e2])
            e4 = self.ve(partial(f_ts, out=self.gfm[0:8, t0:t0 + 512],
                                 in0=self.Gfm[0:8, t0:t0 + 512],
                                 s1=self.bsc[0:8, 1:2].opt(), s2=None,
                                 op0=ALU.mult, op1=ALU.bypass), [e3, enA])
            g_evs.append((eb, e4))
        out_evs = []
        for cch in range(NCH):
            sl = slice(cch * C, (cch + 1) * C)
            eb, e4 = g_evs[cch // 8]
            e1 = self.ve(partial(f_scan, out=self.Gfm[0:8, sl],
                                 d0=self.gfm[0:8, sl], d1=self.gfm[0:8, sl],
                                 init=0.0, op0=ALU.add, op1=ALU.bypass),
                         [e4], inc="d_vx")
            self.bank_tr(7, [(self.psB[7][0:64, 0:8], self.Gfm[0:8, sl],
                              self.c_ident_f32[0:8, 0:8])], [e1])
            e2 = self.bank_r(7, "vector", partial(
                f_copy, out=self.Grows[0:64, cch * 8:(cch + 1) * 8],
                in_=self.psB[7][0:64, 0:8]))
            e2b = self.dma("sync", self.Grows[64:128, cch * 8:(cch + 1) * 8],
                           self.Grows[0:64, cch * 8:(cch + 1) * 8], [e2])
            e3 = self.ve(partial(f_ts,
                                 out=self.nGrows[:, cch * 8:(cch + 1) * 8],
                                 in0=self.Grows[:, cch * 8:(cch + 1) * 8],
                                 s1=-1.0, s2=None, op0=ALU.mult,
                                 op1=ALU.bypass), [e2b])
            e3b = self.ac(partial(f_act,
                                  out=self.eGrows[:, cch * 8:(cch + 1) * 8],
                                  in_=self.Grows[:, cch * 8:(cch + 1) * 8],
                                  func=AF.Exp), [e2b])
            self.bank_tr(7, [(self.psB[7][0:64, 16:32], self.bagg[0:16, sl],
                              self.c_ident_f32[0:16, 0:16])], [eb])
            e5 = self.bank_r(7, "vector", partial(
                f_copy, out=self.Brows[0:64, cch * 8:(cch + 1) * 8],
                in_=self.psB[7][0:64, 16:24]))
            e5c = self.ve(partial(
                f_copy, out=self.sc[0:64, 10:18][:, 0:8]
                if False else self.Brows[0:64, cch * 8:(cch + 1) * 8],
                in_=self.psB[7][0:64, 16:24]), []) if False else None
            e5b0 = self.bank_r(7, "vector", partial(
                f_copy, out=self.l2tmp[0:64, 0:8].bitcast(F32)
                if False else self.af32[0:64, 9160:9168],
                in_=self.psB[7][0:64, 24:32]))
            e5b = self.dma("sync", self.Brows[64:128, cch * 8:(cch + 1) * 8],
                           self.af32[0:64, 9160:9168], [e5b0])
            out_evs += [e3, e3b, e5, e5b]
        dm = merge(out_evs)
        dm += self.tap("dbg_bagg", self.bagg[0:16, :], dm)
        dm += self.tap("dbg_G", self.Gfm[:, :], dm)
        dm += self.tap("dbg_Br", self.Brows[:, :], dm)
        dm += self.tap("dbg_Gr", self.Grows[:, :], dm)
        return dm

    def ph_heads(self, ev_c, ev_bg, ev_stage):
        dr = self.dr
        done = []
        prev_mach = list(ev_bg) + list(ev_stage)
        for h in range(H):
            par = h % 2
            wsl = self.wb2[:, par * 2048:(par + 1) * 2048]
            specs = [("Wq", h * 128), ("Wk", h * 128), ("Wk", KD + h * 128),
                     ("Wv", h * 128), ("Wv", KD + h * 128)]
            oc_global = [h, 8 + h, 16 + h, 24 + h, 32 + h]
            pc_ev = {}
            wprev = prev_mach
            for wi, (wn, c0) in enumerate(specs):
                evw = self.dma("gpsimd", wsl.rearrange(
                    "p (fc o) -> p fc o", o=128),
                    dr[wn].rearrange("(fc p) o -> p fc o", p=128)[
                        :, :, c0:c0 + 128], wprev)
                pre = self.af32[:, 0:2051] if wi % 2 == 0 else \
                    self.af32[:, 2051:4102]
                pre_evs = []
                for tt4 in range(4):
                    bk = tt4 % 2
                    mms = [(self.psB[bk][0:128, 0:512],
                            wsl[:, fc * 128:(fc + 1) * 128],
                            self.xn_rhs(fc, tt4 * 512, 512),
                            fc == 0, fc == NFC - 1) for fc in range(NFC)]
                    self.bank_w(bk, mms,
                                [evw] + (prev_mach if wi < 2 else []))
                    eng = "scalar" if tt4 % 2 == 0 else "vector"
                    pre_evs.append(self.bank_r(bk, eng, partial(
                        f_copy,
                        out=pre[:, 3 + tt4 * 512:3 + (tt4 + 1) * 512],
                        in_=self.psB[bk][0:128, 0:512])))
                mmsh = [(self.psB[2][0:128, 0:3],
                         wsl[:, fc * 128:(fc + 1) * 128],
                         self.xnh[:, fc * 4:fc * 4 + 3],
                         fc == 0, fc == NFC - 1) for fc in range(NFC)]
                self.bank_w(2, mmsh, [evw])
                pre_evs.append(self.bank_r(2, "vector", partial(
                    f_copy, out=pre[:, 0:3], in_=self.psB[2][0:128, 0:3])))
                wprev = pre_evs
                ocg = oc_global[wi]
                cw = self.convw
                cvA = self.af32[:, 4104:6152]
                cvB = self.af32[:, 6152:8200]
                e1 = self.ve(partial(
                    f_ts, out=cvA[:, 0:2048], in0=pre[:, 0:2048],
                    s1=cw[:, ocg * CONV:ocg * CONV + 1].opt(), s2=None,
                    op0=ALU.mult, op1=ALU.bypass), pre_evs)
                e2 = self.ve(partial(
                    f_stt, out=cvB[:, 0:2048], in0=pre[:, 1:2049],
                    s=cw[:, ocg * CONV + 1:ocg * CONV + 2].opt(),
                    in1=cvA[:, 0:2048], op0=ALU.mult, op1=ALU.add), [e1])
                e3 = self.ve(partial(
                    f_stt, out=cvA[:, 0:2048], in0=pre[:, 2:2050],
                    s=cw[:, ocg * CONV + 2:ocg * CONV + 3].opt(),
                    in1=cvB[:, 0:2048], op0=ALU.mult, op1=ALU.add), [e2])
                e4 = self.ve(partial(
                    f_stt, out=cvB[:, 0:2048], in0=pre[:, 3:2051],
                    s=cw[:, ocg * CONV + 3:ocg * CONV + 4].opt(),
                    in1=cvA[:, 0:2048], op0=ALU.mult, op1=ALU.add), [e3])
                e5 = self.ac(partial(f_act, out=cvA[:, 0:2048],
                                     in_=cvB[:, 0:2048], func=AF.Silu),
                             [e4])
                if wi >= 3:
                    j = wi - 3
                    dst = self.vt.ap().rearrange(
                        "p (cc two t) -> p cc two t", two=2, t=64)[:, :, j, :]
                    pc_ev[wi] = [self.ve(partial(
                        f_copy, out=dst,
                        in_=cvA[:, 0:2048].rearrange(
                            "p (cc t) -> p cc t", t=64)), [e5])]
                else:
                    evs_l2 = []
                    for tt4 in range(4):
                        t0 = tt4 * 512
                        sq = self.ve(partial(
                            f_tt, out=self.l2tmp[:, t0:t0 + 512],
                            in0=cvA[:, t0:t0 + 512],
                            in1=cvA[:, t0:t0 + 512], op=ALU.mult), [e5])
                        self.bank_w(5, [(self.psB[5][0:1, 0:512],
                                         self.c_ones_col_bf[:],
                                         self.l2tmp[:, t0:t0 + 512],
                                         True, True)], [sq])
                        n1 = self.bank_r(5, "vector", partial(
                            f_ts, out=self.st1r[0:1, 0:512],
                            in_=None, in0=self.psB[5][0:1, 0:512],
                            s1=1e-30, s2=None, op0=ALU.max,
                            op1=ALU.bypass) if False else partial(
                            f_ts, out=self.st1r[0:1, 0:512],
                            in0=self.psB[5][0:1, 0:512], s1=1e-30, s2=None,
                            op0=ALU.max, op1=ALU.bypass))
                        n2 = self.ac(partial(f_act,
                                             out=self.st1r[0:1, 0:512],
                                             in_=self.st1r[0:1, 0:512],
                                             func=AF.Ln), [n1])
                        n2b = self.ac(partial(f_act,
                                              out=self.st2r[0:1, 0:512],
                                              in_=self.st1r[0:1, 0:512],
                                              func=AF.Exp, scale=-0.5),
                                      [n2])
                        n3 = self.ve(partial(f_ts,
                                             out=self.st2r[0:1, 0:512],
                                             in0=self.st2r[0:1, 0:512],
                                             s1=1e12, s2=None, op0=ALU.min,
                                             op1=ALU.bypass), [n2b])
                        if wi == 0:
                            n3 = self.ve(partial(
                                f_ts, out=self.st2r[0:1, 0:512],
                                in0=self.st2r[0:1, 0:512], s1=ISQD, s2=None,
                                op0=ALU.mult, op1=ALU.bypass), [n3])
                        n4 = self.ve(partial(f_copy, out=self.stb[0:1, 0:512],
                                             in_=self.st2r[0:1, 0:512]),
                                     [n3])
                        self.bank_w(4, [(self.psB[4][0:128, 0:512],
                                         self.c_ones_row_bf[:],
                                         self.stb[0:1, 0:512], True, True)],
                                    [n4])
                        if wi == 0:
                            dst = self.qt[:, t0:t0 + 512]
                        else:
                            j = wi - 1
                            dst = self.kt.ap().rearrange(
                                "p (cc two t) -> p cc two t", two=2, t=64)[
                                :, 8 * tt4:8 * tt4 + 8, j, :]
                        if wi == 0:
                            evs_l2.append(self.bank_r(4, "vector", partial(
                                f_tt, out=dst, in0=cvA[:, t0:t0 + 512],
                                in1=self.psB[4][0:128, 0:512],
                                op=ALU.mult)))
                        else:
                            evs_l2.append(self.bank_r(4, "vector", partial(
                                f_tt, out=dst,
                                in0=cvA[:, t0:t0 + 512].rearrange(
                                    "p (cc t) -> p cc t", t=64),
                                in1=self.psB[4][0:128, 0:512].rearrange(
                                    "p (cc t) -> p cc t", t=64),
                                op=ALU.mult)))
                    pc_ev[wi] = evs_l2
            evwg = self.dma("gpsimd", wsl.rearrange(
                "p (fc o) -> p fc o", o=128),
                dr["Wg"].rearrange("(fc p) o -> p fc o", p=128)[
                    :, :, h * 128:(h + 1) * 128], wprev)
            gate_done = []
            for tcq in range(NTC):
                bk = tcq % 2
                mms = [(self.psB[bk][0:128, 0:128],
                        self.xn_rhs(fc, tcq * 128, 128),
                        wsl[:, fc * 128:(fc + 1) * 128],
                        fc == 0, fc == NFC - 1) for fc in range(NFC)]
                self.bank_w(bk, mms, [evwg])
                gst = self.af32[:, 4104 + bk * 128:4104 + bk * 128 + 128]
                e1 = self.bank_r(bk, "vector", partial(
                    f_copy, out=gst, in_=self.psB[bk][0:128, 0:128]),
                    [gate_done[-2]] if len(gate_done) >= 2 else [])
                gate_done.append(self.dma(
                    "scalar",
                    dr["gate_spill"][tcq, :, h * 128:(h + 1) * 128], gst,
                    [e1]))
            wprev = gate_done[-2:]
            ev_in = merge(pc_ev[0], pc_ev[1], pc_ev[2], pc_ev[3], pc_ev[4],
                          ev_bg, ev_c)
            if h == 0:
                ev_in += self.tap("dbg_qt", self.qt[:, :], ev_in)
                ev_in += self.tap("dbg_kt", self.kt[:, :], ev_in)
                ev_in += self.tap("dbg_vt", self.vt[:, :], ev_in)
            prev_mach = merge(self.machinery_head(h, ev_in), gate_done)
            done += prev_mach
        dm = merge(done)
        dm += self.tap("dbg_S0", self.S0[:, :], dm)
        dm += self.tap("dbg_Phi", self.Phi[:, :], dm)
        dm += self.tap("dbg_gate", self.dr["gate_spill"][0], dm)
        return dm

    def machinery_head(self, h, ev_in):
        dr = self.dr
        mbv = self.mb
        prev_tail = list(ev_in)
        out_evs = []
        spill_prev = [None, None]
        scr_prev = [None, None]
        for cch in range(NCH):
            par = cch % 2
            MA = 8200 if par == 0 else 9224
            mo = par * 1472

            def ma(a, b, MA=MA):
                return self.af32[:, MA + a:MA + b]

            def mbo(a, b, mo=mo):
                return mbv[:, mo + a:mo + b]

            scc = 16 + par * 5
            gr = self.Grows[:, cch * 8 + h:cch * 8 + h + 1]
            ngr = self.nGrows[:, cch * 8 + h:cch * 8 + h + 1]
            br = self.Brows[:, cch * 8 + h:cch * 8 + h + 1]
            egr = self.eGrows[:, cch * 8 + h:cch * 8 + h + 1]
            ktc = self.kt[:, cch * 128:(cch + 1) * 128]
            qtc = self.qt[:, cch * 64:(cch + 1) * 64]
            S0s = self.S0[:, h * 128:(h + 1) * 128]
            Phs = self.Phi[:, h * 128:(h + 1) * 128]
            w0 = ev_in if cch < 2 else scr_prev[par]
            yv = ma(0, 128)
            e1 = self.ve(partial(f_ts, out=yv, in0=self.c_ones_f32[:],
                                 s1=gr.opt(), s2=None, op0=ALU.mult,
                                 op1=ALU.bypass), w0)
            self.bank_tr(7, [(self.psB[7][0:128, 0:128], yv,
                              self.c_ident_f32[:])], [e1])
            pre2 = ma(0, 128)
            ev_pr2 = self.bank_r(7, "vector", partial(
                f_ts, out=pre2, in0=self.psB[7][0:128, 0:128],
                s1=gr.opt(), s2=0.0, op0=ALU.subtract, op1=ALU.max))
            e1n = ma(128, 256)
            ev_e1n = self.ac(partial(f_act, out=e1n, in_=pre2, func=AF.Exp,
                                     scale=-1.0), [ev_pr2])
            pre1 = ma(256, 320)
            ev_pr1 = self.bank_r(7, "vector", partial(
                f_ts, out=pre1, in0=self.psB[7][0:128, 0:64],
                s1=gr.opt(), s2=0.0, op0=ALU.subtract, op1=ALU.min))
            e1p = pre1
            ev_e1p = self.ac(partial(f_act, out=e1p, in_=pre1, func=AF.Exp),
                             [ev_pr1])
            lamq = ma(320, 384)
            ev_lq = self.bank_r(7, "scalar", partial(
                f_act, out=lamq, in_=self.psB[7][0:128, 0:64], func=AF.Exp))
            g63 = self.sc[:, scc:scc + 1]
            ev_g63 = self.bank_r(7, "vector", partial(
                f_copy, out=g63, in_=self.psB[7][0:128, 63:64]))
            khsc = self.sc[:, scc + 1:scc + 2]
            ev_kh = self.ac(partial(f_act, out=khsc, in_=gr, func=AF.Exp,
                                    bias=g63, scale=-1.0), [ev_g63])
            lend = self.sc[:, scc + 2:scc + 3]
            ev_le = self.ac(partial(f_act, out=lend, in_=g63, func=AF.Exp),
                            [ev_g63])
            self.bank_w(0, [(self.psB[0][0:128, 0:128], ktc, ktc,
                             True, True)], w0)
            e1nm = ma(384, 512)
            ev_m = self.ve(partial(f_tt, out=e1nm, in0=e1n,
                                   in1=self.c_amask[:], op=ALU.mult),
                           [ev_e1n])
            nbr = self.sc[:, scc + 3:scc + 4]
            ev_nb = self.ve(partial(f_ts, out=nbr, in0=br, s1=-1.0, s2=None,
                                    op0=ALU.mult, op1=ALU.bypass), w0)
            if h == 0 and cch == 0 and "dbg_P" in self.debug:
                evdbg = self.bank_r(0, "vector", partial(
                    f_copy, out=self.xtok[:, 0:128],
                    in_=self.psB[0][0:128, 0:128]))
                self.tap("dbg_P", self.xtok[:, 0:128], [evdbg])
            nab = mbo(1024, 1152)
            ev_nab = self.bank_r(0, "vector", partial(
                f_stt, out=nab, in0=self.psB[0][0:128, 0:128], s=nbr.opt(),
                in1=e1nm, op0=ALU.mult, op1=ALU.mult), [ev_m, ev_nb])
            if h == 0 and cch == 0 and "dbg_naf" in self.debug:
                ev2dbg = self.ve(partial(f_copy, out=self.xtok[:, 128:256],
                                         in_=nab), [ev_nab])
                self.tap("dbg_naf", self.xtok[:, 128:256], [ev2dbg])
            ppow = mbo(640, 768)
            ev_pp = self.ve(partial(f_copy, out=ppow, in_=nab), [ev_nab])
            self.bank_tr(6, [(self.psB[6][0:128, 0:128], ppow,
                              self.c_ident_bf[:])], [ev_pp])
            ppowT = mbo(768, 896)
            evT = self.bank_r(6, "scalar", partial(
                f_copy, out=ppowT, in_=self.psB[6][0:128, 0:128]))
            gacc = mbo(896, 1024)
            evG = self.ve(partial(f_tt, out=gacc, in0=ppowT,
                                  in1=self.c_ident_bf[:], op=ALU.add),
                          [evT])
            evTn = evT
            for j in range(6):
                self.bank_w(1, [(self.psB[1][0:128, 0:128], ppowT, ppow,
                                 True, True)], [evG, evTn])
                evPn = self.bank_r(1, "vector", partial(
                    f_copy, out=ppow, in_=self.psB[1][0:128, 0:128]))
                if j < 5:
                    self.bank_tr(6, [(self.psB[6][0:128, 0:128], ppow,
                                      self.c_ident_bf[:])], [evPn])
                    evTn = self.bank_r(6, "scalar", partial(
                        f_copy, out=ppowT, in_=self.psB[6][0:128, 0:128]))
                self.bank_w(2, [(self.psB[2][0:128, 0:128], ppow, gacc,
                                 True, True)], [evPn])
                evG = self.bank_r(2, "vector", partial(
                    f_tt, out=gacc, in0=gacc,
                    in1=self.psB[2][0:128, 0:128], op=ALU.add))
            X = mbo(1216, 1472)
            vch = self.vt[:, cch * 128:(cch + 1) * 128]
            kch = self.kt[:, cch * 128:(cch + 1) * 128]
            self.bank_tr(6, [(self.psB[6][0:128, 0:128], vch,
                              self.c_ident_bf[:])], w0 + [evG])
            evxv = self.bank_r(6, "vector", partial(
                f_ts, out=X[:, 0:128], in0=self.psB[6][0:128, 0:128],
                s1=br.opt(), s2=None, op0=ALU.mult, op1=ALU.bypass))
            bl = self.sc[:, scc + 4:scc + 5]
            ev_bl = self.ve(partial(f_tt, out=bl, in0=br, in1=egr,
                                    op=ALU.mult), w0)
            self.bank_tr(6, [(self.psB[6][0:128, 0:128], kch,
                              self.c_ident_bf[:])], [evxv])
            evxk = self.bank_r(6, "vector", partial(
                f_ts, out=X[:, 128:256], in0=self.psB[6][0:128, 0:128],
                s1=bl.opt(), s2=None, op0=ALU.mult, op1=ALU.bypass),
                [ev_bl])
            khat = mbo(256, 384)
            evkh = self.bank_r(6, "vector", partial(
                f_ts, out=khat, in0=self.psB[6][0:128, 0:128],
                s1=khsc.opt(), s2=None, op0=ALU.mult, op1=ALU.bypass),
                [ev_kh])
            self.bank_w(3, [(self.psB[3][0:128, 0:256], gacc, X,
                             True, True)], [evxv, evxk, evG])
            U = mbo(0, 256)
            evUe = self.bank_r(3, "vector", partial(
                f_copy, out=U, in_=self.psB[3][0:128, 0:256]))
            Ul, W = U[:, 0:128], U[:, 128:256]
            self.bank_w(4, [(self.psB[4][0:128, 0:64], ktc, qtc,
                             True, True)], w0 + [evG])
            e1q = ma(640, 704)
            ev_eq = self.ve(partial(f_tt, out=e1q, in0=e1p,
                                    in1=self.c_mmaskT[:, 0:64],
                                    op=ALU.mult), [ev_e1p])
            mtt = mbo(512, 576)
            ev_mt = self.bank_r(4, "vector", partial(
                f_tt, out=mtt, in0=self.psB[4][0:128, 0:64], in1=e1q,
                op=ALU.mult), [ev_eq])
            qlam = mbo(1152, 1216)
            ev_ql = self.ve(partial(f_tt, out=qlam, in0=qtc, in1=lamq,
                                    op=ALU.mult), [ev_lq])
            self.bank_w(4, [(self.psB[4][0:128, 0:64], W, mtt,
                             True, True)], [evUe, ev_mt])
            Et = mbo(576, 640)
            ev_et = self.bank_r(4, "vector", partial(
                f_tt, out=Et, in0=qlam, in1=self.psB[4][0:128, 0:64],
                op=ALU.subtract), [ev_ql])
            self.bank_w(5, [(self.psB[5][0:128, 0:128], W, khat,
                             True, True)], [evUe, evkh])
            diagl = ma(704, 832)
            ev_dg = self.ve(partial(f_ts, out=diagl,
                                    in0=self.c_ident_f32[:],
                                    s1=lend.opt(), s2=None, op0=ALU.mult,
                                    op1=ALU.bypass), [ev_le])
            Ft = mbo(384, 512)
            ev_ft = self.bank_r(5, "vector", partial(
                f_tt, out=Ft, in0=diagl, in1=self.psB[5][0:128, 0:128],
                op=ALU.subtract), [ev_dg])
            self.bank_w(0, [(self.psB[0][0:128, 0:128], khat, Ul,
                             True, True)], [evUe, evkh])
            sloc = ma(832, 960)
            ev_sl = self.bank_r(0, "vector", partial(
                f_copy, out=sloc, in_=self.psB[0][0:128, 0:128]))
            self.bank_w(1, [
                (self.psB[1][0:64, 0:128], mtt, Ul, True, False),
                (self.psB[1][0:64, 0:128], Et, S0s, False, True)],
                [evUe, ev_mt, ev_et] + prev_tail)
            sp = spill_prev[par]
            ev_ope = self.bank_r(1, "scalar", partial(
                f_copy, out=self.opre_st[0:64, par * 128:(par + 1) * 128],
                in_=self.psB[1][0:64, 0:128]), [sp] if sp else [])
            self.bank_w(2, [(self.psB[2][0:128, 0:64], Phs, Et,
                             True, True)], [ev_et] + prev_tail)
            ev_epe = self.bank_r(2, "scalar", partial(
                f_copy, out=self.ephi_st[:, par * 64:(par + 1) * 64],
                in_=self.psB[2][0:128, 0:64]), [sp] if sp else [])
            ev_sp1 = self.dma("scalar",
                              dr["opre_spill"][cch][:, h * 128:(h + 1) * 128],
                              self.opre_st[0:64, par * 128:(par + 1) * 128],
                              [ev_ope], sem="d_ms")
            ev_sp2 = self.dma("scalar",
                              dr["ephi_spill"][cch][:, h * 64:(h + 1) * 64],
                              self.ephi_st[:, par * 64:(par + 1) * 64],
                              [ev_epe], sem="d_ms")
            spill_prev[par] = merge(ev_sp1, ev_sp2)[0]
            self.bank_w(5, [
                (self.psB[5][0:128, 0:128], Ft, S0s, True, True),
                (self.psB[5][0:128, 128:256], Ft, Phs, True, True)],
                [ev_ft])
            ev_s0 = self.bank_r(5, "vector", partial(
                f_tt, out=S0s, in0=self.psB[5][0:128, 0:128], in1=sloc,
                op=ALU.add), [ev_sl])
            ev_ph = self.bank_r(5, "vector", partial(
                f_copy, out=Phs, in_=self.psB[5][0:128, 128:256]))
            prev_tail = [ev_s0, ev_ph]
            scr_prev[par] = merge(ev_s0, ev_ph, ev_ope, ev_epe)
            if h == 0 and cch == 0:
                prev_tail = prev_tail + \
                    self.tap("dbg_mach", mbv[:, 0:1472],
                             [ev_s0, ev_ph, ev_ope, ev_epe]) + \
                    self.tap("dbg_maf", self.af32[:, 8200:9160],
                             [ev_s0, ev_ph, ev_ope, ev_epe])
            out_evs += [ev_sp1, ev_sp2]
        out_evs += prev_tail
        return merge(out_evs)

    def ph_chain_ag(self, ev_at):
        dr = self.dr
        evs = []
        for h in range(H):
            self.bank_tr(6, [(self.psB[6][0:128, 0:128],
                              self.Phi[:, h * 128:(h + 1) * 128],
                              self.c_ident_bf[:])], ev_at)
            evs.append(self.bank_r(6, "vector", partial(
                f_copy, out=self.pgath[:, h * 128:(h + 1) * 128],
                in_=self.psB[6][0:128, 0:128])))
        evs.append(self.ve(partial(f_copy, out=self.sgath[:, 0:1024],
                                   in_=self.S0[:, :]), ev_at))
        e2 = self.dma("gpsimd", dr["phi_ag_in"][:, :], self.pgath[:, :],
                      evs, sem="d_ag")
        e3 = self.dma("gpsimd", dr["s0_ag_in"][:, :], self.sgath[:, 0:1024],
                      evs, sem="d_ag")
        a1 = self.po(partial(f_cc, kind="AllGather", op=ALU.bypass,
                             ins=[dr["phi_ag_in"].opt()],
                             outs=[dr["phi_ag_out"].opt()]), [e2, e3],
                     inc="s_ccC")
        a2 = self.po(partial(f_cc, kind="AllGather", op=ALU.bypass,
                             ins=[dr["s0_ag_in"].opt()],
                             outs=[dr["s0_ag_out"].opt()]), [], inc="s_ccC")
        return [a1, a2]

    def ph_combine(self, ev_ch):
        dr = self.dr
        prev = list(ev_ch)
        last_h = [None] * H
        for j in range(7):
            evp = self.dma("sync", self.pgath[:, :],
                           dr["phi_ag_out"][j * 128:(j + 1) * 128, :], prev,
                           sem="d_cb")
            evsg = self.dma("sync", self.sgath[:, :],
                            dr["s0_ag_out"][j * 128:(j + 1) * 128, :], prev,
                            sem="d_cb")
            prev = []
            for h in range(H):
                hp = h % 2
                ab = self.mb[:, hp * 1472:hp * 1472 + 128]
                e1 = self.ve(partial(
                    f_ts, out=ab, in0=self.pgath[:, h * 128:(h + 1) * 128],
                    s1=self.mrep[:, j:j + 1].opt(), s2=None, op0=ALU.mult,
                    op1=ALU.bypass),
                    [evp] + ([last_h[h]] if last_h[h] else []))
                e2 = self.ve(partial(
                    f_stt, out=ab, in0=self.c_ident_f32[:],
                    s=self.mrep1[:, j:j + 1].opt(), in1=ab, op0=ALU.mult,
                    op1=ALU.add), [e1])
                bk = 3 + hp
                self.bank_w(bk, [(self.psB[bk][0:128, 0:128], ab,
                                  self.Sin[:, h * 128:(h + 1) * 128],
                                  True, True)],
                            [e2] + ([last_h[h]] if last_h[h] else []))
                ms = self.af32[:, 8200 + 128 * hp:8328 + 128 * hp]
                e3 = self.ve(partial(
                    f_ts, out=ms, in0=self.sgath[:, h * 128:(h + 1) * 128],
                    s1=self.mrep[:, j:j + 1].opt(), s2=None, op0=ALU.mult,
                    op1=ALU.bypass), [evsg])
                e4 = self.bank_r(bk, "vector", partial(
                    f_tt, out=self.Sin[:, h * 128:(h + 1) * 128],
                    in0=self.psB[bk][0:128, 0:128], in1=ms, op=ALU.add),
                    [e3])
                last_h[h] = e4
                prev.append(e4)
        dm = merge(prev)
        dm += self.tap("dbg_sin", self.Sin[:, :], dm)
        return dm

    def ph_qxn(self, ev_mx, ev_at, ev_gt):
        dr = self.dr
        done = []
        st1 = self.af32[:, 4104:6152]
        first_w = merge(ev_at, ev_gt)
        for tc in range(NTC):
            evd = self.dma("sync", self.xtok[:, :],
                           dr["x_slice"][tc * 128:(tc + 1) * 128, :],
                           first_w if tc == 0 else done[-NFC:],
                           sem="d_qx")
            e1 = self.ve(partial(f_ts, out=st1, in0=self.xtok[:, :],
                                 s1=self.ln_m[:, tc:tc + 1].opt(),
                                 s2=self.ln_r[:, tc:tc + 1].opt(),
                                 op0=ALU.add, op1=ALU.mult),
                         [evd] + (first_w if tc == 0 else []))
            e2 = self.ve(partial(f_ts, out=self.wqi[:, 0:2048], in0=st1,
                                 s1=self.qcol[:, 0:1].opt(), s2=None,
                                 op0=ALU.mult, op1=ALU.bypass),
                         [e1] + ev_mx)

            e3 = self.ve(partial(f_copy, out=self.l2tmp[:, 0:2048],
                                 in_=self.wqi[:, 0:2048]), [e2])
            for fc in range(NFC):
                self.bank_tr(6, [(self.psB[6][0:128, 0:128],
                                  self.l2tmp[:, fc * 128:(fc + 1) * 128],
                                  self.c_ident_bf[:])], [e3])
                done.append(self.bank_r(6, "vector", partial(
                    f_copy,
                    out=self.arena[:, fc * 2048 + tc * 128:
                                   fc * 2048 + (tc + 1) * 128],
                    in_=self.psB[6][0:128, 0:128])))
        dm = merge(done)
        dm += self.tap("dbg_qxn", self.arena[:, 0:2048], dm)
        return dm

    def ph_ffn1(self, ev_qx, ev_wq, ev_mx):
        dr = self.dr
        e0 = self.ve(partial(f_ts, out=self.sc1sb[:, :],
                             in0=self.sw1sb[:, :],
                             s1=self.qcol[:, 1:2].opt(), s2=None,
                             op0=ALU.mult, op1=ALU.bypass),
                     [ev_wq["sw1"]] + ev_mx)
        done = []
        spills = []
        hm_ev = None
        wev = [None, None]
        for ic in range(NIC):
            par = ic % 2
            wsl = self.wb2[:, par * 2048:(par + 1) * 2048]
            wev[par] = self.dma("sync",
                                wsl.rearrange("p (fc i) -> p fc i", i=128),
                                dr["qw1_ag_out"][ic].rearrange(
                                    "p (fc i) -> p fc i", i=128),
                                [ev_wq["qw1"]] + (ev_qx if ic < 2 else
                                                  [done[-4]]), sem="d_w1")
            tile_evs = []
            for tt4 in range(4):
                bk = tt4
                mms = [(self.psB[bk][0:128, 0:512],
                        wsl[:, fc * 128:(fc + 1) * 128],
                        self.arena[:, fc * 2048 + tt4 * 512:
                                   fc * 2048 + (tt4 + 1) * 512],
                        fc == 0, fc == NFC - 1) for fc in range(NFC)]
                self.bank_w(bk, mms,
                            [wev[par]] + (ev_qx if ic < 2 else []))
                st = self.af32[:, 4104 + par * 2048 + tt4 * 512:
                               4104 + par * 2048 + (tt4 + 1) * 512]
                e1 = self.bank_r(bk, "scalar", partial(
                    f_act, out=st, in_=self.psB[bk][0:128, 0:512],
                    func=AF.Gelu, bias=self.b1sb[:, ic:ic + 1],
                    scale=self.sc1sb[:, ic:ic + 1]),
                    [e0] + ([spills[-2]] if len(spills) >= 2 else []))
                tile_evs.append(e1)
                done.append(e1)
            e2 = self.ve(partial(f_red, out=self.sc[:, 10:11],
                                 in_=self.af32[:, 4104 + par * 2048:
                                               4104 + par * 2048 + 2048],
                                 op=ALU.max, absval=True), tile_evs,
                         inc="d_vx")
            if hm_ev is None:
                hm_ev = self.ve(partial(f_copy, out=self.amx[:, 1:2],
                                        in_=self.sc[:, 10:11]), [e2])
            else:
                hm_ev = self.ve(partial(f_tt, out=self.amx[:, 1:2],
                                        in0=self.amx[:, 1:2],
                                        in1=self.sc[:, 10:11], op=ALU.max),
                                [e2, hm_ev])
            spills.append(self.dma("scalar", dr["h_spill"][ic],
                                   self.af32[:, 4104 + par * 2048:
                                             4104 + par * 2048 + 2048],
                                   tile_evs + [e2]))
        self.bank_tr(7, [(self.psB[7][0:1, 0:128], self.amx[:, 1:2],
                          self.c_ident_f32[:])], [hm_ev])
        e2 = self.bank_r(7, "vector", partial(
            f_red, out=self.st2r[0:1, 0:1], in_=self.psB[7][0:1, 0:128],
            op=ALU.max, absval=False), sem="d_vx")
        e3 = self.dma("gpsimd", dr["mh_in"][0:1, 0:1], self.st2r[0:1, 0:1],
                      [e2])
        e4 = self.po(partial(f_cc, kind="AllReduce", op=ALU.max,
                             ins=[dr["mh_in"].opt()],
                             outs=[dr["mh_out"].opt()]), [e3], inc="s_ccS")
        e5 = self.dma("gpsimd", self.st2r[0:1, 1:2], dr["mh_out"][0:1, 0:1],
                      [e4])
        self.bank_w(7, [(self.psB[7][0:128, 0:1], self.c_ones_row_f32[:],
                         self.st2r[0:1, 1:2], True, True)], [e5])
        e9 = self.bank_r(7, "vector", partial(
            f_ts, out=self.qcol[:, 2:3], in0=self.psB[7][0:128, 0:1],
            s1=1.0 / 127.0, s2=1e-8, op0=ALU.mult, op1=ALU.max))
        e10a = self.ac(partial(f_act, out=self.qcol[:, 5:6],
                               in_=self.qcol[:, 2:3], func=AF.Ln), [e9])
        e10 = self.ac(partial(f_act, out=self.qcol[:, 3:4],
                              in_=self.qcol[:, 5:6], func=AF.Exp,
                              scale=-1.0), [e10a])
        e11 = self.ve(partial(f_ts, out=self.sc2sb[:, :],
                              in0=self.sw2sb[:, :],
                              s1=self.qcol[:, 2:3].opt(), s2=None,
                              op0=ALU.mult, op1=ALU.bypass),
                      [e10, ev_wq["sw2"]])
        self.tap("dbg_qw", dr["qw1_ag_out"][0], [ev_wq["qw1"]])
        self.tap("dbg_swin", dr["sw1_ag_in"][:, :], [ev_wq["sw1"]])
        self.tap("dbg_swout", dr["sw1_ag_out"][:, :], [ev_wq["sw1"]])
        self.tap("dbg_qwin", dr["qw1_ag_in"][0], [ev_wq["qw1"]])
        self.tap("dbg_sw", self.sw1sb[:, :], [e0])
        self.tap("dbg_sc1", self.sc1sb[:, :], [e0])
        self.tap("dbg_b1", self.b1sb[:, :], [e0])
        if "dbg_h" in self.debug:
            for i in range(4):
                self.tap_evs.append(self.dma("gpsimd", self.dr["dbg_h"][i],
                                             dr["h_spill"][i], [spills[i]]))
        return {"h": merge(spills), "inv_s2": e10, "sc2": e11}

    def ph_ffn2(self, ev_f1, ev_wq):
        dr = self.dr
        g_evs = {}
        all_done = []
        blk_done = []
        for blk in range(4):
            t0 = blk * 512
            fills = []
            f_e1s = []
            for ic in range(NIC):
                par = ic % 2
                st = self.af32[:, 4104 + par * 2048:4104 + par * 2048 + 512]
                evd = self.dma("sync", st,
                               dr["h_spill"][ic][:, t0:t0 + 512],
                               merge(ev_f1["h"], blk_done) if ic < 2
                               else [f_e1s[-1]], sem="d_fh")
                e1 = self.ve(partial(
                    f_ts, out=self.wqi[:, par * 512:par * 512 + 512],
                    in0=st, s1=self.qcol[:, 3:4].opt(), s2=None,
                    op0=ALU.mult, op1=ALU.bypass),
                    [evd, ev_f1["inv_s2"]])
                f_e1s.append(e1)
                e2 = self.ve(partial(
                    f_copy, out=self.arena[:, ic * 512:(ic + 1) * 512],
                    in_=self.wqi[:, par * 512:par * 512 + 512]), [e1])
                fills.append(e2)
            blk_done = []
            wprev = [None, None]
            for oc2 in range(NFC):
                bk = oc2 % 2
                for q in range(4):
                    wpar = q % 2
                    wsl = self.wb2[:, wpar * 2048:(wpar + 1) * 2048]
                    evw = self.dma(
                        "sync", wsl.rearrange("p (i o) -> p i o", o=128),
                        dr["qw2_ag_out"][oc2].rearrange(
                            "p (i o) -> p i o", o=128)[
                            :, q * 16:(q + 1) * 16, :],
                        [ev_wq["qw2"]] + [w for w in wprev if w],
                        sem="d_w2")
                    mms = [(self.psB[bk][0:128, 0:512],
                            wsl[:, i16 * 128:(i16 + 1) * 128],
                            self.arena[:, (q * 16 + i16) * 512:
                                       (q * 16 + i16 + 1) * 512],
                            (q == 0 and i16 == 0), (q == 3 and i16 == 15))
                           for i16 in range(16)]
                    if q == 0:
                        ev_mm = self.bank_w(bk, mms, [evw] + fills)
                    else:
                        ev_mm = self.bank_w_more(bk, mms, [evw])
                    wprev[wpar] = ev_mm
                gst = self.af32[:, 2048 + (oc2 % 2) * 512:
                                2048 + (oc2 % 2) * 512 + 512]
                e1 = self.bank_r(bk, "vector", partial(
                    f_ts, out=gst, in0=self.psB[bk][0:128, 0:512],
                    s1=self.sc2sb[:, oc2:oc2 + 1].opt(),
                    s2=self.b2sb[:, oc2:oc2 + 1].opt(),
                    op0=ALU.mult, op1=ALU.add),
                    [ev_f1["sc2"]] +
                    ([g_evs[(oc2 - 2, blk)]] if (oc2 - 2, blk) in g_evs
                     else []))
                e2 = self.ac(partial(f_act, out=gst, in_=gst, func=AF.Silu),
                             [e1])
                ge = self.dma("scalar", dr["g_spill"][oc2, blk], gst, [e2])
                g_evs[(oc2, blk)] = ge
                blk_done.append(ge)
                all_done.append(ge)
        if "dbg_g" in self.debug:
            for i in range(2):
                for bq in range(4):
                    self.tap_evs.append(self.dma(
                        "gpsimd", self.dr["dbg_g"][i, bq],
                        dr["g_spill"][i, bq], [g_evs[(i, bq)]]))
        return merge(all_done), g_evs

    def ph_output(self, ev_f2, ev_si, g_evs):
        dr = self.dr
        evwo = self.dma("gpsimd", self.arena[:, 0:16384].rearrange(
            "p (kc o) -> p kc o", o=2048),
            dr["Wo"].rearrange("(kc p) o -> p kc o", p=128), ev_f2)
        out_evs = []
        prev_o = []
        prev_g = []
        prev_x = []
        for tc in range(NTC):
            w0 = ev_f2 if tc == 0 else prev_o
            ga = [self.dma("sync", self.af32[:, 6152:7176],
                           dr["gate_spill"][tc], w0, sem="d_oe")]
            ga.append(self.dma("sync", self.opre_ld[0:64, :],
                               dr["opre_spill"][tc * 2], w0, sem="d_oe"))
            ga.append(self.dma("sync", self.opre_ld[64:128, :],
                               dr["opre_spill"][tc * 2 + 1], w0, sem="d_oe"))
            ga.append(self.dma("sync", self.ephi_ld[:, :],
                               dr["ephi_spill"][tc * 2], w0, sem="d_oe"))
            ga.append(self.dma("sync", self.xtok[:, :],
                               dr["x_slice"][tc * 128:(tc + 1) * 128, :],
                               merge(w0, prev_x), sem="d_oe"))
            grpA = merge(ga)
            grp = grpA
            evx = grpA
            o_evs = []
            last_e8 = {}
            for half in range(2):
                p0, p1 = half * 64, half * 64 + 64
                if half == 1:
                    evb = self.dma("sync", self.ephi_ld[:, :],
                                   dr["ephi_spill"][tc * 2 + 1], o_evs,
                                   sem="d_oe")
                    grp = merge(grpA, [evb])
                for h in range(H):
                    hp = h % 2
                    bk = hp
                    scb = 8 + hp * 4
                    self.bank_w(bk, [(self.psB[bk][p0:p1, 0:128],
                                      self.ephi_ld[:, h * 64:(h + 1) * 64],
                                      self.Sin[:, h * 128:(h + 1) * 128],
                                      True, True)], grp + ev_si)
                    of = self.af32[p0:p1, 8200 + hp * 128:8328 + hp * 128]
                    e1 = self.bank_r(bk, "vector", partial(
                        f_tt, out=of, in0=self.psB[bk][p0:p1, 0:128],
                        in1=self.opre_ld[p0:p1, h * 128:(h + 1) * 128],
                        op=ALU.add), grp)
                    e2 = self.ac(partial(
                        f_act,
                        out=self.af32[p0:p1, 8456 + hp * 128:8584 + hp * 128],
                        in_=of, func=AF.Square,
                        accum=self.sc[p0:p1, scb:scb + 1]), [e1])
                    e3 = self.ac(partial(
                        f_act, out=self.sc[p0:p1, scb + 1:scb + 2],
                        in_=self.sc[p0:p1, scb:scb + 1],
                        func=AF.Ln, bias=self.epsc[p0:p1, 0:1],
                        scale=1.0 / 128.0), [e2])
                    e4 = self.ac(partial(
                        f_act, out=self.sc[p0:p1, scb + 2:scb + 3],
                        in_=self.sc[p0:p1, scb + 1:scb + 2],
                        func=AF.Exp, scale=-0.5), [e3])
                    e5 = self.ve(partial(
                        f_ts, out=of, in0=of,
                        s1=self.sc[p0:p1, scb + 2:scb + 3].opt(), s2=None,
                        op0=ALU.mult, op1=ALU.bypass), [e4])
                    e6 = self.ve(partial(
                        f_tt, out=of, in0=of,
                        in1=self.c_norm_w_rep[p0:p1, :], op=ALU.mult), [e5])
                    sg = self.af32[p0:p1, 8712 + hp * 128:8840 + hp * 128]
                    e7 = self.ac(partial(
                        f_act, out=sg,
                        in_=self.af32[p0:p1,
                                      6152 + h * 128:6152 + (h + 1) * 128],
                        func=AF.Silu),
                        grp + ([last_e8[hp]] if hp in last_e8 else []))
                    e8 = self.ve(partial(
                        f_tt, out=self.otok[p0:p1, h * 128:(h + 1) * 128],
                        in0=of, in1=sg, op=ALU.mult), [e6, e7])
                    last_e8[hp] = e8
                    o_evs.append(e8)
            tr_evs = []
            for h in range(H):
                self.bank_tr(6, [(self.psB[6][0:128, 0:128],
                                  self.otok[:, h * 128:(h + 1) * 128],
                                  self.c_ident_bf[:])], o_evs)
                tr_evs.append(self.bank_r(6, "vector", partial(
                    f_copy, out=self.ofm[:, h * 128:(h + 1) * 128],
                    in_=self.psB[6][0:128, 0:128])))
            prev_o = []
            for ot in range(4):
                bk = 2 + ot % 2
                mms = [(self.psB[bk][0:128, 0:512],
                        self.ofm[:, kc * 128:(kc + 1) * 128],
                        self.arena[:, kc * 2048 + ot * 512:
                                   kc * 2048 + (ot + 1) * 512],
                        kc == 0, kc == H - 1) for kc in range(H)]
                self.bank_w(bk, mms, tr_evs + [evwo])
                gq = [self.dma(
                    "sync", self.af32[:, 3072 + i4 * 128:
                                      3072 + (i4 + 1) * 128],
                    dr["g_spill"][ot * 4 + i4, tc // 4][
                        :, (tc % 4) * 128:(tc % 4 + 1) * 128],
                    [g_evs[(ot * 4 + i4, tc // 4)]] + prev_g, sem="d_gq")
                    for i4 in range(4)]
                gqa = merge(gq)
                gp_evs = []
                for i4 in range(4):
                    gpst = self.af32[:, 3072 + i4 * 128:3072 + (i4 + 1) * 128]
                    self.bank_tr(7, [(self.psB[7][0:128, 0:128], gpst,
                                      self.c_ident_f32[:])], gqa)
                    gp_evs.append(self.bank_r(7, "vector", partial(
                        f_copy, out=self.gtok[:, i4 * 128:(i4 + 1) * 128],
                        in_=self.psB[7][0:128, 0:128])))
                prev_g = gp_evs
                e1 = self.bank_r(bk, "vector", partial(
                    f_tt, out=self.fin[:, 0:512],
                    in0=self.psB[bk][0:128, 0:512],
                    in1=self.gtok[:, 0:512], op=ALU.mult),
                    gp_evs + (out_evs[-2:] if len(out_evs) >= 2 else []))
                e2 = self.ve(partial(
                    f_tt, out=self.fin[:, 512:1024],
                    in0=self.fin[:, 0:512],
                    in1=self.xtok[:, ot * 512:(ot + 1) * 512],
                    op=ALU.add), [e1] + evx)
                oev = self.dma("sync",
                               dr["out"][tc * 128:(tc + 1) * 128,
                                         ot * 512:(ot + 1) * 512],
                               self.fin[:, 512:1024], [e2])
                out_evs.append(oev)
                prev_o.append(oev)
            prev_x = prev_o[-1:]
        if "dbg_opre" in self.debug:
            for cch in range(NCH):
                self.tap_evs.append(self.dma(
                    "gpsimd", self.dr["dbg_opre"][cch],
                    dr["opre_spill"][cch], ev_f2))
        if "dbg_ephi" in self.debug:
            for cch in range(NCH):
                self.tap_evs.append(self.dma(
                    "gpsimd", self.dr["dbg_ephi"][cch],
                    dr["ephi_spill"][cch], ev_f2))
        if "dbg_gate_all" in self.debug:
            for tcq in range(NTC):
                self.tap_evs.append(self.dma(
                    "gpsimd", self.dr["dbg_gate_all"][tcq],
                    dr["gate_spill"][tcq], ev_f2))
        if "dbg_gall" in self.debug:
            for oc2 in range(NFC):
                for bq in range(4):
                    self.tap_evs.append(self.dma(
                        "gpsimd", self.dr["dbg_gall"][oc2, bq],
                        dr["g_spill"][oc2, bq], ev_f2))
        return merge(out_evs)


def make_consts():
    import ml_dtypes
    bf = ml_dtypes.bfloat16
    t = np.arange(C)
    t_of = np.concatenate([t, t])
    j_of = np.concatenate([np.zeros(C, np.int64), np.ones(C, np.int64)])
    orig = 2 * t_of + j_of
    amask = (orig[:, None] > orig[None, :]).astype(np.float32)
    mmaskT = (t_of[:, None] <= t[None, :]).astype(np.float32)
    i64p = np.concatenate([np.eye(64), np.eye(64)], axis=0).astype(bf)
    return {
        "amask": np.ascontiguousarray(amask),
        "mmaskT": np.ascontiguousarray(mmaskT),
        "ident_bf": np.ascontiguousarray(np.eye(128, dtype=bf)),
        "ident_f32": np.ascontiguousarray(np.eye(128, dtype=np.float32)),
        "i64p_bf": np.ascontiguousarray(i64p),
        "ones_f32": np.ascontiguousarray(np.ones((128, 128), np.float32)),
        "ones_col_bf": np.ascontiguousarray(np.ones((128, 1), dtype=bf)),
        "ones_row_bf": np.ascontiguousarray(np.ones((1, 128), dtype=bf)),
        "ones_row_f32": np.ascontiguousarray(np.ones((1, 128), np.float32)),
    }


def host_prepare(inputs):
    x = np.asarray(inputs["L0_attn_x"], np.float32)
    consts = make_consts()
    w1 = np.asarray(inputs["L0_ffn_w1"], np.float32)
    w2 = np.asarray(inputs["L0_ffn_w2"], np.float32)
    cw = np.concatenate([
        np.asarray(inputs["L0_attn_conv_q_w"], np.float32).reshape(KD, CONV),
        np.asarray(inputs["L0_attn_conv_k_w"],
                   np.float32).reshape(2 * KD, CONV),
        np.asarray(inputs["L0_attn_conv_v_w"],
                   np.float32).reshape(2 * KD, CONV),
    ], axis=0)
    cw_oc = np.zeros((128, 40 * CONV), np.float32)
    for oc in range(40):
        cw_oc[:, oc * CONV:(oc + 1) * CONV] = cw[oc * 128:(oc + 1) * 128]
    per_core = []
    for c in range(8):
        b, p = c // 4, c % 4
        t0 = p * TPC
        m = {
            "x_slice": np.ascontiguousarray(x[b, t0:t0 + TPC]),
            "xhalo": (np.zeros((3, D), np.float32) if p == 0 else
                      np.ascontiguousarray(x[b, t0 - 3:t0])),
            "Wq": np.ascontiguousarray(inputs["L0_attn_Wq"], np.float32),
            "Wk": np.ascontiguousarray(inputs["L0_attn_Wk"], np.float32),
            "Wv": np.ascontiguousarray(inputs["L0_attn_Wv"], np.float32),
            "Wb": np.ascontiguousarray(inputs["L0_attn_Wb"], np.float32),
            "Wa": np.ascontiguousarray(inputs["L0_attn_Wa"], np.float32),
            "Wg": np.ascontiguousarray(inputs["L0_attn_Wg"], np.float32),
            "Wo": np.ascontiguousarray(inputs["L0_attn_Wo"], np.float32),
            "A_log_c": np.ascontiguousarray(
                np.asarray(inputs["L0_attn_A_log"],
                           np.float32).reshape(8, 1)),
            "dt_bias_c": np.ascontiguousarray(
                np.asarray(inputs["L0_attn_dt_bias"],
                           np.float32).reshape(8, 1)),
            "convw_all": cw_oc,
            "norm_w_rep": np.ascontiguousarray(np.broadcast_to(
                np.asarray(inputs["L0_attn_norm_w"], np.float32)[None, :],
                (128, 128))),
            "w1_slice": np.ascontiguousarray(
                w1[c * 1024:(c + 1) * 1024].reshape(8, 128, D)),
            "b1_col": np.ascontiguousarray(
                np.asarray(inputs["L0_ffn_b1"],
                           np.float32).reshape(NIC, 128)),
            "w2_slice": np.ascontiguousarray(
                w2[c * 256:(c + 1) * 256].reshape(2, 128, I_FFN)),
            "b2_col": np.ascontiguousarray(
                np.asarray(inputs["L0_ffn_b2"],
                           np.float32).reshape(NFC, 128)),
            "chain_m_rep": np.ascontiguousarray(np.broadcast_to(
                np.array([1.0 if (j // 4 == b and j % 4 < p) else 0.0
                          for j in range(7)], np.float32)[None, :],
                (128, 7))),
        }
        m.update(consts)
        per_core.append(m)
    return per_core


DEBUG = ()


def kernel(**inputs):
    per_core = host_prepare(inputs)
    nc = Builder(debug=DEBUG).build()
    res = run_bass_kernel_spmd(nc, per_core, core_ids=list(range(8)))
    out = np.empty((B, L, D), np.float32)
    for c in range(8):
        b, p = c // 4, c % 4
        out[b, p * TPC:(p + 1) * TPC] = res.results[c]["out"]
    if DEBUG:
        kernel.dbg = [{k: np.asarray(v) for k, v in r.items()}
                      for r in res.results]
    return out
